# version 05
# baseline (speedup 1.0000x reference)
import contextlib
import numpy as np
import ml_dtypes
import concourse.bass as bass
import concourse.mybir as mybir
from concourse.bass_utils import run_bass_kernel_spmd
from concourse.bass_isa_rust import ReduceOp

RED_MAX = ReduceOp.max

F32 = mybir.dt.float32
BF16 = mybir.dt.bfloat16
I32 = mybir.dt.int32
AF = mybir.ActivationFunctionType
OP = mybir.AluOpType
AX = mybir.AxisListType

D = 1024
NB = 4
TFULL = 4096
H, NH = 16, 64
DI = 2816
DM = 2048
NS = 64
CH = 32

ENG = dict(v="vector", s="scalar", t="tensor", g="gpsimd", y="sync")


class Sem:
    __slots__ = ("h", "n")

    def __init__(self, nc, es, name):
        self.h = es.enter_context(nc.semaphore(name))
        self.n = 0


class Prog:
    def __init__(self, nc, es, block, Tloc):
        self.nc, self.es, self.block = nc, es, block
        self.es0 = es
        self.T = Tloc
        self.ld = Sem(nc, es, "ld")
        self.gld = Sem(nc, es, "gld")
        self.sv = Sem(nc, es, "sv")
        self.ss = Sem(nc, es, "ss")
        self.sm = Sem(nc, es, "sm")
        self.sg = Sem(nc, es, "sg")
        self.sy = Sem(nc, es, "sy")
        self.sbm = Sem(nc, es, "sbm")
        self.glob = {}
        self._streams = []
        self._cpool = es.enter_context(nc.sbuf_tensor("cpool", [128, 16], F32))
        self._cvals = {}
        self._cinit = None

    def const(self, val, rows=128):
        if val not in self._cvals:
            assert len(self._cvals) < 16
            j = len(self._cvals)
            self._cvals[val] = j
            tok = self.op("v", "memset", self._cpool[:, j:j + 1], float(val), inc=self.sv)
            self._cinit = (self.sv, tok)
        j = self._cvals[val]
        return self._cpool[0:rows, j:j + 1]

    def op(self, eng, meth, *args, waits=(), inc=None, inck=1, **kw):
        waits = tuple(waits)

        def f(e):
            for sem, val in waits:
                if val > 0:
                    e.wait_ge(sem.h, val)
            ins = getattr(e, meth)(*args, **kw)
            if inc is not None:
                ins.then_inc(inc.h, inck)
        getattr(self.block, ENG[eng])(f)
        if inc is not None:
            inc.n += inck
            return inc.n
        return None

    def wait(self, eng, waits):
        waits = tuple(waits)

        def f(e):
            for sem, val in waits:
                e.wait_ge(sem.h, val)
        getattr(self.block, ENG[eng])(f)

    def dma(self, eng, dst, src, waits=()):
        sem = self.gld if eng == "g" else self.ld
        return self.op(eng, "dma_start", dst, src, waits=waits, inc=sem, inck=16)

    def stream(self, name, nbuf=2):
        s = DStream(self, name, nbuf)
        self._streams.append(s)
        return s

    @contextlib.contextmanager
    def scope(self):
        old_es = self.es
        es2 = contextlib.ExitStack()
        self.es = es2
        try:
            yield
        finally:
            self.barrier()
            es2.close()
            self.es = old_es

    def sb(self, name, shape, dtype):
        return self.es.enter_context(self.nc.sbuf_tensor(name, shape, dtype))

    def ps(self, name, cols, dtype=F32):
        return self.es.enter_context(self.nc.psum_tensor(name, [128, cols], dtype))

    def barrier(self):
        dws = [(self.ld, self.ld.n), (self.gld, self.gld.n)]
        for st_ in self._streams:
            dws += st_.done()
        self.wait("y", dws)
        tgt = self.sbm.n + 5
        for eng in "vstgy":
            self.op(eng, "nop", inc=self.sbm)
        for eng in "vstgy":
            self.wait(eng, [(self.sbm, tgt)])


class DStream:

    def __init__(self, p, name, nbuf):
        self.p, self.nbuf = p, nbuf
        self.sems = [Sem(p.nc, p.es0, f"{name}{i}") for i in range(nbuf)]
        self.i = 0

    def load(self, dst, src, eng="y", waits=(), adv=True, **kw):
        s = self.sems[self.i % self.nbuf]
        self.p.op(eng, "dma_start", dst, src, waits=waits, inc=s, inck=16, **kw)
        if adv:
            self.i += 1
        return (s, s.n)

    def adv(self):
        self.i += 1

    def done(self):
        return [(s, s.n) for s in self.sems if s.n > 0]



def _bf16(a):
    return np.ascontiguousarray(np.asarray(a).astype(ml_dtypes.bfloat16))


def _f32(a):
    return np.ascontiguousarray(np.asarray(a, dtype=np.float32))


def _tiles(vec):
    v = _f32(vec).reshape(-1)
    return np.ascontiguousarray(v.reshape(v.shape[0] // 128, 128).T)


def prep_inputs(inputs, Tloc):
    x = _f32(inputs["L0_attn_x"])
    sh = {}
    for nm in ["Wr", "Wk", "Wv", "Ww", "Wg", "Wo"]:
        sh["w" + nm] = _bf16(np.asarray(inputs["L0_attn_" + nm]).T)
    for nm in ["ln_w", "ln_b", "mix_k", "mix_v", "mix_r", "mix_w", "time_decay"]:
        sh[nm] = _tiles(inputs["L0_attn_" + nm])
    sh["u_vec"] = _tiles(np.asarray(inputs["L0_attn_time_first"]).reshape(-1))
    sh["clip_x"] = _f32(inputs["L0_ffn_clip_x"]).reshape(1, 1)
    sh["clip_down"] = _f32(inputs["L0_ffn_clip_down"]).reshape(1, 1)
    sh["w_gate_r"] = _f32(inputs["L0_ffn_w_gate"])
    sh["w_up_r"] = _f32(inputs["L0_ffn_w_up"])
    sh["w_down_r"] = _f32(inputs["L0_ffn_w_down"])
    sh["w_gate_t"] = _f32(np.asarray(inputs["L0_ffn_w_gate"]).T)
    sh["w_up_t"] = _f32(np.asarray(inputs["L0_ffn_w_up"]).T)
    sh["w_down_t"] = _f32(np.asarray(inputs["L0_ffn_w_down"]).T)
    sh["norm_w"] = _tiles(inputs["L1_attn_norm_w"])
    sh["w_inproj"] = _bf16(np.asarray(inputs["L1_attn_in_proj_w"]).T)
    sh["conv_w"] = np.ascontiguousarray(_f32(inputs["L1_attn_conv_w"]).reshape(16, 128, 4))
    sh["conv_b"] = _tiles(inputs["L1_attn_conv_b"])
    sh["w_xproj"] = _bf16(np.asarray(inputs["L1_attn_x_proj_w"])[64:128].T)
    sh["w_outproj"] = _bf16(np.asarray(inputs["L1_attn_out_proj_w"]).T)
    sh["D_all"] = _tiles(inputs["L1_attn_D_param"])
    ind = np.zeros((128, 16), np.float32)
    for pp in range(128):
        ind[pp, pp // 8] = 1.0
    sh["ind16"] = _bf16(ind)
    sh["w_win"] = _bf16(inputs["L1_ffn_Win"])
    sh["bin_"] = _f32(inputs["L1_ffn_bin_"]).reshape(NH, 1)
    sh["W1"] = _f32(inputs["L1_ffn_W1"])
    sh["b1"] = _f32(inputs["L1_ffn_b1"])
    sh["W2"] = _f32(inputs["L1_ffn_W2"])
    sh["b2"] = _f32(inputs["L1_ffn_b2"]).reshape(NH, 1)
    sh["w_wout"] = _bf16(inputs["L1_ffn_Wout"])
    sh["bout"] = _tiles(inputs["L1_ffn_bout"])
    sh["ident"] = _f32(np.eye(128))
    sh["identb"] = _bf16(np.eye(128))
    m = np.tril(np.ones((CH, CH), np.float32), -1).T
    sh["mask32"] = _bf16(np.tile(m, (1, H)))
    m64 = np.tril(np.ones((64, 64), np.float32), -1).T
    sh["mask64"] = _bf16(np.tile(m64, (1, H)))
    hs = np.zeros((8, 128, 16), np.float32)
    rs = np.zeros((8, 16, 128), np.float32)
    for ct in range(8):
        hs[ct, 0:64, 2 * ct] = 1.0
        hs[ct, 64:128, 2 * ct + 1] = 1.0
        rs[ct, 2 * ct, 0:64] = 1.0
        rs[ct, 2 * ct + 1, 64:128] = 1.0
    sh["headsel"] = _bf16(hs.transpose(1, 0, 2).reshape(128, 8 * 16))
    sh["rowsel"] = _bf16(rs.transpose(1, 0, 2).reshape(16, 8 * 128))
    bind = np.zeros((NS, NS * 128), np.float32)
    for n in range(NS):
        bind[n, n * 128:(n + 1) * 128] = 1.0
    sh["bind"] = _bf16(bind)

    per_core = []
    for core in range(8):
        b, half = core // 2, core % 2
        d = dict(sh)
        d["x_in"] = np.ascontiguousarray(x[b, half * Tloc:(half + 1) * Tloc])
        d["x_prev"] = (np.zeros((1, D), np.float32) if half == 0
                       else np.ascontiguousarray(x[b, Tloc - 1:Tloc]))
        d["halo_mask"] = np.full((1, 1), float(half), np.float32)
        d["halo_maskv"] = np.full((128, 1), float(half), np.float32)
        sl = slice(half * (DM // 2), (half + 1) * (DM // 2))
        al = _f32(inputs["L1_attn_A_log"])[sl]
        als = np.empty((128, 512), np.float32)
        for cb in range(64):
            blk = al[128 * (cb // 8) + 16 * (cb % 8):][:16]
            als[:, cb * 8:(cb + 1) * 8] = (
                blk.reshape(16, 8, 8).transpose(0, 2, 1).reshape(128, 8))
        d["A_log_h"] = np.ascontiguousarray(als)
        d["w_dt_h"] = np.ascontiguousarray(
            _bf16(np.asarray(inputs["L1_attn_dt_w"]).T)[:, sl])
        d["dt_bh"] = _tiles(np.asarray(inputs["L1_attn_dt_b"])[sl])
        T2 = 2 * Tloc
        offs = [(half * 8 + ct) * (128 * Tloc) for ct in range(8)]
        offs += [s * (8 * 128 * T2) + half * Tloc for s in range(2)]
        offs += [0] * 6
        d["offs"] = np.asarray(offs, np.int32).reshape(1, 16)
        per_core.append(d)
    return per_core


def declare_io(nc, Tloc, dbg):
    io = {}

    def inp(name, shape, dtype=F32):
        io[name] = nc.dram_tensor(name, shape, dtype, kind="ExternalInput").ap()

    def outp(name, shape, dtype=F32):
        io[name] = nc.dram_tensor(name, shape, dtype, kind="ExternalOutput").ap()

    inp("x_in", [Tloc, D]); inp("x_prev", [1, D]); inp("halo_mask", [1, 1])
    for nm in ["Wr", "Wk", "Wv", "Ww", "Wg", "Wo"]:
        inp("w" + nm, [D, D], BF16)
    for nm in ["ln_w", "ln_b", "mix_k", "mix_v", "mix_r", "mix_w", "time_decay", "u_vec",
               "norm_w", "bout"]:
        inp(nm, [128, 8])
    inp("conv_b", [128, 16])
    inp("halo_maskv", [128, 1])
    inp("mask64", [64, H * 64], BF16)
    inp("headsel", [128, 8 * 16], BF16)
    inp("rowsel", [16, 8 * 128], BF16)
    inp("clip_x", [1, 1]); inp("clip_down", [1, 1])
    inp("w_gate_r", [DI, D]); inp("w_up_r", [DI, D]); inp("w_down_r", [D, DI])
    inp("w_gate_t", [D, DI]); inp("w_up_t", [D, DI]); inp("w_down_t", [DI, D])
    inp("w_inproj", [D, 2 * DM], BF16)
    inp("conv_w", [16, 128, 4])
    inp("w_xproj", [DM, NS], BF16); inp("w_dt_h", [DM, DM // 2], BF16)
    inp("w_outproj", [DM, D], BF16)
    inp("D_all", [128, 16]); inp("ind16", [128, 16], BF16); inp("dt_bh", [128, 8])
    io["offs"] = nc.dram_tensor("offs", [1, 16], mybir.dt.int32, kind="ExternalInput").ap()
    inp("w_win", [D, NH], BF16); inp("bin_", [NH, 1])
    inp("W1", [NH, 16]); inp("b1", [NH, 16]); inp("W2", [NH, 16]); inp("b2", [NH, 1])
    inp("w_wout", [NH, D], BF16)
    inp("ident", [128, 128]); inp("identb", [128, 128], BF16)
    inp("mask32", [CH, H * CH], BF16)
    inp("bind", [NS, NS * 128], BF16)
    inp("A_log_h", [128, 512])
    outp("x_out", [Tloc, D])
    if dbg:
        outp("dbg_xn0", [8, 128, Tloc], BF16)
    return io


def phase_load_ln(p, io, dbg):
    T = p.T
    FT = D // 128
    TT = T // 128
    acc = p.glob["acc"]
    xn0 = [p.sb(f"xn0_{i}", [128, 1 + T], BF16) for i in range(FT)]
    p.glob["xn0"] = xn0

    ident = p.glob["ident"]
    identb = p.glob["identb"]
    t_c = p.glob["t_ident"]

    xst = p.stream("ldx", 2)
    xtok = [p.sb(f"xtok{j}", [128, D], F32) for j in range(2)]
    xntok = p.sb("xntok", [128, D], F32)
    sq = p.sb("sq_ln", [128, D], F32)
    st8 = p.sb("st8", [128, 8], F32)
    pbk = p.glob["pb"]

    sv, ss, sm = p.sv, p.ss, p.sm
    gates = {}
    for tt in range(TT):
        buf = xtok[tt % 2]
        w = [(sv, gates[tt - 2])] if (tt - 2) in gates else ()
        tok_ld = xst.load(buf[:], io["x_in"][tt * 128:(tt + 1) * 128, :], waits=w)
        p.op("v", "reduce_sum", st8[:, 0:1], buf[:], axis=AX.X, waits=[tok_ld], inc=sv)
        tok_sq = p.op("s", "activation", sq[:], buf[:], AF.Square, waits=[tok_ld], inc=ss)
        t1 = p.op("v", "reduce_sum", st8[:, 1:2], sq[:], axis=AX.X, waits=[(ss, tok_sq)], inc=sv)
        t2 = p.op("v", "tensor_scalar", st8[:, 2:3], st8[:, 0:1], 1.0 / D, None, OP.mult,
                  waits=[(sv, t1)], inc=sv)
        t3 = p.op("s", "activation", st8[:, 3:4], st8[:, 2:3], AF.Square, waits=[(sv, t2)], inc=ss)
        t4 = p.op("v", "scalar_tensor_tensor", st8[:, 4:5], st8[:, 1:2], 1.0 / D, st8[:, 3:4],
                  OP.mult, OP.subtract, waits=[(ss, t3)], inc=sv)
        t5 = p.op("s", "activation", st8[:, 5:6], st8[:, 4:5], AF.Sqrt, bias=p.const(1e-5),
                  waits=[(sv, t4)], inc=ss)
        t6 = p.op("v", "reciprocal", st8[:, 6:7], st8[:, 5:6], waits=[(ss, t5)], inc=sv)
        t7 = p.op("v", "tensor_scalar", xntok[:], buf[:], st8[:, 2:3], st8[:, 6:7],
                  OP.subtract, OP.mult, waits=[(sv, t6)], inc=sv)
        for grp in range(2):
            bank = pbk[grp]
            tms = 0
            for j in range(4):
                ft = grp * 4 + j
                tms = p.op("t", "transpose", bank[:, j * 128:(j + 1) * 128],
                           buf[:, ft * 128:(ft + 1) * 128], ident[:],
                           waits=[tok_ld, (sv, gates.get(("psx", grp), 0)), t_c], inc=sm)
            for j in range(4):
                ft = grp * 4 + j
                e1 = p.op("v", "tensor_copy", acc[ft][:, tt * 128:(tt + 1) * 128],
                          bank[:, j * 128:(j + 1) * 128],
                          waits=[(sm, tms)], inc=sv)
            gates[("psx", grp)] = e1
        for grp in range(2):
            bank = pbk[2 + grp]
            tms = 0
            for j in range(4):
                ft = grp * 4 + j
                tms = p.op("t", "transpose", bank[:, j * 128:(j + 1) * 128],
                           xntok[:, ft * 128:(ft + 1) * 128], ident[:],
                           waits=[(sv, t7), (sv, gates.get(("psn", grp), 0))], inc=sm)
            for j in range(4):
                ft = grp * 4 + j
                e2 = p.op("v", "tensor_copy", xn0[ft][:, 1 + tt * 128:1 + (tt + 1) * 128],
                          bank[:, j * 128:(j + 1) * 128],
                          waits=[(sm, tms)], inc=sv)
            gates[("psn", grp)] = e2
        gates[tt] = e2
    xprev = p.sb("xprev", [1, D], F32)
    hm = p.sb("hm", [1, 1], F32)
    hst = p.stream("ldh", 1)
    hst.load(xprev[:], io["x_prev"])
    t_h = hst.load(hm[:], io["halo_mask"])
    st1 = p.sb("st1", [1, 8], F32)
    sq1 = p.sb("sq1", [1, D], F32)
    a = p.op("v", "reduce_sum", st1[:, 0:1], xprev[:], axis=AX.X, waits=[t_h], inc=sv)
    b_ = p.op("s", "activation", sq1[:], xprev[:], AF.Square, waits=[t_h], inc=ss)
    c = p.op("v", "reduce_sum", st1[:, 1:2], sq1[:], axis=AX.X, waits=[(ss, b_)], inc=sv)
    d_ = p.op("v", "tensor_scalar", st1[:, 2:3], st1[:, 0:1], 1.0 / D, None, OP.mult,
              waits=[(sv, c)], inc=sv)
    e_ = p.op("s", "activation", st1[:, 3:4], st1[:, 2:3], AF.Square, waits=[(sv, d_)], inc=ss)
    f_ = p.op("v", "scalar_tensor_tensor", st1[:, 4:5], st1[:, 1:2], 1.0 / D, st1[:, 3:4],
              OP.mult, OP.subtract, waits=[(ss, e_)], inc=sv)
    g_ = p.op("s", "activation", st1[:, 5:6], st1[:, 4:5], AF.Sqrt, bias=p.const(1e-5, rows=1),
              waits=[(sv, f_)], inc=ss)
    h_ = p.op("v", "reciprocal", st1[:, 6:7], st1[:, 5:6], waits=[(ss, g_)], inc=sv)
    i_ = p.op("v", "tensor_tensor", st1[:, 6:7], st1[:, 6:7], hm[:, 0:1], OP.mult,
              waits=[(sv, h_)], inc=sv)
    j_ = p.op("v", "tensor_scalar", sq1[:], xprev[:], st1[:, 2:3], st1[:, 6:7],
              OP.subtract, OP.mult, waits=[(sv, i_)], inc=sv)
    prev_ev = gates[("psx", 0)]
    for ft in range(FT):
        tm = p.op("t", "transpose", pbk[0][:, 0:1], sq1[:, ft * 128:(ft + 1) * 128],
                  ident[0:1, 0:1], waits=[(sv, j_), (sv, prev_ev)], inc=sm)
        prev_ev = p.op("v", "tensor_copy", xn0[ft][:, 0:1], pbk[0][:, 0:1],
                       waits=[(sm, tm)], inc=sv)
    p.glob["xn0_done"] = (sv, sv.n)

    if dbg:
        dst = p.stream("dbgs", 1)
        for ft in range(FT):
            dst.load(io["dbg_xn0"][ft], xn0[ft][:, 1:1 + T], eng="y",
                     waits=[(sv, sv.n)])
        p.glob["dbg_st"] = dst


RG_PAIRS = [[0, 1], [2, 3], [4, 5], [6, 7]]
RG_ALL = [[0, 1, 2, 3, 4, 5, 6, 7]]


def set_ncores(n):
    global RG_PAIRS, RG_ALL
    RG_PAIRS = [[i, i + 1] for i in range(0, n, 2)]
    RG_ALL = [list(range(n))]


def vchunk(tens, coff, n, step=512, groups=8, pitch=None):
    pitch = pitch if pitch is not None else tens.shape[1]
    return bass.AP(tens, coff, [[pitch, 128], [step, groups], [1, n]])


def prow(tens, row0, nrow, coff, ncol):
    pitch = tens.shape[1]
    return bass.AP(tens, row0 * pitch + coff, [[pitch, nrow], [1, ncol]])


def rwkv_prelude(p, io, dbg):
    nc, T = p.nc, p.T
    FT = 8
    C = 64
    sv, ss, sm, sg = p.sv, p.ss, p.sm, p.sg
    ident = p.sb("identS", [128, 128], F32)
    identb = p.sb("identbS", [128, 128], BF16)
    p.glob["ident"] = ident
    p.glob["identb"] = identb
    cst = p.stream("ldc", 1)
    cst.load(ident[:], io["ident"], adv=False)
    p.glob["t_ident"] = cst.load(identb[:], io["identb"], adv=False)
    dr = {}
    dbg_vis = {"xn2_sp", "qc_sp", "yraw_sp", "g_sp"}
    for nm, shape, dt_ in [("xn0_sp", [FT, 128, T], BF16), ("xn2_sp", [FT, 128, 1 + T], BF16),
                           ("qc_sp", [128, FT * T], BF16), ("yraw_sp", [128, FT * T], BF16),
                           ("g_sp", [128, FT * T], BF16),
                           ("s_snd", [FT, 128, NH], F32), ("s_rcv", [2, FT, 128, NH], F32),
                           ("armax_in", [1, 1], F32), ("armax_out", [1, 1], F32)]:
        kind = "ExternalOutput" if (dbg and nm in dbg_vis) else "Internal"
        dr[nm] = nc.dram_tensor(nm, shape, dt_, kind=kind)
    dra = {nm: h.ap() for nm, h in dr.items()}
    p.glob["dr"] = dr
    p.glob["dra"] = dra

    vst = p.stream("vld", 1)
    lnw = p.sb("lnw", [128, 8], F32); vst.load(lnw[:], io["ln_w"], adv=False)
    lnb = p.sb("lnb", [128, 8], F32); vst.load(lnb[:], io["ln_b"], adv=False)
    mixv = p.sb("mixv", [128, 32], F32)
    for i, nm in enumerate(["mix_k", "mix_v", "mix_r", "mix_w"]):
        vst.load(mixv[:, i * 8:(i + 1) * 8], io[nm], adv=False)
    tdv = p.sb("tdv", [128, 8], F32); vst.load(tdv[:], io["time_decay"], adv=False)
    uv = p.sb("uv", [128, 8], F32); vst.load(uv[:], io["u_vec"], adv=False)
    hmv = p.sb("hmv", [128, 1], F32); vst.load(hmv[:], io["halo_maskv"], adv=False)
    mk64 = p.sb("mk64", [C, H * C], BF16); vst.load(mk64[:], io["mask64"], adv=False)
    hsel = p.sb("hsel", [128, 8 * 16], BF16); vst.load(hsel[:], io["headsel"], adv=False)
    rsel = p.sb("rsel", [16, 8 * 128], BF16)
    t_vec = vst.load(rsel[:], io["rowsel"], adv=False)
    sgm = p.sb("sgm", [128, 32], F32)
    t_sgm = p.op("s", "activation", sgm[:], mixv[:], AF.Sigmoid, waits=[t_vec], inc=ss)

    onesb = p.sb("onesb", [128, 128], BF16)
    t_ones = p.op("v", "memset", onesb[:], 1.0, inc=sv)
    onesf = p.sb("onesf", [128, 128], F32)
    t_onesf = p.op("v", "memset", onesf[:], 1.0, inc=sv)

    for _k in ("dr", "dra", "lnw", "lnb", "mixv", "tdv", "uv", "hmv", "mk64", "hsel",
               "rsel", "sgm", "t_vec", "t_sgm", "onesb", "onesf", "t_ones", "t_onesf"):
        p.glob[_k] = locals()[_k]
    p.glob["rtok"] = {}


def rwkv_stageA(p, io, dbg):
    nc, T = p.nc, p.T
    FT = 8
    NSC = T // 512
    C = 64
    NCH = 512 // C
    acc, xn0, pb = p.glob["acc"], p.glob.get("xn0"), p.glob["pb"]
    pbT = p.glob["pbT"]
    identb = p.glob["identb"]
    sv, ss, sm, sg = p.sv, p.ss, p.sm, p.sg
    dr, dra = p.glob["dr"], p.glob["dra"]
    lnw, lnb, tdv, uv, hmv = p.glob["lnw"], p.glob["lnb"], p.glob["tdv"], p.glob["uv"], p.glob["hmv"]
    mk64, hsel, rsel, sgm = p.glob["mk64"], p.glob["hsel"], p.glob["rsel"], p.glob["sgm"]
    t_vec, t_sgm = p.glob["t_vec"], p.glob["t_sgm"]
    onesb, onesf, t_ones, t_onesf = p.glob["onesb"], p.glob["onesf"], p.glob["t_ones"], p.glob["t_onesf"]
    tok = p.glob["rtok"]

    xn2f = [p.sb(f"xn2f{i}", [128, 1 + T], BF16) for i in range(FT)]
    sqb = p.sb("sqb", [128, 512], BF16)
    rstd2 = p.sb("rstd2", [128, 512], F32)
    xdone = p.glob["xn0_done"]
    bf6 = 0
    ncol = 1 + T
    xn2_tok = 0
    for ck in range((ncol + 511) // 512):
        c0 = ck * 512
        w_ = min(ncol, c0 + 512) - c0
        mm_t = 0
        for ft in range(FT):
            tsq = p.op("s", "activation", sqb[:, 0:w_], xn0[ft][:, c0:c0 + w_], AF.Square,
                       waits=[xdone, (sm, mm_t), (sv, bf6)], inc=ss)
            mm_t = p.op("t", "matmul", pb[6][:, 0:w_], onesb[:], sqb[:, 0:w_],
                        start=(ft == 0), stop=(ft == FT - 1),
                        waits=[(ss, tsq), (sv, t_ones), (sv, bf6)], inc=sm)
        t1 = p.op("s", "activation", rstd2[:, 0:w_], pb[6][:, 0:w_], AF.Sqrt,
                  bias=p.const(1e-5), scale=1.0 / D, waits=[(sm, mm_t), (sv, xn2_tok)], inc=ss)
        bf6 = p.op("v", "reciprocal", rstd2[:, 0:w_], rstd2[:, 0:w_], waits=[(ss, t1)], inc=sv)
        for ft in range(FT):
            t2 = p.op("v", "tensor_tensor", xn2f[ft][:, c0:c0 + w_], xn0[ft][:, c0:c0 + w_],
                      rstd2[:, 0:w_], OP.mult, waits=[(sv, bf6)], inc=sv)
            xn2_tok = p.op("v", "tensor_scalar", xn2f[ft][:, c0:c0 + w_], xn2f[ft][:, c0:c0 + w_],
                           lnw[:, ft:ft + 1], lnb[:, ft:ft + 1], OP.mult, OP.add,
                           waits=[(sv, t2), t_vec], inc=sv)
    for ft in range(FT):
        xn2_tok = p.op("v", "tensor_tensor", xn2f[ft][:, 0:1], xn2f[ft][:, 0:1], hmv[:],
                       OP.mult, waits=[(sv, xn2_tok), t_vec], inc=sv)

    mx8 = p.sb("mx8", [128, 9], F32)
    tmx = 0
    for ft in range(FT):
        tmx = p.op("v", "reduce_max", mx8[:, ft:ft + 1], xn0[ft][:, 1:1 + T], axis=AX.X,
                   apply_absolute_value=True, waits=[xdone], inc=sv)
    tmx = p.op("v", "reduce_max", mx8[:, 8:9], mx8[:, 0:8], axis=AX.X,
               apply_absolute_value=True, waits=[(sv, tmx)], inc=sv)
    mxr = p.sb("mxr", [1, 129], F32)
    ttr = p.op("t", "transpose", pb[0][0:1, 0:128], mx8[:, 8:9], p.glob["ident"][:],
               waits=[(sv, tmx), xdone], inc=sm)
    tev = p.op("s", "copy", mxr[0:1, 0:128], pb[0][0:1, 0:128], waits=[(sm, ttr)], inc=ss)
    tpr = p.op("v", "reduce_max", mxr[0:1, 128:129], mxr[0:1, 0:128], axis=AX.X,
               waits=[(ss, tev)], inc=sv)
    tdm = p.dma("g", dra["armax_in"], mxr[0:1, 128:129], waits=[(sv, tpr)])
    tcc = p.op("g", "collective_compute", "AllReduce", OP.max, replica_groups=RG_ALL,
               ins=[dra["armax_in"]], outs=[dra["armax_out"]],
               waits=[(p.gld, tdm)], inc=sg)
    p.glob["armax_tok"] = (sg, tcc)

    spst = p.stream("spl", 2)
    for ft in range(FT):
        spst.load(dra["xn0_sp"][ft], xn0[ft][:, 1:1 + T], waits=[xdone])
        spst.load(dra["xn2_sp"][ft], xn2f[ft][:], waits=[(sv, xn2_tok)])
    sp_done = spst.done()
    p.glob["xn0_spilled"] = sp_done

    p.glob["sp_done"] = sp_done

def rwkv_main(p, io, dbg):
    nc, T = p.nc, p.T
    FT = 8
    NSC = T // 512
    C = 64
    NCH = 512 // C
    acc, xn0, pb = p.glob["acc"], p.glob.get("xn0"), p.glob["pb"]
    pbT = p.glob["pbT"]
    identb = p.glob["identb"]
    sv, ss, sm, sg = p.sv, p.ss, p.sm, p.sg
    dr, dra = p.glob["dr"], p.glob["dra"]
    lnw, lnb, tdv, uv, hmv = p.glob["lnw"], p.glob["lnb"], p.glob["tdv"], p.glob["uv"], p.glob["hmv"]
    mk64, hsel, rsel, sgm = p.glob["mk64"], p.glob["hsel"], p.glob["rsel"], p.glob["sgm"]
    t_vec, t_sgm = p.glob["t_vec"], p.glob["t_sgm"]
    onesb, onesf, t_ones, t_onesf = p.glob["onesb"], p.glob["onesf"], p.glob["t_ones"], p.glob["t_onesf"]
    tok = p.glob["rtok"]

    sp_done = p.glob["sp_done"]
    xn2sc = p.sb("xn2sc", [128, FT * 513], BF16)
    dsb = p.sb("dsb", [128, FT * 512], BF16)
    xmix = p.sb("xmix", [128, FT * 512], BF16)
    Wsb = [p.sb(f"Wsb{j}", [128, FT * 1024], BF16) for j in range(2)]
    ksb = p.sb("ksb", [128, FT * 512], BF16)
    vtok = p.sb("vtok", [128, 4 * 1024], BF16)
    vfe = p.sb("vfe", [128, FT * 512], BF16)
    rsb = p.sb("rsb", [128, FT * 512], BF16)
    gsb = p.sb("gsb", [128, FT * 512], BF16)
    ewsb = p.sb("ewsb", [128, FT * 512], BF16)
    Lsb = p.sb("Lsb", [128, FT * 65], F32)
    qh = p.sb("qh", [128, FT * C], BF16)
    qcch = p.sb("qcch", [128, FT * C], BF16)
    kt = p.sb("kt", [128, FT * C], BF16)
    kh = p.sb("kh", [128, FT * C], BF16)
    ka = p.sb("ka", [128, FT * C], F32)
    qe = p.sb("qe", [128, FT * C], BF16)
    dt8 = p.sb("dt8", [128, 8], F32)
    dcum = p.sb("dcum", [128, 8], F32)
    rku = p.sb("rku", [128, FT * C], BF16)
    rhosb = p.sb("rhosb", [16, C], BF16)
    amsb = p.sb("amsb", [128, H * C], BF16)
    khtok = p.sb("khtok", [128, FT * 128], BF16)
    dtmp = p.sb("dtmp", [128, FT * C], BF16)
    yraw = p.sb("yraw", [128, FT * C], BF16)
    Ssb = [p.sb(f"Ssb{i}", [128, NH], F32) for i in range(FT)]
    Sbb = [p.sb(f"Sbb{i}", [128, NH], BF16) for i in range(FT)]
    t_s0 = 0
    for i in range(FT):
        p.op("v", "memset", Ssb[i][:], 0.0, inc=sv)
        t_s0 = p.op("v", "memset", Sbb[i][:], 0.0, inc=sv)
    t_dc = p.op("v", "memset", dcum[:], 1.0, inc=sv)

    wst = p.stream("wld", 2)
    xst2 = p.stream("x2ld", 1)
    qst = p.stream("qsp", 2)
    qgt = p.stream("qg", 1)
    qct = p.stream("qc", 1)
    qyt = p.stream("qy", 1)
    bankfree = {0: None, 1: None, 2: None}
    tok = {}
    t_chain = t_s0
    WNAMES = ["wWk", "wWv", "wWr", "wWg", "wWw"]
    MIXI = {"wWk": 0, "wWv": 1, "wWr": 2, "wWw": 3}

    for sc in range(NSC):
        wv = [(sv, tok[("mix", "wWw", sc - 1)])] if sc else []
        t_x2 = 0
        for ft in range(FT):
            t_x2 = xst2.load(xn2sc[:, ft * 513:(ft + 1) * 513],
                             dra["xn2_sp"][ft][:, sc * 512:sc * 512 + 513],
                             waits=(sp_done + wv) if ft == 0 else (), adv=(ft == FT - 1))
        t_d = p.op("v", "tensor_tensor", vchunk(dsb, 0, 512), vchunk(xn2sc, 1, 512, step=513),
                   vchunk(xn2sc, 0, 512, step=513), OP.subtract,
                   waits=[t_x2, (sv, tok.get(("dfree", sc - 1), 0))], inc=sv)
        for wi, wn in enumerate(WNAMES):
            wbuf = Wsb[wi % 2]
            gtok = tok.get(("wmm", sc, wi - 2)) or tok.get(("wmm", sc - 1, wi + 3)) or 0
            t_w = 0
            for ci in range(FT):
                t_w = wst.load(wbuf[:, ci * 1024:(ci + 1) * 1024],
                               io[wn][ci * 128:(ci + 1) * 128, :],
                               waits=[(sm, gtok)] if (gtok and ci == 0) else (),
                               adv=(ci == FT - 1))
            if wn != "wWg":
                mi = MIXI[wn]
                wmprev = [(sm, tok[("wmm", sc, wi - 1)])] if wi else (
                    [(sm, tok[("wmm", sc - 1, 4)])] if sc else [])
                t_m = 0
                for ft in range(FT):
                    t_m = p.op("v", "scalar_tensor_tensor", xmix[:, ft * 512:(ft + 1) * 512],
                               dsb[:, ft * 512:(ft + 1) * 512], sgm[:, mi * 8 + ft:mi * 8 + ft + 1],
                               xn2sc[:, ft * 513:ft * 513 + 512], OP.mult, OP.add,
                               waits=([(sv, t_d), (ss, t_sgm)] + wmprev) if ft == 0 else [(sv, t_d)],
                               inc=sv)
                tok[("mix", wn, sc)] = t_m
            else:
                t_m = tok[("mix", "wWr", sc)]
            if wn == "wWw":
                tok[("dfree", sc)] = t_m
            if wn == "wWv":
                t_mm = 0
                for tb in range(4):
                    for colh in range(2):
                        gv = tok.get(("vtev", sc, tb - 1, colh)) or bankfree[colh]
                        guse = tok.get(("vtuse", sc - 1))
                        for ci in range(FT):
                            t_mm = p.op("t", "matmul", pb[colh][:, 0:512],
                                        prow(xmix, 0, 128, ci * 512 + tb * 128, 128),
                                        wbuf[:, ci * 1024 + colh * 512:ci * 1024 + colh * 512 + 512],
                                        start=(ci == 0), stop=(ci == FT - 1),
                                        waits=[(sv, t_m), t_w] + ([gv] if gv else []), inc=sm)
                        ev = p.op("s", "copy",
                                  vtok[:, tb * 1024 + colh * 512:tb * 1024 + colh * 512 + 512],
                                  pb[colh][:, 0:512],
                                  waits=[(sm, t_mm)] + ([guse] if (guse and tb == 0 and colh == 0) else []),
                                  inc=ss)
                        tok[("vtev", sc, tb, colh)] = (ss, ev)
                bankfree[0] = tok[("vtev", sc, 3, 0)]
                bankfree[1] = tok[("vtev", sc, 3, 1)]
                tok[("vtok", sc)] = ev
                for co in range(FT):
                    gv = tok.get(("vfev", sc, co - 1))
                    gbank = bankfree[2]
                    guse = tok.get(("vfuse", sc - 1))
                    for ci in range(FT):
                        t_mm = p.op("t", "matmul", pb[2][:, 0:512],
                                    wbuf[:, ci * 1024 + co * 128:ci * 1024 + co * 128 + 128],
                                    xmix[:, ci * 512:(ci + 1) * 512],
                                    start=(ci == 0), stop=(ci == FT - 1),
                                    waits=[(sv, t_m), t_w] + ([gv] if gv else [])
                                    + ([gbank] if (gbank and co == 0) else []), inc=sm)
                    ev = p.op("s", "copy", vfe[:, co * 512:(co + 1) * 512], pb[2][:, 0:512],
                              waits=[(sm, t_mm)] + ([guse] if (guse and co == 0) else []), inc=ss)
                    tok[("vfev", sc, co)] = (ss, ev)
                tok[("vfe", sc)] = ev
                bankfree[2] = tok[("vfev", sc, FT - 1)]
                tok[("wmm", sc, wi)] = t_mm
            else:
                dst = {"wWk": ksb, "wWr": rsb, "wWg": gsb, "wWw": ewsb}[wn]
                bank = wi % 2
                t_mm = 0
                ev = 0
                guse = tok.get((wn + "use", sc - 1))
                if isinstance(guse, Sem):
                    guse = (guse, guse.n)
                for co in range(FT):
                    gv = tok.get((wn + "ev", sc, co - 1)) or bankfree[bank]
                    for ci in range(FT):
                        t_mm = p.op("t", "matmul", pb[bank][:, 0:512],
                                    wbuf[:, ci * 1024 + co * 128:ci * 1024 + co * 128 + 128],
                                    xmix[:, ci * 512:(ci + 1) * 512],
                                    start=(ci == 0), stop=(ci == FT - 1),
                                    waits=[(sv, t_m), t_w] + ([gv] if gv else []), inc=sm)
                    dguard = [guse] if (guse and co == 0) else []
                    if wn == "wWk":
                        ev = p.op("s", "copy", dst[:, co * 512:(co + 1) * 512],
                                  pb[bank][:, 0:512], waits=[(sm, t_mm)] + dguard, inc=ss)
                        evt = (ss, ev)
                    elif wn in ("wWr", "wWg"):
                        ev = p.op("s", "activation", dst[:, co * 512:(co + 1) * 512],
                                  pb[bank][:, 0:512], AF.Sigmoid,
                                  waits=[(sm, t_mm)] + dguard, inc=ss)
                        evt = (ss, ev)
                    else:
                        tw1 = p.op("v", "tensor_scalar", ewsb[:, co * 512:(co + 1) * 512],
                                   pb[bank][:, 0:512], tdv[:, co:co + 1], -10.0, OP.add, OP.max,
                                   waits=[(sm, t_mm), t_vec] + dguard, inc=sv)
                        ev = p.op("v", "tensor_scalar", ewsb[:, co * 512:(co + 1) * 512],
                                  ewsb[:, co * 512:(co + 1) * 512], 10.0, None, OP.min,
                                  waits=[(sv, tw1)], inc=sv)
                        evt = (sv, ev)
                    tok[(wn + "ev", sc, co)] = evt
                bankfree[bank] = tok[(wn + "ev", sc, FT - 1)]
                if wn == "wWw":
                    ev = p.op("s", "activation", ewsb[:], ewsb[:], AF.Exp,
                              waits=[(sv, ev)], inc=ss)
                tok[("proj", wn, sc)] = ev
                tok[("wmm", sc, wi)] = t_mm
        t_gsp = qgt.load(vchunk(dr["g_sp"], sc * 512, 512, step=T, pitch=FT * T),
                         vchunk(gsb, 0, 512), waits=[(ss, tok[("proj", "wWg", sc)])])

        for cc in range(NCH):
            gi = sc * NCH + cc
            co0 = cc * C
            wK = (ss, tok[("proj", "wWk", sc)])
            wR = (ss, tok[("proj", "wWr", sc)])
            wE = (ss, tok[("proj", "wWw", sc)])
            wV = (ss, tok[("vtok", sc)])
            wF = (ss, tok[("vfe", sc)])
            tL0 = p.op("v", "memset", bass.AP(Lsb, 0, [[FT * 65, 128], [65, FT], [1, 1]]), 0.0,
                       waits=[(sv, tok.get(("Lfree", gi - 1), 0)),
                              (ss, tok.get(("Lfree_ss", gi - 1), 0))], inc=sv)
            tsc_ = 0
            for ct in range(FT):
                tsc_ = p.op("v", "tensor_tensor_scan", Lsb[:, ct * 65 + 1:ct * 65 + 65],
                            onesf[:, 0:C], ewsb[:, ct * 512 + co0:ct * 512 + co0 + C],
                            0.0, OP.mult, OP.subtract,
                            waits=[(sv, tL0), (sv, t_onesf), wE], inc=sv)
            t_qe = p.op("s", "activation", qe[:], bass.AP(Lsb, 0, [[FT * 65, 128], [65, FT], [1, C]]),
                        AF.Exp, waits=[(sv, tsc_), (sv, tok.get(("qefree", gi - 1), 0))], inc=ss)
            t_qh = p.op("v", "tensor_tensor", qh[:], qe[:], vchunk(rsb, co0, C), OP.mult,
                        waits=[(ss, t_qe), wR, (sm, tok.get(("qhfree", gi - 1), 0))], inc=sv)
            tok[("qefree", gi)] = t_qh
            t_dt8 = p.op("s", "activation", dt8[:], bass.AP(Lsb, 64, [[FT * 65, 128], [65, FT], [1, 1]]),
                         AF.Exp, waits=[(sv, tsc_), (sv, tok.get(("dt8free", gi - 1), 0))], inc=ss)
            tok[("Lfree_ss", gi)] = t_dt8
            t_qc = 0
            gqc = tok.get(("qcsp", gi - 1))
            for ft in range(FT):
                t_qc = p.op("v", "tensor_scalar", qcch[:, ft * C:(ft + 1) * C],
                            qh[:, ft * C:(ft + 1) * C], dcum[:, ft:ft + 1], None, OP.mult,
                            waits=([(sv, t_qh), (sv, t_dc)] + ([gqc] if gqc else []))
                            if ft == 0 else [(sv, t_qh)], inc=sv)
            t_ka = p.op("v", "tensor_scalar", ka[:], bass.AP(Lsb, 1, [[FT * 65, 128], [65, FT], [1, C]]),
                        -1.0, 60.0, OP.mult, OP.min, waits=[(sv, tsc_), (ss, tok.get(("kafree", gi - 1), 0))], inc=sv)
            t_ke = p.op("s", "activation", kt[:], ka[:], AF.Exp,
                        waits=[(sv, t_ka), (sm, tok.get(("ktfree", gi - 1), 0))], inc=ss)
            t_kt = p.op("v", "tensor_tensor", kt[:], kt[:], vchunk(ksb, co0, C), OP.mult,
                        waits=[(ss, t_ke), wK], inc=sv)
            t_kha = p.op("v", "tensor_tensor", ka[:],
                         bass.AP(Lsb, 64, [[FT * 65, 128], [65, FT], [0, C]]),
                         bass.AP(Lsb, 1, [[FT * 65, 128], [65, FT], [1, C]]), OP.subtract,
                         waits=[(sv, t_ka), (ss, t_ke)], inc=sv)
            tok[("kafree", gi)] = t_ke
            t_khe = p.op("s", "activation", kh[:], ka[:], AF.Exp,
                         waits=[(sv, t_kha), (sm, tok.get(("khfree", gi - 1), 0))], inc=ss)
            tok[("kafree", gi)] = t_khe
            t_kh = p.op("v", "tensor_tensor", kh[:], kh[:], vchunk(ksb, co0, C), OP.mult,
                        waits=[(ss, t_khe), wK], inc=sv)
            tok[("Lfree", gi)] = t_kha
            tok[("khfree2", gi)] = t_kh
            tok[("scanlast", gi)] = t_kha
            t_rk = p.op("v", "tensor_tensor", rku[:], vchunk(rsb, co0, C), vchunk(ksb, co0, C),
                        OP.mult, waits=[wR, wK, (sm, tok.get(("rkufree", gi - 1), 0))], inc=sv)
            tok[("rkufree2", gi)] = t_rk
            t_ru = 0
            for ft in range(FT):
                t_ru = p.op("v", "tensor_scalar", rku[:, ft * C:(ft + 1) * C],
                            rku[:, ft * C:(ft + 1) * C], uv[:, ft:ft + 1], None, OP.mult,
                            waits=[(sv, t_rk), t_vec] if ft == 0 else [(sv, t_rk)], inc=sv)
            ro = co0 % 128
            t_gr = 0
            for h in range(H):
                ct, hh = h // 2, h % 2
                bank = 2 + h // 8
                t_gr = p.op("t", "matmul", prow(pb[bank], ro, C, (h % 8) * C, C),
                            prow(kt, hh * 64, 64, ct * C, C),
                            prow(qh, hh * 64, 64, ct * C, C),
                            start=(h % 8 == 0), stop=(h % 8 == 7), skip_group_check=(ro == 64),
                            waits=[(sv, t_kt), (sv, t_qh), (sv, tok.get(("grfree", gi - 1), 0))],
                            inc=sm)
            tok[("ktfree", gi)] = t_gr
            tok[("qhfree", gi)] = 0
            t_am1 = p.op("v", "tensor_tensor", prow(amsb, ro, C, 0, 8 * C), prow(pb[2], ro, C, 0, 8 * C),
                         mk64[:, 0:8 * C], OP.mult,
                         waits=[(sm, t_gr), t_vec, (sm, tok.get(("amfree", gi - 1), 0))], inc=sv)
            t_am = p.op("v", "tensor_tensor", prow(amsb, ro, C, 8 * C, 8 * C), prow(pb[3], ro, C, 0, 8 * C),
                        mk64[:, 8 * C:16 * C], OP.mult, waits=[(sm, t_gr)], inc=sv)
            t_rho = 0
            for ct in range(FT):
                t_rho = p.op("t", "matmul", prow(pb[4], 0, 16, 0, C), hsel[:, ct * 16:(ct + 1) * 16],
                             rku[:, ct * C:(ct + 1) * C], start=(ct == 0), stop=(ct == FT - 1),
                             waits=[(sv, t_ru), t_vec, (ss, tok.get(("rhofree", gi - 1), 0))], inc=sm)
            tok[("rkufree", gi)] = t_rho
            t_rhoe = p.op("s", "copy", rhosb[:], prow(pb[4], 0, 16, 0, C),
                          waits=[(sm, t_rho)], inc=ss)
            tok[("rhofree", gi)] = t_rhoe
            t_rep = 0
            for ct in range(FT):
                t_rep = p.op("t", "matmul", pb[5][:, ct * C:(ct + 1) * C], rsel[:, ct * 128:(ct + 1) * 128], rhosb[:],
                             start=(ct == 0), stop=(ct == FT - 1),
                             waits=[(ss, t_rhoe), t_vec, (sv, tok.get(("repfree", gi - 1), 0))], inc=sm)
            t_y = 0
            for h in range(H):
                ct, hh = h // 2, h % 2
                tb = co0 // 128
                t_y = p.op("t", "matmul", prow(pb[6], hh * 64, 64, ct * C, C),
                           prow(vtok, ro, 64, tb * 1024 + h * 64, 64),
                           prow(amsb, ro, C, (h % 8) * C + (h // 8) * 8 * C, C),
                           start=(h <= 1), stop=False, skip_group_check=(hh == 1),
                           waits=[(sv, t_am), (sv, t_am1), wV,
                                  (sv, tok.get(("yfree", gi - 1), 0))], inc=sm)
            for h in range(H):
                ct, hh = h // 2, h % 2
                t_y = p.op("t", "matmul", prow(pb[6], hh * 64, 64, ct * C, C),
                           prow(Sbb[ct], hh * 64, 64, 0, NH),
                           prow(qh, hh * 64, 64, ct * C, C),
                           start=False, stop=(h >= H - 2), skip_group_check=(hh == 1),
                           waits=[(sv, t_qh), (sv, t_chain)], inc=sm)
            tok[("qhfree", gi)] = t_y
            tok[("amfree", gi)] = t_y
            for rnd in range(2):
                t_tr = 0
                for j in range(4):
                    ct = rnd * 4 + j
                    gv = tok.get(("khtev", gi, rnd - 1)) or tok.get(("khtuse", gi - 1))
                    t_tr = p.op("t", "transpose", prow(pbT, ro, C, j * 128, 128),
                                kh[:, ct * C:(ct + 1) * C], identb[:],
                                waits=[(sv, t_kh)] + ([gv] if (gv and j == 0) else []), inc=sm)
                tok[("khtev", gi, rnd)] = (ss, p.op("s", "copy", prow(khtok, ro, C, rnd * 512, 512),
                                                    prow(pbT, ro, C, 0, 512),
                                                    waits=[(sm, t_tr)], inc=ss))
            tok[("khfree", gi)] = t_tr
            t_khtok = tok[("khtev", gi, 1)]
            t_ds = 0
            for h in range(H):
                ct, hh = h // 2, h % 2
                bank = 2 + ct // 4
                tb = co0 // 128
                t_ds = p.op("t", "matmul", prow(pb[bank], hh * 64, 64, (ct % 4) * 64, 64),
                            prow(khtok, ro, C, ct * 128 + hh * 64, 64),
                            prow(vtok, ro, C, tb * 1024 + h * 64, 64),
                            start=(h % 8 <= 1), stop=(h % 8 >= 6), skip_group_check=(hh == 1),
                            waits=[t_khtok, wV, (sv, t_am), (sv, t_am1),
                                   (sv, tok.get(("dsfree", gi - 1), 0))], inc=sm)
            tok[("khtuse", gi)] = (sm, t_ds)
            t_su = 0
            for ct in range(FT):
                bank = 2 + ct // 4
                t_su = p.op("v", "scalar_tensor_tensor", Ssb[ct][:], Ssb[ct][:],
                            dt8[:, ct:ct + 1], prow(pb[bank], 0, 128, (ct % 4) * 64, 64),
                            OP.mult, OP.add,
                            waits=[(ss, t_dt8), (sm, t_ds), (sv, t_chain)] if ct == 0
                            else [(sm, t_ds)], inc=sv)
            for ct in range(FT):
                t_su = p.op("v", "tensor_copy", Sbb[ct][:], Ssb[ct][:],
                            waits=[(sv, t_su)] if ct == 0 else [], inc=sv)
            t_chain = t_su
            tok[("dsfree", gi)] = t_su
            tok[("grfree", gi)] = t_su
            t_dc = p.op("v", "tensor_tensor", dcum[:], dcum[:], dt8[:], OP.mult,
                        waits=[(sv, t_qc), (ss, t_dt8)], inc=sv)
            tok[("dt8free", gi)] = t_dc
            t_d1 = p.op("v", "tensor_tensor", dtmp[:], pb[5][:, 0:8 * C], vchunk(vfe, co0, C),
                        OP.mult, waits=[(sm, t_rep), wF, (sv, tok.get(("dtmpfree", gi - 1), 0))],
                        inc=sv)
            tok[("repfree", gi)] = t_d1
            gys = tok.get(("ysp", gi - 1))
            t_ye = p.op("v", "tensor_tensor", yraw[:], prow(pb[6], 0, 128, 0, 8 * C), dtmp[:],
                        OP.add, waits=[(sm, t_y), (sv, t_d1)] + ([gys] if gys else []), inc=sv)
            tok[("dtmpfree", gi)] = t_ye
            tok[("yfree", gi)] = t_ye
            t_sq = qct.load(vchunk(dr["qc_sp"], sc * 512 + co0, C, step=T, pitch=FT * T),
                            qcch[:], waits=[(sv, t_qc)])
            tok[("qcsp", gi)] = t_sq
            t_sy = qyt.load(vchunk(dr["yraw_sp"], sc * 512 + co0, C, step=T, pitch=FT * T),
                            yraw[:], waits=[(sv, t_ye)])
            tok[("ysp", gi)] = t_sy
        lg = sc * NCH + NCH - 1
        tok[("vtuse", sc)] = (sv, tok[("dsfree", lg)])
        tok[("vfuse", sc)] = (sv, tok[("dtmpfree", lg)])
        tok[("wWkuse", sc)] = (sv, tok[("khfree2", lg)])
        tok[("wWruse", sc)] = (sv, tok[("rkufree2", lg)])
        tok[("wWguse", sc)] = t_gsp
        tok[("wWwuse", sc)] = (sv, tok[("scanlast", lg)])

    sst = p.stream("sste", 1)
    for ft in range(FT):
        t_se = sst.load(dra["s_snd"][ft], Ssb[ft][:], eng="g",
                        waits=[(sv, t_chain)] if ft == 0 else (), adv=False)
    t_cc2 = p.op("g", "collective_compute", "AllGather", OP.bypass, replica_groups=RG_PAIRS,
                 ins=[dra["s_snd"]], outs=[dra["s_rcv"]], waits=[t_se], inc=sg)
    sinb = p.sb("sinb", [128, FT * NH], BF16)
    t_si = 0
    for ft in range(FT):
        t_si = p.dma("g", sinb[:, ft * NH:(ft + 1) * NH], dra["s_rcv"][0][ft],
                     waits=[(sg, t_cc2)] if ft == 0 else ())
    t_sin0 = 0
    for ft in range(FT):
        t_sin0 = p.op("v", "tensor_scalar", sinb[:, ft * NH:(ft + 1) * NH],
                      sinb[:, ft * NH:(ft + 1) * NH], hmv[:], None, OP.mult,
                      waits=[(p.gld, t_si), t_vec] if ft == 0 else [], inc=sv)
    t_sin = (sv, t_sin0)

    p.glob["t_sin"] = t_sin
    p.glob["qst_done"] = qgt.done() + qct.done() + qyt.done()
    p.glob["sinb"] = sinb

def rwkv_post(p, io, dbg):
    nc, T = p.nc, p.T
    FT = 8
    NSC = T // 512
    C = 64
    NCH = 512 // C
    acc, xn0, pb = p.glob["acc"], p.glob.get("xn0"), p.glob["pb"]
    pbT = p.glob["pbT"]
    identb = p.glob["identb"]
    sv, ss, sm, sg = p.sv, p.ss, p.sm, p.sg
    dr, dra = p.glob["dr"], p.glob["dra"]
    lnw, lnb, tdv, uv, hmv = p.glob["lnw"], p.glob["lnb"], p.glob["tdv"], p.glob["uv"], p.glob["hmv"]
    mk64, hsel, rsel, sgm = p.glob["mk64"], p.glob["hsel"], p.glob["rsel"], p.glob["sgm"]
    t_vec, t_sgm = p.glob["t_vec"], p.glob["t_sgm"]
    onesb, onesf, t_ones, t_onesf = p.glob["onesb"], p.glob["onesf"], p.glob["t_ones"], p.glob["t_onesf"]
    tok = p.glob["rtok"]

    t_sin = p.glob["t_sin"]
    qdone = p.glob["qst_done"]
    sinb = p.glob["sinb"]
    dtmp = p.sb("dtmp2", [128, 8 * C], BF16)
    Wo_sb = p.sb("Wo_sb", [128, FT * 1024], BF16)
    wost = p.stream("wost", 1)
    for ci in range(FT):
        t_wo = wost.load(Wo_sb[:, ci * 1024:(ci + 1) * 1024],
                         io["wWo"][ci * 128:(ci + 1) * 128, :], adv=False)
    ysc = p.sb("ysc", [128, FT * 512], BF16)
    qcsc = p.sb("qcsc", [128, FT * 512], BF16)
    gsc2 = p.sb("gsc2", [128, FT * 512], BF16)
    pst2 = p.stream("pp", 1)
    for sc in range(NSC):
        wprev = [(sv, tok[("y2done", sc - 1)])] if sc else []
        t_ly = pst2.load(ysc[:], vchunk(dr["yraw_sp"], sc * 512, 512, step=T, pitch=FT * T),
                         waits=(qdone + wprev) if sc == 0 else wprev, adv=False)
        t_lq = pst2.load(qcsc[:], vchunk(dr["qc_sp"], sc * 512, 512, step=T, pitch=FT * T),
                         adv=False)
        t_lg = pst2.load(gsc2[:], vchunk(dr["g_sp"], sc * 512, 512, step=T, pitch=FT * T),
                         adv=False)
        t_l = t_lg
        for cc in range(NCH):
            co0 = cc * C
            t_cm = 0
            for h in range(H):
                ct, hh = h // 2, h % 2
                t_cm = p.op("t", "matmul", prow(pb[6], hh * 64, 64, ct * C, C),
                            prow(sinb, hh * 64, 64, ct * NH, NH),
                            prow(qcsc, hh * 64, 64, ct * 512 + co0, C),
                            start=(h <= 1), stop=(h >= H - 2), skip_group_check=(hh == 1),
                            waits=[t_sin, t_l, (sv, tok.get(("y2e", sc, cc - 1),
                                                            tok.get(("pb6free", sc - 1), 0)))],
                            inc=sm)
            t_ad = p.op("v", "tensor_tensor", dtmp[:], prow(pb[6], 0, 128, 0, 8 * C),
                        vchunk(ysc, co0, C), OP.add,
                        waits=[(sm, t_cm), t_l, (sv, tok.get(("y2e", sc, cc - 1), 0))], inc=sv)
            t_y2 = p.op("v", "tensor_tensor", vchunk(ysc, co0, C), dtmp[:],
                        vchunk(gsc2, co0, C), OP.mult, waits=[(sv, t_ad)], inc=sv)
            tok[("y2e", sc, cc)] = t_y2
        tok[("pb6free", sc)] = tok[("y2e", sc, NCH - 1)]
        t_acc = 0
        for co in range(FT):
            t_mm = 0
            for ci in range(FT):
                t_mm = p.op("t", "matmul", pb[0][:, 0:512],
                            Wo_sb[:, ci * 1024 + co * 128:ci * 1024 + co * 128 + 128],
                            ysc[:, ci * 512:(ci + 1) * 512],
                            start=(ci == 0), stop=(ci == FT - 1),
                            waits=[(sv, tok[("y2e", sc, NCH - 1)]), t_wo,
                                   (sv, tok.get(("accadd", sc, co - 1),
                                                tok.get(("accadd", sc - 1, FT - 1), 0)))],
                            inc=sm)
            t_acc = p.op("v", "tensor_tensor", acc[co][:, sc * 512:(sc + 1) * 512],
                         acc[co][:, sc * 512:(sc + 1) * 512], pb[0][:, 0:512], OP.add,
                         waits=[(sm, t_mm)], inc=sv)
            tok[("accadd", sc, co)] = t_acc
        tok[("y2done", sc)] = t_acc
    p.glob["attn_done"] = (sv, sv.n)



def phase_ffn0(p, io, dbg):
    nc, T = p.nc, p.T
    FT = 8
    IT = DI // 128
    NTC = T // 512
    acc, pb, pbT = p.glob["acc"], p.glob["pb"], p.glob["pbT"]
    sv, ss, sm, sg = p.sv, p.ss, p.sm, p.sg
    dr, dra = p.glob["dr"], p.glob["dra"]
    identb = p.glob["identb"]
    onesf = p.glob["onesf"]
    t_onesf = p.glob["t_onesf"]
    tok = {}

    h_sp = nc.dram_tensor("h_sp", [IT, 128, T], BF16,
                          kind="ExternalOutput" if dbg else "Internal")
    hra = h_sp.ap()
    ar2_in = nc.dram_tensor("ar2_in", [1, 1], F32, kind="Internal").ap()
    ar2_out = nc.dram_tensor("ar2_out", [1, 1], F32, kind="Internal").ap()

    sc1 = p.sb("fsc", [1, 8], F32)
    scb = p.sb("fscb", [128, 8], F32)
    cst = p.stream("fcl", 1)
    t_cx = cst.load(sc1[0:1, 0:1], io["clip_x"], adv=False)
    t_cd = cst.load(sc1[0:1, 1:2], io["clip_down"], adv=False)
    t_am = cst.load(sc1[0:1, 2:3], dra["armax_out"], adv=False,
                    waits=[p.glob["armax_tok"]])
    t_s1 = p.op("s", "activation", sc1[0:1, 3:4], sc1[0:1, 0:1], AF.Sigmoid,
                waits=[t_am], inc=ss)
    t_s2 = p.op("v", "tensor_tensor", sc1[0:1, 4:5], sc1[0:1, 3:4], sc1[0:1, 2:3],
                OP.mult, waits=[(ss, t_s1)], inc=sv)
    t_s3 = p.op("v", "reciprocal", sc1[0:1, 5:6], sc1[0:1, 4:5], waits=[(sv, t_s2)], inc=sv)
    t_s4 = p.op("v", "tensor_scalar", sc1[0:1, 5:6], sc1[0:1, 5:6], 127.0, None, OP.mult,
                waits=[(sv, t_s3)], inc=sv)
    t_s5 = p.op("v", "tensor_scalar", sc1[0:1, 4:5], sc1[0:1, 4:5], 1.0 / 127.0, None,
                OP.mult, waits=[(sv, t_s4)], inc=sv)
    t_bc = p.op("t", "matmul", pb[3][:, 0:2],
                onesf[0:1, 0:128], sc1[0:1, 4:6], start=True, stop=True,
                waits=[(sv, t_s5), (sv, t_onesf), p.glob["attn_done"]], inc=sm)
    t_bce = p.op("s", "copy", scb[:, 0:2], pb[3][:, 0:2], waits=[(sm, t_bc)], inc=ss)

    wrow = [p.sb("fwr0", [128, max(D, DI)], F32), p.sb("fwr1", [128, max(D, DI)], F32)]
    wi32 = p.sb("fwi", [128, max(D, DI)], I32)
    wtmp = p.sb("fwt", [128, max(D, DI)], F32)
    wqrow = p.sb("fwq", [128, max(D, DI)], BF16)
    rmax = p.sb("frm", [128, 4], F32)
    wst = p.stream("fwl", 2)
    with p.scope():
        xq = p.sb("xq", [128, FT * T], BF16)
        xb0 = [p.sb(f"xb0_{j}", [128, T], BF16) for j in range(2)]
        xi32 = p.sb("xi32", [128, T], I32)
        xf1 = p.sb("xf1", [128, T], F32)
        xf2 = p.sb("xf2", [128, T], F32)
        xst = p.stream("fxl", 2)
        for ft in range(FT):
            t_lx = xst.load(xb0[ft % 2][:], dra["xn0_sp"][ft],
                            waits=[(sv, tok.get(("xq", ft - 2), 0))] if ft >= 2 else [])
            t1 = p.op("v", "tensor_scalar", xf2[:], xb0[ft % 2][:], scb[:, 1:2], 512.0, OP.mult, OP.add,
                      waits=[t_lx, (ss, t_bce), (sv, tok.get(("xq", ft - 1), 0))], inc=sv)
            t2 = p.op("v", "tensor_copy", xi32[:], xf2[:], waits=[(sv, t1)], inc=sv)
            t3 = p.op("v", "tensor_copy", xf1[:], xi32[:], waits=[(sv, t2)], inc=sv)
            t4 = p.op("v", "tensor_tensor", xf2[:], xf2[:], xf1[:], OP.subtract,
                      waits=[(sv, t3)], inc=sv)
            t5 = p.op("v", "scalar_tensor_tensor", xf1[:], xf2[:], 0.5, xf1[:], OP.is_ge,
                      OP.add, waits=[(sv, t4)], inc=sv)
            t6 = p.op("v", "tensor_scalar", xf1[:], xf1[:], -512.0, -128.0, OP.add, OP.max,
                      waits=[(sv, t5)], inc=sv)
            t7 = p.op("v", "tensor_scalar", xq[:, ft * T:(ft + 1) * T], xf1[:], 127.0, None,
                      OP.min, waits=[(sv, t6)], inc=sv)
            tok[("xq", ft)] = t7
        t_xq = tok[("xq", FT - 1)]


        def quant_row_tile(buf, ncols, t_lw, scale_col, gate_prev=None):
            tq0 = p.op("v", "reduce_max", rmax[:, scale_col:scale_col + 1], buf[:, 0:ncols],
                       axis=AX.X, apply_absolute_value=True,
                       waits=[t_lw], inc=sv)
            tq1 = p.op("v", "reciprocal", rmax[:, 3:4], rmax[:, scale_col:scale_col + 1],
                       waits=[(sv, tq0)], inc=sv)
            tq2 = p.op("v", "tensor_scalar", rmax[:, 3:4], rmax[:, 3:4], 127.0, None, OP.mult,
                       waits=[(sv, tq1)], inc=sv)
            tq3 = p.op("v", "tensor_scalar", wtmp[:, 0:ncols], buf[:, 0:ncols], rmax[:, 3:4],
                       512.0, OP.mult, OP.add, waits=[(sv, tq2)], inc=sv)
            tq4 = p.op("v", "tensor_copy", wi32[:, 0:ncols], wtmp[:, 0:ncols],
                       waits=[(sv, tq3)], inc=sv)
            tq5 = p.op("v", "tensor_copy", buf[:, 0:ncols], wi32[:, 0:ncols],
                       waits=[(sv, tq4)], inc=sv)
            tq6 = p.op("v", "tensor_tensor", wtmp[:, 0:ncols], wtmp[:, 0:ncols], buf[:, 0:ncols],
                       OP.subtract, waits=[(sv, tq5)], inc=sv)
            tq7 = p.op("v", "scalar_tensor_tensor", buf[:, 0:ncols], wtmp[:, 0:ncols], 0.5,
                       buf[:, 0:ncols], OP.is_ge, OP.add, waits=[(sv, tq6)], inc=sv)
            tq8 = p.op("v", "tensor_scalar", wqrow[:, 0:ncols], buf[:, 0:ncols], -512.0, None,
                       OP.add, waits=[(sv, tq7), (sm, tok.get("wqrow_free", 0))], inc=sv)
            return tq8

        def transpose_to_arr(dst, ncolt, t_q, arr_guard=None, dst_off=0):
            t_e = tok.get("pbT", 0)
            for jb in range(0, ncolt, 4):
                nb = min(4, ncolt - jb)
                t_tr = 0
                for j in range(nb):
                    t_tr = p.op("t", "transpose", pbT[:, j * 128:(j + 1) * 128],
                                wqrow[:, (jb + j) * 128:(jb + j + 1) * 128], identb[:],
                                waits=[(sv, t_q), (ss, t_e)], inc=sm)
                t_e = p.op("s", "copy", dst[:, dst_off + jb * 128:dst_off + (jb + nb) * 128],
                           pbT[:, 0:nb * 128],
                           waits=[(sm, t_tr)] + ([arr_guard] if (arr_guard and jb == 0) else []),
                           inc=ss)
                tok["wqrow_free"] = t_tr
            tok["pbT"] = t_e
            return t_e

        wqg2 = [p.sb(f"wqg{j}", [128, FT * 128], BF16) for j in range(2)]
        wqu2 = [p.sb(f"wqu{j}", [128, FT * 128], BF16) for j in range(2)]
        sxg = p.sb("sxg", [128, 2], F32)
        hsb2 = [p.sb(f"hsb{j}", [128, 512], BF16) for j in range(2)]
        sgb2 = [p.sb(f"sgb{j}", [128, 512], BF16) for j in range(2)]
        hmx = p.sb("hmx", [128, IT + 2], F32)
        hst = p.stream("fhs", 2)
        for it in range(IT):
            t_lg = wst.load(wrow[0][:, 0:D], io["w_gate_r"][it * 128:(it + 1) * 128, :],
                            waits=[(sv, tok.get(("gq", it - 1), 0))])
            t_lu = wst.load(wrow[1][:, 0:D], io["w_up_r"][it * 128:(it + 1) * 128, :],
                            waits=[(sv, tok.get(("uq", it - 1), 0))])
            wqg = wqg2[it % 2]
            wqu = wqu2[it % 2]
            tqg = quant_row_tile(wrow[0], D, t_lg, 0)
            tok[("gq", it)] = tqg
            teg = transpose_to_arr(wqg, FT, tqg,
                                   (sm, tok.get(("gmm", it - 2), 0)) if it >= 2 else None)
            ts1 = p.op("v", "tensor_scalar", sxg[:, 0:1], rmax[:, 0:1], 1.0 / 127.0, None,
                       OP.mult, waits=[(sv, tqg)], inc=sv)
            ts2 = p.op("v", "tensor_tensor", sxg[:, 0:1], sxg[:, 0:1], scb[:, 0:1], OP.mult,
                       waits=[(sv, ts1), (ss, t_bce)], inc=sv)
            tqu = quant_row_tile(wrow[1], D, t_lu, 1)
            tok[("uq", it)] = tqu
            teu = transpose_to_arr(wqu, FT, tqu,
                                   (sm, tok.get(("umm", it - 2), 0)) if it >= 2 else None)
            ts3 = p.op("v", "tensor_scalar", sxg[:, 1:2], rmax[:, 1:2], 1.0 / 127.0, None,
                       OP.mult, waits=[(sv, tqu)], inc=sv)
            ts4 = p.op("v", "tensor_tensor", sxg[:, 1:2], sxg[:, 1:2], scb[:, 0:1], OP.mult,
                       waits=[(sv, ts3)], inc=sv)

            for tc in range(NTC):
                t_mmg = 0
                for ci in range(FT):
                    t_mmg = p.op("t", "matmul", pb[0][:, 0:512],
                                 wqg[:, ci * 128:(ci + 1) * 128],
                                 xq[:, ci * T + tc * 512:ci * T + tc * 512 + 512],
                                 start=(ci == 0), stop=(ci == FT - 1),
                                 waits=[(ss, teg), (sv, t_xq), (sv, tok.get(("hev", it, tc - 1), tok.get(("hev", it - 1, NTC - 1), 0)))],
                                 inc=sm)
                t_mmu = 0
                for ci in range(FT):
                    t_mmu = p.op("t", "matmul", pb[1][:, 0:512],
                                 wqu[:, ci * 128:(ci + 1) * 128],
                                 xq[:, ci * T + tc * 512:ci * T + tc * 512 + 512],
                                 start=(ci == 0), stop=(ci == FT - 1),
                                 waits=[(ss, teu), (sv, t_xq), (sv, tok.get(("hev", it, tc - 1), tok.get(("hev", it - 1, NTC - 1), 0)))],
                                 inc=sm)
                par = (it * NTC + tc) % 2
                hsb, sgb = hsb2[par], sgb2[par]
                t_sg = p.op("s", "activation", sgb[:], pb[0][:, 0:512], AF.Silu,
                            scale=sxg[:, 0:1],
                            waits=[(sm, t_mmg), (sv, ts2),
                                   tok.get(("hq2", it * NTC + tc - 2), (sv, 0))], inc=ss)
                t_h = p.op("v", "scalar_tensor_tensor", hsb[:], pb[1][:, 0:512], sxg[:, 1:2],
                           sgb[:], OP.mult, OP.mult,
                           waits=[(sm, t_mmu), (ss, t_sg), (sv, ts4)] +
                                 ([tok[("hst", (it * NTC + tc - 2) // NTC, (it * NTC + tc - 2) % NTC)]]
                                  if it * NTC + tc >= 2 else []), inc=sv)
                t_hm = p.op("v", "reduce_max", hmx[:, it:it + 1], hsb[:], axis=AX.X,
                            apply_absolute_value=True, waits=[(sv, t_h)],
                            inc=sv) if tc == 0 else p.op(
                    "v", "tensor_tensor_reduce" if False else "reduce_max",
                    hmx[:, IT:IT + 1], hsb[:], axis=AX.X, apply_absolute_value=True,
                    waits=[(sv, t_h)], inc=sv)
                if tc != 0:
                    t_hm = p.op("v", "tensor_tensor", hmx[:, it:it + 1], hmx[:, it:it + 1],
                                hmx[:, IT:IT + 1], OP.max, waits=[(sv, t_hm)], inc=sv)
                t_hs = hst.load(hra[it][:, tc * 512:(tc + 1) * 512], hsb[:], waits=[(sv, t_h)])
                tok[("hev", it, tc)] = t_h
                tok[("hst", it, tc)] = t_hs
                tok[("gmm", it)] = t_mmg
                tok[("umm", it)] = t_mmu
                tok[("hq2", it * NTC + tc)] = (sv, t_h)
        t_hm2 = p.op("v", "reduce_max", hmx[:, IT + 1:IT + 2], hmx[:, 0:IT], axis=AX.X,
                     apply_absolute_value=True, waits=[(sv, sv.n)], inc=sv)
        mxr2 = p.sb("mxr2", [1, 129], F32)
        ttr2 = p.op("t", "transpose", pb[2][0:1, 0:128], hmx[:, IT + 1:IT + 2],
                    p.glob["ident"][:], waits=[(sv, t_hm2)], inc=sm)
        tev2 = p.op("s", "copy", mxr2[0:1, 0:128], pb[2][0:1, 0:128], waits=[(sm, ttr2)], inc=ss)
        tpr2 = p.op("v", "reduce_max", mxr2[0:1, 128:129], mxr2[0:1, 0:128], axis=AX.X,
                    waits=[(ss, tev2)], inc=sv)
        tdm2 = p.dma("g", ar2_in, mxr2[0:1, 128:129], waits=[(sv, tpr2)])
        tcc2 = p.op("g", "collective_compute", "AllReduce", OP.max, replica_groups=RG_ALL,
                    ins=[ar2_in], outs=[ar2_out], waits=[(p.gld, tdm2)], inc=sg)
        t_h1 = cst.load(sc1[0:1, 6:7], ar2_out, adv=False, waits=[(sg, tcc2)])
        t_h2 = p.op("s", "activation", sc1[0:1, 1:2], sc1[0:1, 1:2], AF.Sigmoid,
                    waits=[t_h1], inc=ss)
        t_h3 = p.op("v", "tensor_tensor", sc1[0:1, 6:7], sc1[0:1, 6:7], sc1[0:1, 1:2],
                    OP.mult, waits=[(ss, t_h2)], inc=sv)
        t_h4 = p.op("v", "reciprocal", sc1[0:1, 7:8], sc1[0:1, 6:7], waits=[(sv, t_h3)], inc=sv)
        t_h5 = p.op("v", "tensor_scalar", sc1[0:1, 7:8], sc1[0:1, 7:8], 127.0, None, OP.mult,
                    waits=[(sv, t_h4)], inc=sv)
        t_h6 = p.op("v", "tensor_scalar", sc1[0:1, 6:7], sc1[0:1, 6:7], 1.0 / 127.0, None,
                    OP.mult, waits=[(sv, t_h5)], inc=sv)
        t_b2 = p.op("t", "matmul", pb[3][:, 2:4], onesf[0:1, 0:128], sc1[0:1, 6:8],
                    start=True, stop=True, waits=[(sv, t_h6), (sv, t_onesf), (ss, t_bce)], inc=sm)
        t_b2e = p.op("s", "copy", scb[:, 2:4], pb[3][:, 2:4], waits=[(sm, t_b2)], inc=ss)

        hq_sp = nc.dram_tensor("hq_sp", [IT, 128, T], BF16, kind="Internal")
        hqa = hq_sp.ap()
        hq2 = [p.sb(f"fhq{j}", [128, T], BF16) for j in range(2)]
        hf1 = xf1[:, 0:T]
        hf2 = xf2[:, 0:T]
        hqi = xi32[:, 0:T]
        hlst = p.stream("fhl", 2)
        hqst = p.stream("fhq", 2)
        for it in range(IT):
            t_lh = hlst.load(hq2[it % 2][:], hra[it][:, 0:T],
                             waits=[tok[("hst", it, NTC - 1)],
                                    tok.get(("hqs", it - 2), (sv, 0)),
                                    (sv, tok.get(("hqq", it - 2), 0))])
            q1 = p.op("v", "tensor_scalar", hf2, hq2[it % 2][:], scb[:, 3:4], 512.0,
                      OP.mult, OP.add, waits=[t_lh, (ss, t_b2e)], inc=sv)
            q2 = p.op("v", "tensor_copy", hqi, hf2, waits=[(sv, q1)], inc=sv)
            q3 = p.op("v", "tensor_copy", hf1, hqi, waits=[(sv, q2)], inc=sv)
            q4 = p.op("v", "tensor_tensor", hf2, hf2, hf1, OP.subtract,
                      waits=[(sv, q3)], inc=sv)
            q5 = p.op("v", "scalar_tensor_tensor", hf1, hf2, 0.5, hf1,
                      OP.is_ge, OP.add, waits=[(sv, q4)], inc=sv)
            q6 = p.op("v", "tensor_scalar", hf1, hf1, -512.0, -128.0, OP.add,
                      OP.max, waits=[(sv, q5)], inc=sv)
            q7 = p.op("v", "tensor_scalar", hq2[it % 2][:], hf1, 127.0, None, OP.min,
                      waits=[(sv, q6), tok.get(("hqs", it - 2), (sv, 0))], inc=sv)
            tok[("hqq", it)] = q7
            tok[("hqs", it)] = hqst.load(hqa[it][:, 0:T], hq2[it % 2][:],
                                         waits=[(sv, q7)])
        hq_done = hqst.done()

    wqd = p.sb("wqd", [128, FT * IT * 128], BF16)
    sdh = p.sb("sdh", [128, 8], F32)
    for co in range(FT):
        wg0 = ([(sv, tok[("gq", IT - 1)])] if co == 0 else
               [(sv, tok[("uq", IT - 1)])] if co == 1 else [])
        t_ld = wst.load(wrow[co % 2][:, 0:DI], io["w_down_r"][co * 128:(co + 1) * 128, :],
                        waits=[(sv, tok.get(("dq", co - 2), 0))] + wg0)
        tqd = quant_row_tile(wrow[co % 2], DI, t_ld, 0)
        tok[("dq", co)] = tqd
        ted = transpose_to_arr(wqd, IT, tqd, dst_off=co * IT * 128)
        td1 = p.op("v", "tensor_scalar", sdh[:, co:co + 1], rmax[:, 0:1], 1.0 / 127.0,
                   None, OP.mult, waits=[(sv, tqd)], inc=sv)
        td2 = p.op("v", "tensor_tensor", sdh[:, co:co + 1], sdh[:, co:co + 1], scb[:, 2:3],
                   OP.mult, waits=[(sv, td1), (ss, t_b2e)], inc=sv)
        tok[("dsc", co)] = td2
        tok[("dte", co)] = ted
    hrd = [p.sb("fhr0", [128, IT * 512], BF16)]
    hrst = p.stream("fhr", 2)
    for tc in range(NTC):
        wfree = ([(sm, tok[("dmm", tc - 1, FT - 1)])] if tc >= 1 else [])
        t_lq = 0
        for it in range(IT):
            t_lq = hrst.load(hrd[0][:, it * 512:(it + 1) * 512],
                             hqa[it][:, tc * 512:(tc + 1) * 512],
                             waits=(wfree if it == 0 else [])
                             + (hq_done if (tc == 0 and it == 0) else []),
                             adv=(it == IT - 1))
        for co in range(FT):
            bank = co % 4
            t_mm = 0
            for it in range(IT):
                t_mm = p.op("t", "matmul", pb[bank][:, 0:512],
                            wqd[:, co * IT * 128 + it * 128:co * IT * 128 + (it + 1) * 128],
                            hrd[0][:, it * 512:(it + 1) * 512],
                            start=(it == 0), stop=(it == IT - 1),
                            waits=[(ss, tok[("dte", co)]), t_lq]
                            + ([(sv, tok.get(("dacc", tc, co - 4),
                                             tok.get(("dacc", tc - 1, co + 4), 0)))]
                               if it == 0 else []),
                            inc=sm)
            tok[("dmm", tc, co)] = t_mm
            t_ac = p.op("v", "scalar_tensor_tensor", acc[co][:, tc * 512:(tc + 1) * 512],
                        pb[bank][:, 0:512], sdh[:, co:co + 1],
                        acc[co][:, tc * 512:(tc + 1) * 512],
                        OP.mult, OP.add,
                        waits=[(sm, t_mm), (sv, tok[("dsc", co)]), p.glob["attn_done"]],
                        inc=sv)
            tok[("dacc", tc, co)] = t_ac
    p.glob["ffn0_done"] = (sv, sv.n)


def phase_l1pre(p, io, dbg):
    nc, T = p.nc, p.T
    FT = 8
    NTC = T // 512
    acc, pb, pbT = p.glob["acc"], p.glob["pb"], p.glob["pbT"]
    sv, ss, sm, sg = p.sv, p.ss, p.sm, p.sg
    identb, ident = p.glob["identb"], p.glob["ident"]
    onesb = p.glob["onesb"]
    t_ones = p.glob["t_ones"]
    hmv = p.glob["hmv"]
    tok = {}

    xc_sp = nc.dram_tensor("xc_int", [16, 128, T], BF16, kind="Internal")
    xca = xc_sp.ap()
    if dbg:
        xc_dbg = nc.dram_tensor("xc_sp", [16, 128, T], BF16, kind="ExternalOutput")
    xc_ag = nc.dram_tensor("xc_ag", [2, 16, 128, T], BF16, kind="Internal")
    res_sp = nc.dram_tensor("res_sp", [16, 128, T], BF16,
                            kind="ExternalOutput" if dbg else "Internal")
    resa = res_sp.ap()
    p.glob["res_sp"] = res_sp
    x13_snd_h = nc.dram_tensor("x13_snd", [3, D], F32, kind="Internal")
    x13_snd = x13_snd_h.ap()
    x13_rcv = nc.dram_tensor("x13_rcv", [2, 3, D], F32, kind="Internal").ap()
    p.glob["xc_ag"] = xc_ag
    p.glob["xc_sp"] = xc_sp
    if dbg:
        dbg_xn1 = nc.dram_tensor("dbg_xn1", [FT, 128, T], BF16, kind="ExternalOutput").ap()
        dbg_xm = nc.dram_tensor("dbg_xm", [FT, 128, T], BF16, kind="ExternalOutput").ap()
    ffn0_done = p.glob["ffn0_done"]

    hst = p.stream("l1h", 1)
    h3s = p.stream("l1h3", 1)
    dbx = p.stream("l1dx", 1)
    dbm = p.stream("l1dm", 1)
    for ft in range(FT):
        p.op("g", "dma_start", bass.AP(x13_snd_h, ft * 128, [[1, 128], [D, 3]]),
             bass.AP(acc[ft], T - 3, [[T, 128], [1, 3]]),
             waits=[ffn0_done] if ft == 0 else (), inc=h3s.sems[0], inck=16,
             allow_slow_non_contiguous=True)
    t_halo = (h3s.sems[0], h3s.sems[0].n)
    t_cc3 = p.op("g", "collective_compute", "AllGather", OP.bypass, replica_groups=RG_PAIRS,
                 ins=[x13_snd], outs=[x13_rcv], waits=[t_halo], inc=sg)
    xh = p.sb("xh", [3, D], F32)
    xhs = p.stream("l1xh", 1)
    t_lh3 = xhs.load(xh[:], x13_rcv[0], eng="g", waits=[(sg, t_cc3)], adv=False)

    xm = [p.sb(f"xm_{i}", [128, T], BF16) for i in range(FT)]
    wwin = p.sb("l1win", [128, FT * NH], BF16)
    wwout = p.sb("l1wo", [NH, D], BF16)
    binv = p.sb("l1bin", [NH, 1], F32)
    b2v = p.sb("l1b2", [NH, 1], F32)
    boutv = p.sb("l1bo", [128, 8], F32)
    nwv = p.sb("l1nw", [128, 8], F32)
    w1f = p.sb("l1w1f", [NH, 16], F32)
    w2f = p.sb("l1w2f", [NH, 16], F32)
    b1f = p.sb("l1b1f", [NH, 16], F32)
    W1b = p.sb("l1w1", [NH, 16], BF16)
    W2b = p.sb("l1w2", [NH, 16], BF16)
    b1b = p.sb("l1b1", [NH, 16], BF16)
    cwv = p.sb("l1cw", [128, 16 * 4], F32)
    cbv = p.sb("l1cb", [128, 16], F32)
    vst = p.stream("l1v", 1)
    for ci in range(FT):
        vst.load(wwin[:, ci * NH:(ci + 1) * NH],
                 io["w_win"][ci * 128:(ci + 1) * 128, :], adv=False)
    vst.load(wwout[:], io["w_wout"], adv=False)
    vst.load(binv[:], io["bin_"], adv=False)
    vst.load(b2v[:], io["b2"], adv=False)
    vst.load(boutv[:], io["bout"], adv=False)
    vst.load(nwv[:], io["norm_w"], adv=False)
    vst.load(w1f[:], io["W1"], adv=False)
    vst.load(w2f[:], io["W2"], adv=False)
    vst.load(b1f[:], io["b1"], adv=False)
    for ct in range(16):
        vst.load(cwv[:, ct * 4:(ct + 1) * 4], io["conv_w"][ct], adv=False)
    t_vl = vst.load(cbv[:], io["conv_b"], adv=False)
    t_c1 = p.op("v", "tensor_copy", W1b[:], w1f[:], waits=[t_vl], inc=sv)
    t_c2 = p.op("v", "tensor_copy", W2b[:], w2f[:], inc=sv)
    t_cv = p.op("v", "tensor_copy", b1b[:], b1f[:], inc=sv)

    xn1 = [p.sb(f"xn1_{i}", [128, 512], BF16) for i in range(FT)]
    xbb = p.sb("l1xb", [128, 512], BF16)
    sqb = p.sb("l1sq", [128, 512], BF16)
    meanb = p.sb("l1mn", [128, 512], F32)
    rstdb = p.sb("l1rs", [128, 512], F32)
    vtmp = p.sb("l1vt", [128, 512], F32)
    rrb = p.sb("l1rr", [128, 512], F32)
    sffb = p.sb("l1sf", [NH, 512], BF16)
    hfb = p.sb("l1hf", [NH, 512 * 16], BF16)
    g1b = p.sb("l1g1", [NH, 512], BF16)
    g1f = p.sb("l1g1f", [NH, 512], F32)
    for tc in range(NTC):
        c0 = tc * 512
        gprev = (sv, tok.get(("xmd", tc - 1), 0))
        t_mm1 = 0
        t_mm2 = 0
        t_sq = 0
        for ft in range(FT):
            t_xb = p.op("v", "tensor_copy", xbb[:], acc[ft][:, c0:c0 + 512],
                        waits=[ffn0_done, (sm, t_mm1), (ss, t_sq), gprev], inc=sv)
            t_sq = p.op("s", "activation", sqb[:], xbb[:], AF.Square,
                        waits=[(sv, t_xb), (sm, t_mm2)], inc=ss)
            t_mm1 = p.op("t", "matmul", pb[4][:, 0:512], onesb[:], xbb[:],
                         start=(ft == 0), stop=(ft == FT - 1),
                         waits=[(sv, t_xb), (sv, t_ones), (ss, tok.get(("st1", tc - 1), 0))],
                         inc=sm)
            t_mm2 = p.op("t", "matmul", pb[5][:, 0:512], onesb[:], sqb[:],
                         start=(ft == 0), stop=(ft == FT - 1),
                         waits=[(ss, t_sq), (sv, tok.get(("st2", tc - 1), 0))], inc=sm)
        t_s1 = p.op("s", "activation", meanb[:], pb[4][:, 0:512], AF.Copy,
                    scale=1.0 / D, waits=[(sm, t_mm1), (sv, tok.get(("mnu", tc - 1), 0))],
                    inc=ss)
        tok[("st1", tc)] = t_s1
        t_s2 = p.op("s", "activation", vtmp[:], meanb[:], AF.Square,
                    waits=[(ss, t_s1)], inc=ss)
        t_s3 = p.op("v", "scalar_tensor_tensor", rstdb[:], pb[5][:, 0:512], 1.0 / D,
                    vtmp[:], OP.mult, OP.subtract, waits=[(ss, t_s2), (sm, t_mm2)], inc=sv)
        tok[("st2", tc)] = t_s3
        t_s4 = p.op("s", "activation", rstdb[:], rstdb[:], AF.Sqrt, bias=p.const(1e-5),
                    waits=[(sv, t_s3)], inc=ss)
        t_s5 = p.op("v", "reciprocal", rstdb[:], rstdb[:], waits=[(ss, t_s4)], inc=sv)
        t_x1 = 0
        for ft in range(FT):
            t_d = p.op("v", "tensor_tensor", vtmp[:], acc[ft][:, c0:c0 + 512], meanb[:],
                       OP.subtract, waits=[(sv, t_s5), (sv, t_x1), (ss, 0 if ft else t_s2)],
                       inc=sv)
            t_x1 = p.op("v", "tensor_tensor", xn1[ft][:], vtmp[:], rstdb[:], OP.mult,
                        waits=[(sv, t_d)] + ([tok[("dbx", tc - 1)]] if (dbg and tc and ft == 0)
                                             else []), inc=sv)
            tok[("xn1t", tc, ft)] = t_x1
        tok[("mnu", tc)] = t_x1
        if dbg:
            for ft in range(FT):
                tok[("dbx", tc)] = dbx.load(dbg_xn1[ft][:, c0:c0 + 512], xn1[ft][:], eng="g",
                                            waits=[(sv, tok[("xn1t", tc, ft)])], adv=False)
        t_mm3 = 0
        for ft in range(FT):
            t_sq2 = p.op("s", "activation", sqb[:], xn1[ft][:], AF.Square,
                         waits=[(sv, tok[("xn1t", tc, ft)]), (sm, t_mm3)], inc=ss)
            t_mm3 = p.op("t", "matmul", pb[6][:, 0:512], onesb[:], sqb[:],
                         start=(ft == 0), stop=(ft == FT - 1),
                         waits=[(ss, t_sq2), (sv, tok.get(("rru", tc - 1), 0))], inc=sm)
        t_r1 = p.op("s", "activation", rrb[:], pb[6][:, 0:512], AF.Sqrt,
                    bias=p.const(1e-6), scale=1.0 / D, waits=[(sm, t_mm3)], inc=ss)
        t_r2 = p.op("v", "reciprocal", rrb[:], rrb[:], waits=[(ss, t_r1)], inc=sv)
        t_xm = 0
        for ft in range(FT):
            t_m1 = p.op("v", "tensor_tensor", vtmp[:], xn1[ft][:], rrb[:], OP.mult,
                        waits=[(sv, t_r2), (sv, t_xm)], inc=sv)
            t_xm = p.op("v", "tensor_scalar", xm[ft][:, c0:c0 + 512], vtmp[:],
                        nwv[:, ft:ft + 1], None, OP.mult, waits=[(sv, t_m1), t_vl], inc=sv)
            tok[("xmt", tc, ft)] = t_xm
        tok[("rru", tc)] = t_xm
        tok[("xmd", tc)] = t_xm
        if dbg:
            for ft in range(FT):
                tok[("dbm", tc)] = dbm.load(dbg_xm[ft][:, c0:c0 + 512], xm[ft][:, c0:c0 + 512],
                                            eng="g", waits=[(sv, tok[("xmt", tc, ft)])], adv=False)
        t_sm1 = 0
        for ci in range(FT):
            t_sm1 = p.op("t", "matmul", pb[0][0:NH, 0:512],
                         wwin[:, ci * NH:(ci + 1) * NH], xn1[ci][:],
                         start=(ci == 0), stop=(ci == FT - 1),
                         waits=[(sv, tok[("xn1t", tc, ci)]), t_vl,
                                (sv, tok.get(("sevac", tc - 1), 0))], inc=sm)
        t_se = p.op("v", "tensor_scalar", sffb[:], pb[0][0:NH, 0:512], binv[:, 0:1], None,
                    OP.add, waits=[(sm, t_sm1), t_vl, (sv, tok.get(("hfu", tc - 1), 0))],
                    inc=sv)
        tok[("sevac", tc)] = t_se
        t_h1 = p.op("v", "tensor_tensor", hfb[:],
                    bass.AP(sffb, 0, [[512, NH], [1, 512], [0, 16]]),
                    bass.AP(W1b, 0, [[16, NH], [0, 512], [1, 16]]), OP.mult,
                    waits=[(sv, t_se), (sv, t_c1), (sv, tok.get(("g1u", tc - 1), 0))], inc=sv)
        t_h2 = p.op("v", "tensor_tensor", hfb[:], hfb[:],
                    bass.AP(b1b, 0, [[16, NH], [0, 512], [1, 16]]), OP.add,
                    waits=[(sv, t_h1)], inc=sv)
        t_h3 = p.op("v", "tensor_relu", hfb[:], hfb[:], waits=[(sv, t_h2)], inc=sv)
        t_g1 = p.op("v", "tensor_tensor", hfb[:], hfb[:],
                    bass.AP(W2b, 0, [[16, NH], [0, 512], [1, 16]]), OP.mult,
                    waits=[(sv, t_h3)], inc=sv)
        tok[("hfu", tc)] = t_g1
        t_g2 = p.op("v", "reduce_sum", g1f[:], bass.AP(hfb, 0, [[8192, NH], [16, 512], [1, 16]]),
                    axis=AX.X, waits=[(sv, t_g1), (sv, tok.get(("g1fu", tc - 1), 0))], inc=sv)
        t_g3 = p.op("v", "tensor_scalar", g1b[:], g1f[:], b2v[:, 0:1], None, OP.add,
                    waits=[(sv, t_g2), t_vl], inc=sv)
        tok[("g1fu", tc)] = t_g3
        tok[("g1u", tc)] = t_g3
        for ft in range(FT):
            t_sk = p.op("v", "tensor_tensor", acc[ft][:, c0:c0 + 512],
                        acc[ft][:, c0:c0 + 512], xn1[ft][:], OP.add,
                        waits=[(sv, tok[("xmt", tc, ft)]), (sm, t_mm3), (sm, t_sm1)]
                        + ([t_halo] if tc == NTC - 1 else []), inc=sv)
            tok[("skip", tc, ft)] = t_sk
        for co in range(FT):
            t_om = p.op("t", "matmul", pb[1][:, 0:512], wwout[0:NH, co * 128:(co + 1) * 128],
                        g1b[:], start=True, stop=True,
                        waits=[(sv, t_g3), t_vl, (sv, tok.get(("oacc", tc, co - 1),
                                                              tok.get(("oacc", tc - 1, FT - 1), 0)))],
                        inc=sm)
            t_oa = p.op("v", "scalar_tensor_tensor", acc[co][:, c0:c0 + 512],
                        pb[1][:, 0:512], boutv[:, co:co + 1], acc[co][:, c0:c0 + 512],
                        OP.add, OP.add, waits=[(sm, t_om), (sv, tok[("skip", tc, co)]), t_vl],
                        inc=sv)
            tok[("oacc", tc, co)] = t_oa
    p.glob["l1acc_done"] = (sv, sv.n)

    st3 = p.sb("l1s3", [3, 8], F32)
    sq3 = p.sb("l1q3", [3, D], F32)
    xmh = p.sb("l1xmh", [128, FT * 4], BF16)
    a_ = p.op("v", "reduce_sum", st3[:, 0:1], xh[:], axis=AX.X, waits=[t_lh3], inc=sv)
    b_ = p.op("s", "activation", sq3[:], xh[:], AF.Square, waits=[t_lh3], inc=ss)
    c_ = p.op("v", "reduce_sum", st3[:, 1:2], sq3[:], axis=AX.X, waits=[(ss, b_)], inc=sv)
    d_ = p.op("v", "tensor_scalar", st3[:, 2:3], st3[:, 0:1], 1.0 / D, None, OP.mult,
              waits=[(sv, c_)], inc=sv)
    e_ = p.op("s", "activation", st3[:, 3:4], st3[:, 2:3], AF.Square, waits=[(sv, d_)], inc=ss)
    f_ = p.op("v", "scalar_tensor_tensor", st3[:, 4:5], st3[:, 1:2], 1.0 / D, st3[:, 3:4],
              OP.mult, OP.subtract, waits=[(ss, e_)], inc=sv)
    g_ = p.op("s", "activation", st3[:, 5:6], st3[:, 4:5], AF.Sqrt, bias=p.const(1e-5, rows=3),
              waits=[(sv, f_)], inc=ss)
    h_ = p.op("v", "reciprocal", st3[:, 6:7], st3[:, 5:6], waits=[(ss, g_)], inc=sv)
    i_ = p.op("v", "tensor_scalar", xh[:], xh[:], st3[:, 2:3], st3[:, 6:7],
              OP.subtract, OP.mult, waits=[(sv, h_)], inc=sv)
    j_ = p.op("s", "activation", sq3[:], xh[:], AF.Square, waits=[(sv, i_)], inc=ss)
    k_ = p.op("v", "reduce_sum", st3[:, 1:2], sq3[:], axis=AX.X, waits=[(ss, j_)], inc=sv)
    l_ = p.op("s", "activation", st3[:, 5:6], st3[:, 1:2], AF.Sqrt, bias=p.const(1e-6, rows=3),
              scale=1.0 / D, waits=[(sv, k_)], inc=ss)
    m_ = p.op("v", "reciprocal", st3[:, 6:7], st3[:, 5:6], waits=[(ss, l_)], inc=sv)
    n_ = p.op("v", "tensor_scalar", xh[:], xh[:], st3[:, 6:7], None, OP.mult,
              waits=[(sv, m_)], inc=sv)
    t_pe = 0
    for ft in range(FT):
        t_tr = p.op("t", "transpose", pb[2][0:128, ft * 4:ft * 4 + 3],
                    xh[:, ft * 128:(ft + 1) * 128], ident[0:3, 0:3],
                    waits=[(sv, n_), (ss, t_pe) if t_pe else (sv, n_)], inc=sm)
        t_pe = p.op("s", "copy", xmh[:, ft * 4:ft * 4 + 3], pb[2][0:128, ft * 4:ft * 4 + 3],
                    waits=[(sm, t_tr)], inc=ss)
    t_xm1 = 0
    for ft in range(FT):
        t_xm1 = p.op("v", "tensor_scalar", xmh[:, ft * 4:ft * 4 + 3], xmh[:, ft * 4:ft * 4 + 3],
                     nwv[:, ft:ft + 1], None, OP.mult, waits=[(ss, t_pe), t_vl], inc=sv)
    t_xmh = 0
    for ft in range(FT):
        t_xmh = p.op("v", "tensor_scalar", xmh[:, ft * 4:ft * 4 + 3], xmh[:, ft * 4:ft * 4 + 3],
                     hmv[:], None, OP.mult, waits=[(sv, t_xm1)], inc=sv)

    wist = p.stream("l1wi", 2)
    xist = p.stream("l1xs", 2)
    wtile = [p.sb(f"l1wt{j}", [128, FT * 128], BF16) for j in range(2)]
    xib = [p.sb(f"l1xi{j}", [128, 3 + T], BF16) for j in range(2)]
    rstage = p.sb("l1rst", [128, 512], BF16)
    ctmp = p.sb("l1ct", [128, T], F32)
    xcst = p.sb("l1xc", [128, T], BF16)
    for ro in range(32):
        wb = wtile[ro % 2]
        t_lw = 0
        for ci in range(FT):
            t_lw = wist.load(wb[:, ci * 128:(ci + 1) * 128],
                             io["w_inproj"][ci * 128:(ci + 1) * 128, ro * 128:(ro + 1) * 128],
                             waits=[(sm, tok.get(("ipm", ro - 2), 0))] if ci == 0 else (),
                             adv=(ci == FT - 1))
        is_xi = ro < 16
        if is_xi:
            t_hm = 0
            for ci in range(FT):
                t_hm = p.op("t", "matmul", pb[2][:, 0:3], wb[:, ci * 128:(ci + 1) * 128],
                            xmh[:, ci * 4:ci * 4 + 3], start=(ci == 0), stop=(ci == FT - 1),
                            waits=[(sv, t_xmh), t_lw, (sv, tok.get(("xiev", ro - 2), 0)),
                                   (ss, t_pe)], inc=sm)
        t_evp = None
        for tc in range(NTC):
            t_mm = 0
            for ci in range(FT):
                w_mm = [(sv, tok[("xmt", tc, ci)]), t_lw]
                if ci == 0 and t_evp is not None:
                    w_mm.append(t_evp)
                if is_xi:
                    w_mm.append((sv, tok.get(("xiev", ro - 2), 0)))
                elif tok.get(("rsev", ro - 2)):
                    w_mm.append((ss, tok[("rsev", ro - 2)]))
                t_mm = p.op("t", "matmul", pb[0 if ro % 2 == 0 else 1][:, 0:512],
                            wb[:, ci * 128:(ci + 1) * 128],
                            xm[ci][:, tc * 512:(tc + 1) * 512],
                            start=(ci == 0), stop=(ci == FT - 1), waits=w_mm, inc=sm)
            if is_xi:
                t_ev = p.op("v", "tensor_copy", xib[ro % 2][:, 3 + tc * 512:3 + (tc + 1) * 512],
                            pb[0 if ro % 2 == 0 else 1][:, 0:512], waits=[(sm, t_mm)], inc=sv)
                t_evp = (sv, t_ev)
            else:
                t_ev = p.op("s", "activation", rstage[:], pb[0 if ro % 2 == 0 else 1][:, 0:512],
                            AF.Silu,
                            waits=[(sm, t_mm)] + ([tok[("rspl", ro, tc - 1)]] if tc else
                                                  ([tok[("rspl", ro - 1, NTC - 1)]] if ro > 16 else [])),
                            inc=ss)
                tok[("rsev", ro)] = t_ev
                t_evp = (ss, t_ev)
                tok[("rspl", ro, tc)] = xist.load(resa[ro - 16][:, tc * 512:(tc + 1) * 512],
                                                  rstage[:], waits=[(ss, t_ev)])
            tok[("ipm", ro)] = t_mm
        if is_xi:
            t_he = p.op("v", "tensor_copy", xib[ro % 2][:, 0:3], pb[2][:, 0:3],
                        waits=[(sm, t_hm)], inc=sv)
            ct = ro
            t_cv1 = p.op("v", "tensor_scalar", ctmp[:], xib[ro % 2][:, 0:T],
                         cwv[:, ct * 4:ct * 4 + 1], None, OP.mult,
                         waits=[(sv, t_ev), (sv, t_he), t_vl,
                                (ss, tok.get(("xcs", ro - 1), 0))], inc=sv)
            for kk in range(1, 4):
                t_cv1 = p.op("v", "scalar_tensor_tensor", ctmp[:],
                             xib[ro % 2][:, kk:kk + T], cwv[:, ct * 4 + kk:ct * 4 + kk + 1],
                             ctmp[:], OP.mult, OP.add, waits=[(sv, t_cv1)], inc=sv)
            t_si = p.op("s", "activation", xcst[:], ctmp[:], AF.Silu,
                        bias=cbv[:, ct:ct + 1],
                        waits=[(sv, t_cv1)] + ([tok[("xcspl", ct - 1)]] if ct else []),
                        inc=ss)
            tok[("xcs", ro)] = t_si
            tok[("xiev", ro)] = t_cv1
            tok[("xcspl", ct)] = xist.load(xca[ct], xcst[:], waits=[(ss, t_si)])
    spdone = xist.done()
    if dbg:
        p.op("y", "dma_start", xc_dbg.ap(), xc_sp.ap(), waits=spdone, inc=hst.sems[0], inck=16)
    t_cc4 = p.op("g", "collective_compute", "AllGather", OP.bypass, replica_groups=RG_PAIRS,
                 ins=[xc_sp.ap()], outs=[xc_ag.ap()], waits=spdone, inc=sg)
    p.glob["xc_ag_tok"] = (sg, t_cc4)
    p.glob["res_done"] = spdone


def phase_acc_spill(p, io):
    T = p.T
    nc = p.nc
    acc = p.glob["acc"]
    acc_sp = nc.dram_tensor("acc_sp", [8, 128, T], F32, kind="Internal")
    p.glob["acc_sp"] = acc_sp
    asp = p.stream("asp", 1)
    ad = p.glob["acc_done"]
    toks = []
    for ft in range(8):
        toks.append(asp.load(acc_sp.ap()[ft], acc[ft][:], waits=[ad] if ft == 0 else (),
                             adv=False))
    p.glob["acc_sp_tok"] = toks[-1]


def phase_mamba(p, io, dbg):
    T = p.T
    T2 = 2 * T
    NT2 = T2 // 512
    nc = p.nc
    sv, ss, sm, sg = p.sv, p.ss, p.sm, p.sg
    big = p.glob["big"]
    ofs = p.glob["ofs"]
    rg = p.glob["rg"]
    tok = {}

    dt_sp = nc.dram_tensor("dt_sp", [8, 128, T2], BF16, kind="Internal")
    y_sp = nc.dram_tensor("y_sp", [8, 128, T2], BF16, kind="Internal")
    p.glob["y_sp"] = y_sp
    xc_ag = p.glob["xc_ag"]
    t_xcag = p.glob["xc_ag_tok"]

    vst = p.stream("mvs", 1)
    ind16 = p.sb("mind", [128, 16], BF16)
    dtbv = p.sb("mdtb", [128, 8], F32)
    A_sb = p.sb("mA", [128, 512], F32)
    vst.load(ind16[:], io["ind16"], adv=False)
    vst.load(dtbv[:], io["dt_bh"], adv=False)
    vst.load(ofs[:], io["offs"], adv=False)
    t_va = vst.load(A_sb[:], io["A_log_h"], adv=False)
    p.glob["t_ofs"] = t_va
    t_ae = p.op("s", "activation", A_sb[:], A_sb[:], AF.Exp, waits=[t_va], inc=ss)
    t_an = p.op("v", "tensor_scalar", A_sb[:], A_sb[:], -1.0, None, OP.mult,
                waits=[(ss, t_ae)], inc=sv)

    B_sb = p.sb("mB", [NS, T2], BF16)
    with p.scope():
        wdt = p.sb("mwdt", [128, 16 * 1024], BF16)
        wxp = p.sb("mwxp", [128, 16 * NS], BF16)
        wst = p.stream("mws", 1)
        for ci in range(16):
            wst.load(wdt[:, ci * 1024:(ci + 1) * 1024],
                     io["w_dt_h"][ci * 128:(ci + 1) * 128, :], adv=False)
        for ci in range(16):
            t_w = wst.load(wxp[:, ci * NS:(ci + 1) * NS],
                           io["w_xproj"][ci * 128:(ci + 1) * 128, :], adv=False)
        xk = [p.sb(f"mxk{j}", [128, 16 * 512], BF16) for j in range(2)]
        dstg = [p.sb(f"mds{j}", [128, 512], F32) for j in range(4)]
        dbf = [p.sb(f"mdb{j}", [128, 512], BF16) for j in range(4)]
        sps = [p.sb(f"msp{j}", [128, 512], F32) for j in range(4)]
        bstg = [p.sb(f"mbs{j}", [NS, 512], F32) for j in range(2)]
        c1 = p.const(1.0)
        xkst = p.stream("mxk", 2)
        dsp = p.stream("mdsp", 2)
        for tch in range(NT2):
            slot, lc = tch // (NT2 // 2), (tch % (NT2 // 2)) * 512
            buf = tch % 2
            wx = [t_xcag] if tch == 0 else []
            if tch >= 2:
                wx.append((sm, tok[("bmm", tch - 2)]))
            for ci in range(16):
                t_xk = xkst.load(xk[buf][:, ci * 512:(ci + 1) * 512],
                                 bass.AP(xc_ag, slot * (16 * 128 * T) + ci * 128 * T + lc,
                                         [[T, 128], [1, 512]]),
                                 waits=wx if ci == 0 else (), adv=(ci == 15))
            t_mm = 0
            for co in range(8):
                for ki in range(16):
                    t_mm = p.op("t", "matmul", big[:, co * 512:(co + 1) * 512],
                                wdt[:, ki * 1024 + co * 128:ki * 1024 + co * 128 + 128],
                                xk[buf][:, ki * 512:(ki + 1) * 512],
                                start=(ki == 0), stop=(ki == 15),
                                waits=[t_xk, t_w]
                                + (([(ss, tok.get(("dtev", tch - 1, co), 0))]
                                    + ([(sv, tok.get(("bev", tch - 1), 0))] if co == 0 else []))
                                   if ki == 0 else []), inc=sm)
                tok[("comm", tch, co)] = t_mm
            tok[("dtmm", tch)] = t_mm
            for cg in range(2):
                cos = range(cg * 4, cg * 4 + 4)
                te1, te3 = {}, {}
                for co in cos:
                    te1[co] = p.op("s", "activation", sps[co % 4][:],
                                   big[:, co * 512:(co + 1) * 512],
                                   AF.Abs, bias=dtbv[:, co:co + 1],
                                   waits=[(sm, tok[("comm", tch, co)]), t_va,
                                          (sv, tok.get(("dtadd", tch, co - 4),
                                                       tok.get(("dtadd", tch - 1, 4 + co), 0)))],
                                   inc=ss)
                for co in cos:
                    p.op("s", "activation", sps[co % 4][:], sps[co % 4][:], AF.Exp,
                         scale=-1.0, waits=[(ss, te1[co])], inc=ss)
                for co in cos:
                    te3[co] = p.op("s", "activation", sps[co % 4][:], sps[co % 4][:], AF.Ln,
                                   bias=c1, waits=[p._cinit], inc=ss)
                for co in cos:
                    wsp = [tok.get(("dtsp", tch, co - 4), tok.get(("dtsp", tch - 1, 4 + co),
                                                                  (sv, 0)))]
                    t_ev = p.op("s", "activation", dstg[co % 4][:],
                                big[:, co * 512:(co + 1) * 512],
                                AF.Relu, bias=dtbv[:, co:co + 1],
                                waits=[(ss, te3[co])] + wsp, inc=ss)
                    tok[("dtev", tch, co)] = t_ev
                for co in cos:
                    t_add = p.op("v", "tensor_tensor", dbf[co % 4][:], dstg[co % 4][:],
                                 sps[co % 4][:], OP.add,
                                 waits=[(ss, tok[("dtev", tch, co)])], inc=sv)
                    tok[("dtadd", tch, co)] = t_add
                    tok[("dtsp", tch, co)] = dsp.load(
                        bass.AP(dt_sp, co * 128 * T2 + tch * 512, [[T2, 128], [1, 512]]),
                        dbf[co % 4][:], waits=[(sv, t_add)])
            t_bm = 0
            for ki in range(16):
                t_bm = p.op("t", "matmul", big[0:NS, 0:512],
                            wxp[:, ki * NS:(ki + 1) * NS], xk[buf][:, ki * 512:(ki + 1) * 512],
                            start=(ki == 0), stop=(ki == 15),
                            waits=[(ss, tok[("dtev", tch, 0)])] if ki == 0 else [], inc=sm)
            t_be = p.op("v", "tensor_copy", bstg[buf][:], big[0:NS, 0:512],
                        waits=[(sm, t_bm), (sv, tok.get(("bcp", tch - 2), 0))], inc=sv)
            tok[("bev", tch)] = t_be
            t_bc = p.op("v", "tensor_copy", B_sb[:, tch * 512:(tch + 1) * 512], bstg[buf][:],
                        waits=[(sv, t_be)], inc=sv)
            tok[("bcp", tch)] = t_bc
            tok[("bmm", tch)] = t_bm
        t_ball = tok[("bcp", NT2 - 1)]
        dt_done = dsp.done()
        p.glob["m_dt_done"] = dt_done
        if dbg:
            dt_dbg = nc.dram_tensor("dt_dbg", [8, 128, T2], BF16, kind="ExternalOutput")
            p.op("y", "dma_start", dt_dbg.ap(), dt_sp.ap(), waits=dt_done, inc=p.ld, inck=16)
            B_dbg = nc.dram_tensor("B_dbg", [NS, T2], BF16, kind="ExternalOutput")
            p.op("y", "dma_start", B_dbg.ap(), B_sb[:], waits=[(sv, t_ball)],
                 inc=p.ld, inck=16)

    Brep = [p.sb(f"mBr{k}", [128, T2], BF16) for k in range(8)]
    brs = p.stream("mbrs", 1)
    for k in range(8):
        for c in range(16):
            t_br = brs.load(bass.AP(Brep[k], c * 8 * T2, [[T2, 8], [1, T2]]),
                            bass.AP(B_sb, 8 * k * T2, [[T2, 8], [1, T2]]),
                            waits=[(sv, t_ball)] if (k == 0 and c == 0) else (), adv=False)
    dt_done = p.glob["m_dt_done"]

    xcsm = p.sb("mxcs", [128, T2], BF16)
    xcrep = [p.sb(f"mxc{j}", [128, T2], BF16) for j in range(2)]
    dtrep = [p.sb(f"mdt{j}", [128, T2], BF16) for j in range(2)]
    af = [p.sb(f"maf{j}", [128, T2], F32) for j in range(2)]
    bxb = [p.sb(f"mbx{j}", [128, T2], BF16) for j in range(2)]
    ystg = [p.sb(f"mys{j}", [16, T2], BF16) for j in range(2)]
    mrd = p.stream("mrd", 2)
    mrd2 = p.stream("mrd2", 2)
    mrx = p.stream("mrx", 2)
    mrx2 = p.stream("mrx2", 2)
    mrg = p.stream("mrg", 1)
    ysp = p.stream("mysp", 2)
    zc = p.const(0.0)
    EV = T2 // 512

    def emit_yevac(cbe):
        bufe = cbe % 2
        cte, rbe = cbe // 8, cbe % 8
        for hh in range(2):
            t_ye = p.op("v", "tensor_copy", ystg[bufe][:, hh * (T2 // 2):(hh + 1) * (T2 // 2)],
                        big[0:16, hh * (T2 // 2):(hh + 1) * (T2 // 2)],
                        waits=[(sm, tok[("sum", cbe, 7, EV - 1)])]
                        + ([tok[("yspl", cbe - 2)]] if (hh == 0 and cbe >= 2) else []), inc=sv)
            tok[("yev", cbe, hh)] = t_ye
        tok[("yspl", cbe)] = ysp.load(
            bass.AP(y_sp, cte * 128 * T2 + 16 * rbe * T2, [[T2, 16], [1, T2]]),
            ystg[bufe][:], waits=[(sv, tok[("yev", cbe, 1)])])

    for cb in range(64):
        buf = cb % 2
        ct, rb = cb // 8, cb % 8
        if rb == 0:
            gfirst = ([t_va, t_xcag, t_br] if cb == 0 else [])
            gfree = [tok[("mlx", cb - 1)], tok[("mlx2", cb - 1)]] if cb else []
            p.op("g", "reg_load", rg, ofs[0:1, ct:ct + 1], waits=gfree + gfirst)
            tok[("xcs", ct)] = mrg.load(
                xcsm[:], bass.AP(xc_ag, rg, [[T, 128], [16 * 128 * T, 2], [1, T]]),
                eng="g", adv=False)
        yfirst = (dt_done + [(sv, t_an), tok[("xcs", 0)]]) if cb == 0 else []
        yfree = ([(ss, tok[("aexp", cb - 2, 7)]),
                  (sv, tok[("bx", cb - 2, 7)])] if cb >= 2 else [])
        dfree = [(ss, tok[("aexp", cb - 2, 7)])] if cb >= 2 else []
        xfree = [(sv, tok[("bx", cb - 2, 7)])] if cb >= 2 else []
        wxc = [tok[("xcs", ct)]] if rb == 0 else []
        t_d1 = mrd.load(dtrep[buf][0:64, :],
                        bass.AP(dt_sp, ct * 128 * T2 + 16 * rb * T2,
                                [[T2, 8], [0, 8], [1, T2]]),
                        waits=dfree + (yfirst if cb == 0 else []))
        t_d2 = mrd2.load(bass.AP(dtrep[buf], 64 * T2, [[T2, 64], [1, T2]]),
                         bass.AP(dt_sp, ct * 128 * T2 + (16 * rb + 8) * T2,
                                 [[T2, 8], [0, 8], [1, T2]]),
                         eng="s", waits=dfree)
        t_x1 = mrx.load(xcrep[buf][0:64, :],
                        bass.AP(xcsm, 16 * rb * T2, [[T2, 8], [0, 8], [1, T2]]),
                        waits=xfree + wxc)
        t_x2 = mrx2.load(bass.AP(xcrep[buf], 64 * T2, [[T2, 64], [1, T2]]),
                         bass.AP(xcsm, (16 * rb + 8) * T2, [[T2, 8], [0, 8], [1, T2]]),
                         eng="s", waits=xfree + wxc)
        tok[("mld", cb)] = t_d1
        tok[("mld2", cb)] = t_d2
        tok[("mlx", cb)] = t_x1
        tok[("mlx2", cb)] = t_x2
        for k in range(8):
            kb = k % 2
            t_a = p.op("s", "activation", af[kb][:], dtrep[buf][:], AF.Exp,
                       scale=A_sb[:, cb * 8 + k:cb * 8 + k + 1],
                       waits=[tok[("mld", cb)], tok[("mld2", cb)],
                              (sv, tok.get(("scan", cb, k - 2),
                                           tok.get(("scan", cb - 1, 6 + k), 0)))],
                       inc=ss)
            tok[("aexp", cb, k)] = t_a
            bfree = tok.get(("sum", cb, k - 2, EV - 1),
                            tok.get(("sum", cb - 1, 6 + k, EV - 1), 0))
            t_bx = p.op("v", "tensor_tensor", bxb[kb][:], Brep[k][:], xcrep[buf][:], OP.mult,
                        waits=[tok[("mlx", cb)], tok[("mlx2", cb)], t_br, (sm, bfree)],
                        inc=sv)
            tok[("bx", cb, k)] = t_bx
            t_sc = p.op("v", "tensor_tensor_scan", bxb[kb][:], af[kb][:], bxb[kb][:],
                        zc, OP.mult, OP.add,
                        waits=[(ss, t_a), (sv, t_bx)], inc=sv)
            tok[("scan", cb, k)] = t_sc
            if k == 0 and cb > 0:
                emit_yevac(cb - 1)
            for tc in range(EV):
                t_sum = p.op("t", "matmul", big[0:16, tc * 512:(tc + 1) * 512], ind16[:],
                             bxb[kb][:, tc * 512:(tc + 1) * 512],
                             start=(k == 0), stop=(k == 7),
                             waits=[(sv, t_sc)]
                             + ([(sv, tok.get(("yev", cb - 1, 1), 0))]
                                if (k == 0 and tc == 0) else [])
                             + ([t_va] if (cb == 0 and k == 0 and tc == 0) else []), inc=sm)
                tok[("sum", cb, k, tc)] = t_sum
    emit_yevac(63)
    p.glob["m_y_done"] = ysp.done()
    if dbg:
        y_dbg = nc.dram_tensor("y_dbg", [8, 128, T2], BF16, kind="ExternalOutput")
        p.op("y", "dma_start", y_dbg.ap(), y_sp.ap(), waits=p.glob["m_y_done"],
             inc=p.ld, inck=16)


def phase_epilogue(p, io, dbg):
    T = p.T
    T2 = 2 * T
    nc = p.nc
    sv, ss, sm, sg = p.sv, p.ss, p.sm, p.sg
    big = p.glob["big"]
    ofs = p.glob["ofs"]
    rg = p.glob["rg"]
    xc_sp = p.glob["xc_sp"]
    y_sp = p.glob["y_sp"]
    res_sp = p.glob["res_sp"]
    accE = p.glob["acc"]
    tok = {}

    y_ag = nc.dram_tensor("y_ag", [2, 8, 128, T2], BF16, kind="Internal")
    t_cc = p.op("g", "collective_compute", "AllGather", OP.bypass, replica_groups=RG_PAIRS,
                ins=[y_sp.ap()], outs=[y_ag.ap()], waits=p.glob["m_y_done"], inc=sg)

    ast = p.stream("ast2", 1)
    acc_sp = p.glob["acc_sp"]
    for ft in range(8):
        t_ar = ast.load(accE[ft][:], acc_sp.ap()[ft],
                        waits=[p.glob["acc_sp_tok"]] if ft == 0 else (), adv=False)

    vst = p.stream("evs", 1)
    Dv = p.sb("eDv", [128, 16], F32)
    t_dv = vst.load(Dv[:], io["D_all"], adv=False)
    wo = p.sb("ewo", [128, 16 * 1024], BF16)
    wst = p.stream("ews", 1)
    for ci in range(16):
        t_wo = wst.load(wo[:, ci * 1024:(ci + 1) * 1024],
                        io["w_outproj"][ci * 128:(ci + 1) * 128, :], adv=False)

    y2 = p.sb("ey2", [128, 16 * T], BF16)
    xstg = [p.sb(f"exs{j}", [128, T], BF16) for j in range(2)]
    rstg = [p.sb(f"ers{j}", [128, T], BF16) for j in range(2)]
    lstg = p.stream("elg", 1)
    lsty = p.stream("els", 2)
    res_done = p.glob["res_done"]
    for s in range(2):
        p.op("g", "reg_load", rg, ofs[0:1, 8 + s:8 + s + 1],
             waits=[(sg, t_cc), p.glob["t_ofs"]] if s == 0 else [])
        t_ly = lstg.load(bass.AP(y2, s * 8 * T, [[16 * T, 128], [T, 8], [1, T]]),
                         bass.AP(y_ag, rg, [[T2, 128], [128 * T2, 8], [1, T]]),
                         eng="g", adv=False)
    for ct in range(16):
        buf = ct % 2
        wfree = [(sv, tok[("y2", ct - 2)])] if ct >= 2 else []
        t_lx = lsty.load(xstg[buf][:], xc_sp.ap()[ct],
                         waits=wfree + (res_done if ct == 0 else []), adv=False)
        t_l = lsty.load(rstg[buf][:], res_sp.ap()[ct])
        t_t = p.op("v", "scalar_tensor_tensor", y2[:, ct * T:(ct + 1) * T], xstg[buf][:],
                   Dv[:, ct:ct + 1], y2[:, ct * T:(ct + 1) * T], OP.mult, OP.add,
                   waits=[t_l, t_ly, t_dv, (sv, tok.get(("y2", ct - 2), 0))], inc=sv)
        t_y2 = p.op("v", "tensor_tensor", y2[:, ct * T:(ct + 1) * T],
                    y2[:, ct * T:(ct + 1) * T], rstg[buf][:], OP.mult,
                    waits=[(sv, t_t)], inc=sv)
        tok[("y2", ct)] = t_y2
    NTCH = T // 512
    for co in range(8):
        for tch in range(NTCH):
            t_mm = 0
            for ci in range(16):
                t_mm = p.op("t", "matmul", big[:, co * 512:(co + 1) * 512],
                            wo[:, ci * 1024 + co * 128:ci * 1024 + co * 128 + 128],
                            y2[:, ci * T + tch * 512:ci * T + (tch + 1) * 512],
                            start=(ci == 0), stop=(ci == 15),
                            waits=[(sv, tok[("y2", ci)]), t_wo]
                            + ([(sv, tok.get(("oa", co, tch - 1),
                                             tok.get(("oa", co - 1, NTCH - 1), 0)))]
                               if ci == 0 else []), inc=sm)
            t_oa = p.op("v", "tensor_tensor", accE[co][:, tch * 512:(tch + 1) * 512],
                        accE[co][:, tch * 512:(tch + 1) * 512], big[:, co * 512:(co + 1) * 512],
                        OP.add, waits=[(sm, t_mm), t_ar], inc=sv)
            tok[("oa", co, tch)] = t_oa
    p.glob["acc_done"] = (sv, sv.n)


class ColView:

    def __init__(self, t, base):
        self.t, self.base = t, base

    def __getitem__(self, idx):
        s0, s1 = idx
        return self.t[s0, self.base + s1.start:self.base + s1.stop]


def phase_out(p, io):
    T = p.T
    FT = D // 128
    TT = T // 128
    acc = p.glob["acc"]
    ident = p.glob["ident"]
    pbk = p.glob["pb"]
    otok = [p.sb(f"otok{j}", [128, D], F32) for j in range(2)]
    ost = p.stream("ost", 2)
    done = {}
    acc_done = p.glob["acc_done"]
    for tt in range(TT):
        buf = otok[tt % 2]
        evs = 0
        for grp in range(2):
            bank = pbk[grp]
            tms = 0
            for j in range(4):
                ft = grp * 4 + j
                w = [acc_done, (p.sv, done.get(("e", grp), 0))]
                if (tt - 2) in done:
                    w.append(done[tt - 2])
                tms = p.op("t", "transpose", bank[:, j * 128:(j + 1) * 128],
                           acc[ft][:, tt * 128:(tt + 1) * 128], ident[:], waits=w, inc=p.sm)
            for j in range(4):
                ft = grp * 4 + j
                evs = p.op("v", "tensor_copy", buf[:, ft * 128:(ft + 1) * 128],
                           bank[:, j * 128:(j + 1) * 128],
                           waits=[(p.sm, tms)], inc=p.sv)
            done[("e", grp)] = evs
        done[tt] = ost.load(io["x_out"][tt * 128:(tt + 1) * 128, :], buf[:],
                            waits=[(p.sv, evs)])
    p.wait("y", [done[TT - 1], done[TT - 2]])


def build(Tloc=TFULL // 2, dbg=False, ncores=8):
    set_ncores(ncores)
    nc = bass.Bass("TRN2", target_bir_lowering=False)
    io = declare_io(nc, Tloc, dbg)
    es = contextlib.ExitStack()
    with es:
        block = es.enter_context(nc.Block())
        p = Prog(nc, es, block, Tloc)
        for cv in (1e-5, 1e-6, 0.0, 1.0):
            p.const(cv)
        p.glob["ofs"] = p.sb("ofs", [1, 16], I32)
        p.glob["rg"] = es.enter_context(nc.gpsimd.register("rg0"))
        rwkv_prelude(p, io, dbg)
        with p.scope():
            p.glob["pb"] = [p.ps(f"pb{i}", 512) for i in range(7)]
            p.glob["pbT"] = p.ps("pbT", 1024, BF16)
            p.glob["acc"] = [p.sb(f"acc{i}", [128, Tloc], F32) for i in range(D // 128)]
            with p.scope(), nc.named_scope("ph_ln"):
                phase_load_ln(p, io, dbg)
                rwkv_stageA(p, io, dbg)
            with p.scope(), nc.named_scope("ph_rwkv"):
                rwkv_main(p, io, dbg)
            with p.scope(), nc.named_scope("ph_rwkvpost"):
                rwkv_post(p, io, dbg)
            p.glob["attn_done"] = (p.sv, p.sv.n)
            with p.scope(), nc.named_scope("ph_ffn0"):
                phase_ffn0(p, io, dbg)
            with p.scope(), nc.named_scope("ph_l1pre"):
                phase_l1pre(p, io, dbg)
            p.glob["acc_done"] = (p.sv, p.sv.n)
            phase_acc_spill(p, io)
        big = p.ps("big", 4096)
        p.glob["big"] = big
        with p.scope(), nc.named_scope("ph_mamba"):
            phase_mamba(p, io, dbg)
        p.glob["acc"] = [p.sb(f"accE{i}", [128, Tloc], F32) for i in range(D // 128)]
        with p.scope(), nc.named_scope("ph_epi"):
            phase_epilogue(p, io, dbg)
        p.glob["pb"] = [ColView(big, i * 512) for i in range(7)]
        with nc.named_scope("ph_out"):
            phase_out(p, io)
    return nc


def kernel(**inputs):
    Tloc = TFULL // 2
    per_core = prep_inputs(inputs, Tloc)
    nc = build(Tloc)
    res = run_bass_kernel_spmd(nc, per_core, core_ids=list(range(8)))
    out = np.zeros((NB, TFULL, D), np.float32)
    for core in range(8):
        b, half = core // 2, core % 2
        out[b, half * Tloc:(half + 1) * Tloc] = res.results[core]["x_out"]
    return out


# version 06
# speedup vs baseline: 1.0144x; 1.0144x over previous
import contextlib
import numpy as np
import ml_dtypes
import concourse.bass as bass
import concourse.mybir as mybir
from concourse.bass_utils import run_bass_kernel_spmd
from concourse.bass_isa_rust import ReduceOp

RED_MAX = ReduceOp.max

F32 = mybir.dt.float32
BF16 = mybir.dt.bfloat16
I32 = mybir.dt.int32
AF = mybir.ActivationFunctionType
OP = mybir.AluOpType
AX = mybir.AxisListType

D = 1024
NB = 4
TFULL = 4096
H, NH = 16, 64
DI = 2816
DM = 2048
NS = 64
CH = 32

ENG = dict(v="vector", s="scalar", t="tensor", g="gpsimd", y="sync")


class Sem:
    __slots__ = ("h", "n")

    def __init__(self, nc, es, name):
        self.h = es.enter_context(nc.semaphore(name))
        self.n = 0


class Prog:
    def __init__(self, nc, es, block, Tloc):
        self.nc, self.es, self.block = nc, es, block
        self.es0 = es
        self.T = Tloc
        self.ld = Sem(nc, es, "ld")
        self.gld = Sem(nc, es, "gld")
        self.sv = Sem(nc, es, "sv")
        self.ss = Sem(nc, es, "ss")
        self.sm = Sem(nc, es, "sm")
        self.sg = Sem(nc, es, "sg")
        self.sy = Sem(nc, es, "sy")
        self.sbm = Sem(nc, es, "sbm")
        self.glob = {}
        self._streams = []
        self._cpool = es.enter_context(nc.sbuf_tensor("cpool", [128, 16], F32))
        self._cvals = {}
        self._cinit = None

    def const(self, val, rows=128):
        if val not in self._cvals:
            assert len(self._cvals) < 16
            j = len(self._cvals)
            self._cvals[val] = j
            tok = self.op("v", "memset", self._cpool[:, j:j + 1], float(val), inc=self.sv)
            self._cinit = (self.sv, tok)
        j = self._cvals[val]
        return self._cpool[0:rows, j:j + 1]

    def op(self, eng, meth, *args, waits=(), inc=None, inck=1, **kw):
        waits = tuple(waits)

        def f(e):
            for sem, val in waits:
                if val > 0:
                    e.wait_ge(sem.h, val)
            ins = getattr(e, meth)(*args, **kw)
            if inc is not None:
                ins.then_inc(inc.h, inck)
        getattr(self.block, ENG[eng])(f)
        if inc is not None:
            inc.n += inck
            return inc.n
        return None

    def wait(self, eng, waits):
        waits = tuple(waits)

        def f(e):
            for sem, val in waits:
                e.wait_ge(sem.h, val)
        getattr(self.block, ENG[eng])(f)

    def dma(self, eng, dst, src, waits=()):
        sem = self.gld if eng == "g" else self.ld
        return self.op(eng, "dma_start", dst, src, waits=waits, inc=sem, inck=16)

    def stream(self, name, nbuf=2):
        s = DStream(self, name, nbuf)
        self._streams.append(s)
        return s

    @contextlib.contextmanager
    def scope(self):
        old_es = self.es
        es2 = contextlib.ExitStack()
        self.es = es2
        try:
            yield
        finally:
            self.barrier()
            es2.close()
            self.es = old_es

    def sb(self, name, shape, dtype):
        return self.es.enter_context(self.nc.sbuf_tensor(name, shape, dtype))

    def ps(self, name, cols, dtype=F32):
        return self.es.enter_context(self.nc.psum_tensor(name, [128, cols], dtype))

    def barrier(self):
        dws = [(self.ld, self.ld.n), (self.gld, self.gld.n)]
        for st_ in self._streams:
            dws += st_.done()
        self.wait("y", dws)
        tgt = self.sbm.n + 5
        for eng in "vstgy":
            self.op(eng, "nop", inc=self.sbm)
        for eng in "vstgy":
            self.wait(eng, [(self.sbm, tgt)])


class DStream:

    def __init__(self, p, name, nbuf):
        self.p, self.nbuf = p, nbuf
        self.sems = [Sem(p.nc, p.es0, f"{name}{i}") for i in range(nbuf)]
        self.i = 0

    def load(self, dst, src, eng="y", waits=(), adv=True, **kw):
        s = self.sems[self.i % self.nbuf]
        self.p.op(eng, "dma_start", dst, src, waits=waits, inc=s, inck=16, **kw)
        if adv:
            self.i += 1
        return (s, s.n)

    def adv(self):
        self.i += 1

    def done(self):
        return [(s, s.n) for s in self.sems if s.n > 0]



def _bf16(a):
    return np.ascontiguousarray(np.asarray(a).astype(ml_dtypes.bfloat16))


def _f32(a):
    return np.ascontiguousarray(np.asarray(a, dtype=np.float32))


def _tiles(vec):
    v = _f32(vec).reshape(-1)
    return np.ascontiguousarray(v.reshape(v.shape[0] // 128, 128).T)


def prep_inputs(inputs, Tloc):
    x = _f32(inputs["L0_attn_x"])
    sh = {}
    for nm in ["Wr", "Wk", "Wv", "Ww", "Wg", "Wo"]:
        sh["w" + nm] = _bf16(np.asarray(inputs["L0_attn_" + nm]).T)
    for nm in ["ln_w", "ln_b", "mix_k", "mix_v", "mix_r", "mix_w", "time_decay"]:
        sh[nm] = _tiles(inputs["L0_attn_" + nm])
    sh["u_vec"] = _tiles(np.asarray(inputs["L0_attn_time_first"]).reshape(-1))
    sh["clip_x"] = _f32(inputs["L0_ffn_clip_x"]).reshape(1, 1)
    sh["clip_down"] = _f32(inputs["L0_ffn_clip_down"]).reshape(1, 1)
    sh["w_gate_r"] = _f32(inputs["L0_ffn_w_gate"])
    sh["w_up_r"] = _f32(inputs["L0_ffn_w_up"])
    sh["w_down_r"] = _f32(inputs["L0_ffn_w_down"])
    sh["w_gate_t"] = _f32(np.asarray(inputs["L0_ffn_w_gate"]).T)
    sh["w_up_t"] = _f32(np.asarray(inputs["L0_ffn_w_up"]).T)
    sh["w_down_t"] = _f32(np.asarray(inputs["L0_ffn_w_down"]).T)
    sh["norm_w"] = _tiles(inputs["L1_attn_norm_w"])
    sh["w_inproj"] = _bf16(np.asarray(inputs["L1_attn_in_proj_w"]).T)
    sh["conv_w"] = np.ascontiguousarray(_f32(inputs["L1_attn_conv_w"]).reshape(16, 128, 4))
    sh["conv_b"] = _tiles(inputs["L1_attn_conv_b"])
    sh["w_xproj"] = _bf16(np.asarray(inputs["L1_attn_x_proj_w"])[64:128].T)
    sh["w_outproj"] = _bf16(np.asarray(inputs["L1_attn_out_proj_w"]).T)
    sh["D_all"] = _tiles(inputs["L1_attn_D_param"])
    ind = np.zeros((128, 16), np.float32)
    for pp in range(128):
        ind[pp, pp // 8] = 1.0
    sh["ind16"] = _bf16(ind)
    sh["w_win"] = _bf16(inputs["L1_ffn_Win"])
    sh["bin_"] = _f32(inputs["L1_ffn_bin_"]).reshape(NH, 1)
    sh["W1"] = _f32(inputs["L1_ffn_W1"])
    sh["b1"] = _f32(inputs["L1_ffn_b1"])
    sh["W2"] = _f32(inputs["L1_ffn_W2"])
    sh["b2"] = _f32(inputs["L1_ffn_b2"]).reshape(NH, 1)
    sh["w_wout"] = _bf16(inputs["L1_ffn_Wout"])
    sh["bout"] = _tiles(inputs["L1_ffn_bout"])
    sh["ident"] = _f32(np.eye(128))
    sh["identb"] = _bf16(np.eye(128))
    m = np.tril(np.ones((CH, CH), np.float32), -1).T
    sh["mask32"] = _bf16(np.tile(m, (1, H)))
    m64 = np.tril(np.ones((64, 64), np.float32), -1).T
    sh["mask64"] = _bf16(np.tile(m64, (1, H)))
    hs = np.zeros((8, 128, 16), np.float32)
    rs = np.zeros((8, 16, 128), np.float32)
    for ct in range(8):
        hs[ct, 0:64, 2 * ct] = 1.0
        hs[ct, 64:128, 2 * ct + 1] = 1.0
        rs[ct, 2 * ct, 0:64] = 1.0
        rs[ct, 2 * ct + 1, 64:128] = 1.0
    sh["headsel"] = _bf16(hs.transpose(1, 0, 2).reshape(128, 8 * 16))
    sh["rowsel"] = _bf16(rs.transpose(1, 0, 2).reshape(16, 8 * 128))
    bind = np.zeros((NS, NS * 128), np.float32)
    for n in range(NS):
        bind[n, n * 128:(n + 1) * 128] = 1.0
    sh["bind"] = _bf16(bind)

    per_core = []
    for core in range(8):
        b, half = core // 2, core % 2
        d = dict(sh)
        d["x_in"] = np.ascontiguousarray(x[b, half * Tloc:(half + 1) * Tloc])
        d["x_prev"] = (np.zeros((1, D), np.float32) if half == 0
                       else np.ascontiguousarray(x[b, Tloc - 1:Tloc]))
        d["halo_mask"] = np.full((1, 1), float(half), np.float32)
        d["halo_maskv"] = np.full((128, 1), float(half), np.float32)
        sl = slice(half * (DM // 2), (half + 1) * (DM // 2))
        al = _f32(inputs["L1_attn_A_log"])[sl]
        als = np.empty((128, 512), np.float32)
        for cb in range(64):
            blk = al[128 * (cb // 8) + 16 * (cb % 8):][:16]
            als[:, cb * 8:(cb + 1) * 8] = (
                blk.reshape(16, 8, 8).transpose(0, 2, 1).reshape(128, 8))
        d["A_log_h"] = np.ascontiguousarray(als)
        d["w_dt_h"] = np.ascontiguousarray(
            _bf16(np.asarray(inputs["L1_attn_dt_w"]).T)[:, sl])
        d["dt_bh"] = _tiles(np.asarray(inputs["L1_attn_dt_b"])[sl])
        T2 = 2 * Tloc
        offs = [(half * 8 + ct) * (128 * Tloc) for ct in range(8)]
        offs += [s * (8 * 128 * T2) + half * Tloc for s in range(2)]
        offs += [0] * 6
        d["offs"] = np.asarray(offs, np.int32).reshape(1, 16)
        per_core.append(d)
    return per_core


def declare_io(nc, Tloc, dbg):
    io = {}

    def inp(name, shape, dtype=F32):
        io[name] = nc.dram_tensor(name, shape, dtype, kind="ExternalInput").ap()

    def outp(name, shape, dtype=F32):
        io[name] = nc.dram_tensor(name, shape, dtype, kind="ExternalOutput").ap()

    inp("x_in", [Tloc, D]); inp("x_prev", [1, D]); inp("halo_mask", [1, 1])
    for nm in ["Wr", "Wk", "Wv", "Ww", "Wg", "Wo"]:
        inp("w" + nm, [D, D], BF16)
    for nm in ["ln_w", "ln_b", "mix_k", "mix_v", "mix_r", "mix_w", "time_decay", "u_vec",
               "norm_w", "bout"]:
        inp(nm, [128, 8])
    inp("conv_b", [128, 16])
    inp("halo_maskv", [128, 1])
    inp("mask64", [64, H * 64], BF16)
    inp("headsel", [128, 8 * 16], BF16)
    inp("rowsel", [16, 8 * 128], BF16)
    inp("clip_x", [1, 1]); inp("clip_down", [1, 1])
    inp("w_gate_r", [DI, D]); inp("w_up_r", [DI, D]); inp("w_down_r", [D, DI])
    inp("w_gate_t", [D, DI]); inp("w_up_t", [D, DI]); inp("w_down_t", [DI, D])
    inp("w_inproj", [D, 2 * DM], BF16)
    inp("conv_w", [16, 128, 4])
    inp("w_xproj", [DM, NS], BF16); inp("w_dt_h", [DM, DM // 2], BF16)
    inp("w_outproj", [DM, D], BF16)
    inp("D_all", [128, 16]); inp("ind16", [128, 16], BF16); inp("dt_bh", [128, 8])
    io["offs"] = nc.dram_tensor("offs", [1, 16], mybir.dt.int32, kind="ExternalInput").ap()
    inp("w_win", [D, NH], BF16); inp("bin_", [NH, 1])
    inp("W1", [NH, 16]); inp("b1", [NH, 16]); inp("W2", [NH, 16]); inp("b2", [NH, 1])
    inp("w_wout", [NH, D], BF16)
    inp("ident", [128, 128]); inp("identb", [128, 128], BF16)
    inp("mask32", [CH, H * CH], BF16)
    inp("bind", [NS, NS * 128], BF16)
    inp("A_log_h", [128, 512])
    outp("x_out", [Tloc, D])
    if dbg:
        outp("dbg_xn0", [8, 128, Tloc], BF16)
    return io


def phase_load_ln(p, io, dbg):
    T = p.T
    FT = D // 128
    TT = T // 128
    acc = p.glob["acc"]
    xn0 = [p.sb(f"xn0_{i}", [128, 1 + T], BF16) for i in range(FT)]
    p.glob["xn0"] = xn0

    ident = p.glob["ident"]
    identb = p.glob["identb"]
    t_c = p.glob["t_ident"]

    xst = p.stream("ldx", 2)
    xtok = [p.sb(f"xtok{j}", [128, D], F32) for j in range(2)]
    xntok = p.sb("xntok", [128, D], F32)
    sq = p.sb("sq_ln", [128, D], F32)
    st8 = p.sb("st8", [128, 8], F32)
    pbk = p.glob["pb"]

    sv, ss, sm = p.sv, p.ss, p.sm
    gates = {}
    for tt in range(TT):
        buf = xtok[tt % 2]
        w = [(sv, gates[tt - 2])] if (tt - 2) in gates else ()
        tok_ld = xst.load(buf[:], io["x_in"][tt * 128:(tt + 1) * 128, :], waits=w)
        p.op("v", "reduce_sum", st8[:, 0:1], buf[:], axis=AX.X, waits=[tok_ld], inc=sv)
        tok_sq = p.op("s", "activation", sq[:], buf[:], AF.Square, waits=[tok_ld], inc=ss)
        t1 = p.op("v", "reduce_sum", st8[:, 1:2], sq[:], axis=AX.X, waits=[(ss, tok_sq)], inc=sv)
        t2 = p.op("v", "tensor_scalar", st8[:, 2:3], st8[:, 0:1], 1.0 / D, None, OP.mult,
                  waits=[(sv, t1)], inc=sv)
        t3 = p.op("s", "activation", st8[:, 3:4], st8[:, 2:3], AF.Square, waits=[(sv, t2)], inc=ss)
        t4 = p.op("v", "scalar_tensor_tensor", st8[:, 4:5], st8[:, 1:2], 1.0 / D, st8[:, 3:4],
                  OP.mult, OP.subtract, waits=[(ss, t3)], inc=sv)
        t5 = p.op("s", "activation", st8[:, 5:6], st8[:, 4:5], AF.Sqrt, bias=p.const(1e-5),
                  waits=[(sv, t4)], inc=ss)
        t6 = p.op("v", "reciprocal", st8[:, 6:7], st8[:, 5:6], waits=[(ss, t5)], inc=sv)
        t7 = p.op("v", "tensor_scalar", xntok[:], buf[:], st8[:, 2:3], st8[:, 6:7],
                  OP.subtract, OP.mult, waits=[(sv, t6)], inc=sv)
        for grp in range(2):
            bank = pbk[grp]
            tms = 0
            for j in range(4):
                ft = grp * 4 + j
                tms = p.op("t", "transpose", bank[:, j * 128:(j + 1) * 128],
                           buf[:, ft * 128:(ft + 1) * 128], ident[:],
                           waits=[tok_ld, (sv, gates.get(("psx", grp), 0)), t_c], inc=sm)
            for j in range(4):
                ft = grp * 4 + j
                e1 = p.op("v", "tensor_copy", acc[ft][:, tt * 128:(tt + 1) * 128],
                          bank[:, j * 128:(j + 1) * 128],
                          waits=[(sm, tms)], inc=sv)
            gates[("psx", grp)] = e1
        for grp in range(2):
            bank = pbk[2 + grp]
            tms = 0
            for j in range(4):
                ft = grp * 4 + j
                tms = p.op("t", "transpose", bank[:, j * 128:(j + 1) * 128],
                           xntok[:, ft * 128:(ft + 1) * 128], ident[:],
                           waits=[(sv, t7), (sv, gates.get(("psn", grp), 0))], inc=sm)
            for j in range(4):
                ft = grp * 4 + j
                e2 = p.op("v", "tensor_copy", xn0[ft][:, 1 + tt * 128:1 + (tt + 1) * 128],
                          bank[:, j * 128:(j + 1) * 128],
                          waits=[(sm, tms)], inc=sv)
            gates[("psn", grp)] = e2
        gates[tt] = e2
    xprev = p.sb("xprev", [1, D], F32)
    hm = p.sb("hm", [1, 1], F32)
    hst = p.stream("ldh", 1)
    hst.load(xprev[:], io["x_prev"])
    t_h = hst.load(hm[:], io["halo_mask"])
    st1 = p.sb("st1", [1, 8], F32)
    sq1 = p.sb("sq1", [1, D], F32)
    a = p.op("v", "reduce_sum", st1[:, 0:1], xprev[:], axis=AX.X, waits=[t_h], inc=sv)
    b_ = p.op("s", "activation", sq1[:], xprev[:], AF.Square, waits=[t_h], inc=ss)
    c = p.op("v", "reduce_sum", st1[:, 1:2], sq1[:], axis=AX.X, waits=[(ss, b_)], inc=sv)
    d_ = p.op("v", "tensor_scalar", st1[:, 2:3], st1[:, 0:1], 1.0 / D, None, OP.mult,
              waits=[(sv, c)], inc=sv)
    e_ = p.op("s", "activation", st1[:, 3:4], st1[:, 2:3], AF.Square, waits=[(sv, d_)], inc=ss)
    f_ = p.op("v", "scalar_tensor_tensor", st1[:, 4:5], st1[:, 1:2], 1.0 / D, st1[:, 3:4],
              OP.mult, OP.subtract, waits=[(ss, e_)], inc=sv)
    g_ = p.op("s", "activation", st1[:, 5:6], st1[:, 4:5], AF.Sqrt, bias=p.const(1e-5, rows=1),
              waits=[(sv, f_)], inc=ss)
    h_ = p.op("v", "reciprocal", st1[:, 6:7], st1[:, 5:6], waits=[(ss, g_)], inc=sv)
    i_ = p.op("v", "tensor_tensor", st1[:, 6:7], st1[:, 6:7], hm[:, 0:1], OP.mult,
              waits=[(sv, h_)], inc=sv)
    j_ = p.op("v", "tensor_scalar", sq1[:], xprev[:], st1[:, 2:3], st1[:, 6:7],
              OP.subtract, OP.mult, waits=[(sv, i_)], inc=sv)
    prev_ev = gates[("psx", 0)]
    for ft in range(FT):
        tm = p.op("t", "transpose", pbk[0][:, 0:1], sq1[:, ft * 128:(ft + 1) * 128],
                  ident[0:1, 0:1], waits=[(sv, j_), (sv, prev_ev)], inc=sm)
        prev_ev = p.op("v", "tensor_copy", xn0[ft][:, 0:1], pbk[0][:, 0:1],
                       waits=[(sm, tm)], inc=sv)
    p.glob["xn0_done"] = (sv, sv.n)

    if dbg:
        dst = p.stream("dbgs", 1)
        for ft in range(FT):
            dst.load(io["dbg_xn0"][ft], xn0[ft][:, 1:1 + T], eng="y",
                     waits=[(sv, sv.n)])
        p.glob["dbg_st"] = dst


RG_PAIRS = [[0, 1], [2, 3], [4, 5], [6, 7]]
RG_ALL = [[0, 1, 2, 3, 4, 5, 6, 7]]


def set_ncores(n):
    global RG_PAIRS, RG_ALL
    RG_PAIRS = [[i, i + 1] for i in range(0, n, 2)]
    RG_ALL = [list(range(n))]


def vchunk(tens, coff, n, step=512, groups=8, pitch=None):
    pitch = pitch if pitch is not None else tens.shape[1]
    return bass.AP(tens, coff, [[pitch, 128], [step, groups], [1, n]])


def prow(tens, row0, nrow, coff, ncol):
    pitch = tens.shape[1]
    return bass.AP(tens, row0 * pitch + coff, [[pitch, nrow], [1, ncol]])


def rwkv_prelude(p, io, dbg):
    nc, T = p.nc, p.T
    FT = 8
    C = 64
    sv, ss, sm, sg = p.sv, p.ss, p.sm, p.sg
    ident = p.sb("identS", [128, 128], F32)
    identb = p.sb("identbS", [128, 128], BF16)
    p.glob["ident"] = ident
    p.glob["identb"] = identb
    cst = p.stream("ldc", 1)
    cst.load(ident[:], io["ident"], adv=False)
    p.glob["t_ident"] = cst.load(identb[:], io["identb"], adv=False)
    dr = {}
    dbg_vis = {"xn2_sp", "qc_sp", "yraw_sp", "g_sp"}
    for nm, shape, dt_ in [("xn0_sp", [FT, 128, T], BF16), ("xn2_sp", [FT, 128, 1 + T], BF16),
                           ("qc_sp", [128, FT * T], BF16), ("yraw_sp", [128, FT * T], BF16),
                           ("g_sp", [128, FT * T], BF16),
                           ("s_snd", [FT, 128, NH], F32), ("s_rcv", [2, FT, 128, NH], F32),
                           ("armax_in", [1, 1], F32), ("armax_out", [1, 1], F32)]:
        kind = "ExternalOutput" if (dbg and nm in dbg_vis) else "Internal"
        dr[nm] = nc.dram_tensor(nm, shape, dt_, kind=kind)
    dra = {nm: h.ap() for nm, h in dr.items()}
    p.glob["dr"] = dr
    p.glob["dra"] = dra

    vst = p.stream("vld", 1)
    lnw = p.sb("lnw", [128, 8], F32); vst.load(lnw[:], io["ln_w"], adv=False)
    lnb = p.sb("lnb", [128, 8], F32); vst.load(lnb[:], io["ln_b"], adv=False)
    mixv = p.sb("mixv", [128, 32], F32)
    for i, nm in enumerate(["mix_k", "mix_v", "mix_r", "mix_w"]):
        vst.load(mixv[:, i * 8:(i + 1) * 8], io[nm], adv=False)
    tdv = p.sb("tdv", [128, 8], F32); vst.load(tdv[:], io["time_decay"], adv=False)
    uv = p.sb("uv", [128, 8], F32); vst.load(uv[:], io["u_vec"], adv=False)
    hmv = p.sb("hmv", [128, 1], F32); vst.load(hmv[:], io["halo_maskv"], adv=False)
    mk64 = p.sb("mk64", [C, H * C], BF16); vst.load(mk64[:], io["mask64"], adv=False)
    hsel = p.sb("hsel", [128, 8 * 16], BF16); vst.load(hsel[:], io["headsel"], adv=False)
    rsel = p.sb("rsel", [16, 8 * 128], BF16)
    t_vec = vst.load(rsel[:], io["rowsel"], adv=False)
    sgm = p.sb("sgm", [128, 32], F32)
    t_sgm = p.op("s", "activation", sgm[:], mixv[:], AF.Sigmoid, waits=[t_vec], inc=ss)

    onesb = p.sb("onesb", [128, 128], BF16)
    t_ones = p.op("v", "memset", onesb[:], 1.0, inc=sv)
    onesf = p.sb("onesf", [128, 128], F32)
    t_onesf = p.op("v", "memset", onesf[:], 1.0, inc=sv)

    for _k in ("dr", "dra", "lnw", "lnb", "mixv", "tdv", "uv", "hmv", "mk64", "hsel",
               "rsel", "sgm", "t_vec", "t_sgm", "onesb", "onesf", "t_ones", "t_onesf"):
        p.glob[_k] = locals()[_k]
    p.glob["rtok"] = {}


def rwkv_stageA(p, io, dbg):
    nc, T = p.nc, p.T
    FT = 8
    NSC = T // 512
    C = 64
    NCH = 512 // C
    acc, xn0, pb = p.glob["acc"], p.glob.get("xn0"), p.glob["pb"]
    pbT = p.glob["pbT"]
    identb = p.glob["identb"]
    sv, ss, sm, sg = p.sv, p.ss, p.sm, p.sg
    dr, dra = p.glob["dr"], p.glob["dra"]
    lnw, lnb, tdv, uv, hmv = p.glob["lnw"], p.glob["lnb"], p.glob["tdv"], p.glob["uv"], p.glob["hmv"]
    mk64, hsel, rsel, sgm = p.glob["mk64"], p.glob["hsel"], p.glob["rsel"], p.glob["sgm"]
    t_vec, t_sgm = p.glob["t_vec"], p.glob["t_sgm"]
    onesb, onesf, t_ones, t_onesf = p.glob["onesb"], p.glob["onesf"], p.glob["t_ones"], p.glob["t_onesf"]
    tok = p.glob["rtok"]

    xn2f = [p.sb(f"xn2f{i}", [128, 1 + T], BF16) for i in range(FT)]
    sqb = p.sb("sqb", [128, 512], BF16)
    rstd2 = p.sb("rstd2", [128, 512], F32)
    xdone = p.glob["xn0_done"]
    bf6 = 0
    ncol = 1 + T
    xn2_tok = 0
    for ck in range((ncol + 511) // 512):
        c0 = ck * 512
        w_ = min(ncol, c0 + 512) - c0
        mm_t = 0
        for ft in range(FT):
            tsq = p.op("s", "activation", sqb[:, 0:w_], xn0[ft][:, c0:c0 + w_], AF.Square,
                       waits=[xdone, (sm, mm_t), (sv, bf6)], inc=ss)
            mm_t = p.op("t", "matmul", pb[6][:, 0:w_], onesb[:], sqb[:, 0:w_],
                        start=(ft == 0), stop=(ft == FT - 1),
                        waits=[(ss, tsq), (sv, t_ones), (sv, bf6)], inc=sm)
        t1 = p.op("s", "activation", rstd2[:, 0:w_], pb[6][:, 0:w_], AF.Sqrt,
                  bias=p.const(1e-5), scale=1.0 / D, waits=[(sm, mm_t), (sv, xn2_tok)], inc=ss)
        bf6 = p.op("v", "reciprocal", rstd2[:, 0:w_], rstd2[:, 0:w_], waits=[(ss, t1)], inc=sv)
        for ft in range(FT):
            t2 = p.op("v", "tensor_tensor", xn2f[ft][:, c0:c0 + w_], xn0[ft][:, c0:c0 + w_],
                      rstd2[:, 0:w_], OP.mult, waits=[(sv, bf6)], inc=sv)
            xn2_tok = p.op("v", "tensor_scalar", xn2f[ft][:, c0:c0 + w_], xn2f[ft][:, c0:c0 + w_],
                           lnw[:, ft:ft + 1], lnb[:, ft:ft + 1], OP.mult, OP.add,
                           waits=[(sv, t2), t_vec], inc=sv)
    for ft in range(FT):
        xn2_tok = p.op("v", "tensor_tensor", xn2f[ft][:, 0:1], xn2f[ft][:, 0:1], hmv[:],
                       OP.mult, waits=[(sv, xn2_tok), t_vec], inc=sv)

    mx8 = p.sb("mx8", [128, 9], F32)
    tmx = 0
    for ft in range(FT):
        tmx = p.op("v", "reduce_max", mx8[:, ft:ft + 1], xn0[ft][:, 1:1 + T], axis=AX.X,
                   apply_absolute_value=True, waits=[xdone], inc=sv)
    tmx = p.op("v", "reduce_max", mx8[:, 8:9], mx8[:, 0:8], axis=AX.X,
               apply_absolute_value=True, waits=[(sv, tmx)], inc=sv)
    mxr = p.sb("mxr", [1, 129], F32)
    ttr = p.op("t", "transpose", pb[0][0:1, 0:128], mx8[:, 8:9], p.glob["ident"][:],
               waits=[(sv, tmx), xdone], inc=sm)
    tev = p.op("s", "copy", mxr[0:1, 0:128], pb[0][0:1, 0:128], waits=[(sm, ttr)], inc=ss)
    tpr = p.op("v", "reduce_max", mxr[0:1, 128:129], mxr[0:1, 0:128], axis=AX.X,
               waits=[(ss, tev)], inc=sv)
    tdm = p.dma("g", dra["armax_in"], mxr[0:1, 128:129], waits=[(sv, tpr)])
    tcc = p.op("g", "collective_compute", "AllReduce", OP.max, replica_groups=RG_ALL,
               ins=[dra["armax_in"]], outs=[dra["armax_out"]],
               waits=[(p.gld, tdm)], inc=sg)
    p.glob["armax_tok"] = (sg, tcc)

    spst = p.stream("spl", 2)
    for ft in range(FT):
        spst.load(dra["xn0_sp"][ft], xn0[ft][:, 1:1 + T], waits=[xdone])
        spst.load(dra["xn2_sp"][ft], xn2f[ft][:], waits=[(sv, xn2_tok)])
    sp_done = spst.done()
    p.glob["xn0_spilled"] = sp_done

    p.glob["sp_done"] = sp_done

def rwkv_main(p, io, dbg):
    nc, T = p.nc, p.T
    FT = 8
    NSC = T // 512
    C = 64
    NCH = 512 // C
    acc, xn0, pb = p.glob["acc"], p.glob.get("xn0"), p.glob["pb"]
    pbT = p.glob["pbT"]
    identb = p.glob["identb"]
    sv, ss, sm, sg = p.sv, p.ss, p.sm, p.sg
    dr, dra = p.glob["dr"], p.glob["dra"]
    lnw, lnb, tdv, uv, hmv = p.glob["lnw"], p.glob["lnb"], p.glob["tdv"], p.glob["uv"], p.glob["hmv"]
    mk64, hsel, rsel, sgm = p.glob["mk64"], p.glob["hsel"], p.glob["rsel"], p.glob["sgm"]
    t_vec, t_sgm = p.glob["t_vec"], p.glob["t_sgm"]
    onesb, onesf, t_ones, t_onesf = p.glob["onesb"], p.glob["onesf"], p.glob["t_ones"], p.glob["t_onesf"]
    tok = p.glob["rtok"]

    sp_done = p.glob["sp_done"]
    xn2sc = p.sb("xn2sc", [128, FT * 513], BF16)
    dsb = p.sb("dsb", [128, FT * 512], BF16)
    xmix = p.sb("xmix", [128, FT * 512], BF16)
    Wsb = [p.sb(f"Wsb{j}", [128, FT * 1024], BF16) for j in range(2)]
    ksb = p.sb("ksb", [128, FT * 512], BF16)
    vtok = p.sb("vtok", [128, 4 * 1024], BF16)
    vfe = p.sb("vfe", [128, FT * 512], BF16)
    rsb = p.sb("rsb", [128, FT * 512], BF16)
    gsb = p.sb("gsb", [128, FT * 512], BF16)
    ewsb = p.sb("ewsb", [128, FT * 512], BF16)
    Lsb = p.sb("Lsb", [128, FT * 65], F32)
    qh = p.sb("qh", [128, FT * C], BF16)
    qcch = p.sb("qcch", [128, FT * C], BF16)
    kt = p.sb("kt", [128, FT * C], BF16)
    kh = p.sb("kh", [128, FT * C], BF16)
    ka = p.sb("ka", [128, FT * C], F32)
    qe = p.sb("qe", [128, FT * C], BF16)
    dt8 = p.sb("dt8", [128, 8], F32)
    dcum = p.sb("dcum", [128, 8], F32)
    rku = p.sb("rku", [128, FT * C], BF16)
    rhosb = p.sb("rhosb", [16, C], BF16)
    amsb = p.sb("amsb", [128, H * C], BF16)
    khtok = p.sb("khtok", [128, FT * 128], BF16)
    dtmp = p.sb("dtmp", [128, FT * C], BF16)
    yraw = p.sb("yraw", [128, FT * C], BF16)
    Ssb = [p.sb(f"Ssb{i}", [128, NH], F32) for i in range(FT)]
    Sbb = [p.sb(f"Sbb{i}", [128, NH], BF16) for i in range(FT)]
    t_s0 = 0
    for i in range(FT):
        p.op("v", "memset", Ssb[i][:], 0.0, inc=sv)
        t_s0 = p.op("v", "memset", Sbb[i][:], 0.0, inc=sv)
    t_dc = p.op("v", "memset", dcum[:], 1.0, inc=sv)

    wst = p.stream("wld", 2)
    xst2 = p.stream("x2ld", 1)
    qst = p.stream("qsp", 2)
    qgt = p.stream("qg", 1)
    qct = p.stream("qc", 1)
    qyt = p.stream("qy", 1)
    bankfree = {0: None, 1: None, 2: None}
    tok = {}
    t_chain = t_s0
    WNAMES = ["wWk", "wWv", "wWr", "wWg", "wWw"]
    MIXI = {"wWk": 0, "wWv": 1, "wWr": 2, "wWw": 3}

    for sc in range(NSC):
        wv = [(sv, tok[("mix", "wWw", sc - 1)])] if sc else []
        t_x2 = 0
        for ft in range(FT):
            t_x2 = xst2.load(xn2sc[:, ft * 513:(ft + 1) * 513],
                             dra["xn2_sp"][ft][:, sc * 512:sc * 512 + 513],
                             waits=(sp_done + wv) if ft == 0 else (), adv=(ft == FT - 1))
        t_d = p.op("v", "tensor_tensor", vchunk(dsb, 0, 512), vchunk(xn2sc, 1, 512, step=513),
                   vchunk(xn2sc, 0, 512, step=513), OP.subtract,
                   waits=[t_x2, (sv, tok.get(("dfree", sc - 1), 0))], inc=sv)
        for wi, wn in enumerate(WNAMES):
            wbuf = Wsb[wi % 2]
            gtok = tok.get(("wmm", sc, wi - 2)) or tok.get(("wmm", sc - 1, wi + 3)) or 0
            t_w = 0
            for ci in range(FT):
                t_w = wst.load(wbuf[:, ci * 1024:(ci + 1) * 1024],
                               io[wn][ci * 128:(ci + 1) * 128, :],
                               waits=[(sm, gtok)] if (gtok and ci == 0) else (),
                               adv=(ci == FT - 1))
            if wn != "wWg":
                mi = MIXI[wn]
                wmprev = [(sm, tok[("wmm", sc, wi - 1)])] if wi else (
                    [(sm, tok[("wmm", sc - 1, 4)])] if sc else [])
                t_m = 0
                for ft in range(FT):
                    t_m = p.op("v", "scalar_tensor_tensor", xmix[:, ft * 512:(ft + 1) * 512],
                               dsb[:, ft * 512:(ft + 1) * 512], sgm[:, mi * 8 + ft:mi * 8 + ft + 1],
                               xn2sc[:, ft * 513:ft * 513 + 512], OP.mult, OP.add,
                               waits=([(sv, t_d), (ss, t_sgm)] + wmprev) if ft == 0 else [(sv, t_d)],
                               inc=sv)
                tok[("mix", wn, sc)] = t_m
            else:
                t_m = tok[("mix", "wWr", sc)]
            if wn == "wWw":
                tok[("dfree", sc)] = t_m
            if wn == "wWv":
                t_mm = 0
                for tb in range(4):
                    for colh in range(2):
                        gv = tok.get(("vtev", sc, tb - 1, colh)) or bankfree[colh]
                        guse = tok.get(("vtuse", sc - 1))
                        for ci in range(FT):
                            t_mm = p.op("t", "matmul", pb[colh][:, 0:512],
                                        prow(xmix, 0, 128, ci * 512 + tb * 128, 128),
                                        wbuf[:, ci * 1024 + colh * 512:ci * 1024 + colh * 512 + 512],
                                        start=(ci == 0), stop=(ci == FT - 1),
                                        waits=[(sv, t_m), t_w] + ([gv] if gv else []), inc=sm)
                        ev = p.op("s", "copy",
                                  vtok[:, tb * 1024 + colh * 512:tb * 1024 + colh * 512 + 512],
                                  pb[colh][:, 0:512],
                                  waits=[(sm, t_mm)] + ([guse] if (guse and tb == 0 and colh == 0) else []),
                                  inc=ss)
                        tok[("vtev", sc, tb, colh)] = (ss, ev)
                bankfree[0] = tok[("vtev", sc, 3, 0)]
                bankfree[1] = tok[("vtev", sc, 3, 1)]
                tok[("vtok", sc)] = ev
                for co in range(FT):
                    gv = tok.get(("vfev", sc, co - 1))
                    gbank = bankfree[2]
                    guse = tok.get(("vfuse", sc - 1))
                    for ci in range(FT):
                        t_mm = p.op("t", "matmul", pb[2][:, 0:512],
                                    wbuf[:, ci * 1024 + co * 128:ci * 1024 + co * 128 + 128],
                                    xmix[:, ci * 512:(ci + 1) * 512],
                                    start=(ci == 0), stop=(ci == FT - 1),
                                    waits=[(sv, t_m), t_w] + ([gv] if gv else [])
                                    + ([gbank] if (gbank and co == 0) else []), inc=sm)
                    ev = p.op("s", "copy", vfe[:, co * 512:(co + 1) * 512], pb[2][:, 0:512],
                              waits=[(sm, t_mm)] + ([guse] if (guse and co == 0) else []), inc=ss)
                    tok[("vfev", sc, co)] = (ss, ev)
                tok[("vfe", sc)] = ev
                bankfree[2] = tok[("vfev", sc, FT - 1)]
                tok[("wmm", sc, wi)] = t_mm
            else:
                dst = {"wWk": ksb, "wWr": rsb, "wWg": gsb, "wWw": ewsb}[wn]
                bank = wi % 2
                t_mm = 0
                ev = 0
                guse = tok.get((wn + "use", sc - 1))
                if isinstance(guse, Sem):
                    guse = (guse, guse.n)
                for co in range(FT):
                    gv = tok.get((wn + "ev", sc, co - 1)) or bankfree[bank]
                    for ci in range(FT):
                        t_mm = p.op("t", "matmul", pb[bank][:, 0:512],
                                    wbuf[:, ci * 1024 + co * 128:ci * 1024 + co * 128 + 128],
                                    xmix[:, ci * 512:(ci + 1) * 512],
                                    start=(ci == 0), stop=(ci == FT - 1),
                                    waits=[(sv, t_m), t_w] + ([gv] if gv else []), inc=sm)
                    dguard = [guse] if (guse and co == 0) else []
                    if wn == "wWk":
                        ev = p.op("s", "copy", dst[:, co * 512:(co + 1) * 512],
                                  pb[bank][:, 0:512], waits=[(sm, t_mm)] + dguard, inc=ss)
                        evt = (ss, ev)
                    elif wn in ("wWr", "wWg"):
                        ev = p.op("s", "activation", dst[:, co * 512:(co + 1) * 512],
                                  pb[bank][:, 0:512], AF.Sigmoid,
                                  waits=[(sm, t_mm)] + dguard, inc=ss)
                        evt = (ss, ev)
                    else:
                        tw1 = p.op("v", "tensor_scalar", ewsb[:, co * 512:(co + 1) * 512],
                                   pb[bank][:, 0:512], tdv[:, co:co + 1], -10.0, OP.add, OP.max,
                                   waits=[(sm, t_mm), t_vec] + dguard, inc=sv)
                        ev = p.op("v", "tensor_scalar", ewsb[:, co * 512:(co + 1) * 512],
                                  ewsb[:, co * 512:(co + 1) * 512], 10.0, None, OP.min,
                                  waits=[(sv, tw1)], inc=sv)
                        evt = (sv, ev)
                    tok[(wn + "ev", sc, co)] = evt
                bankfree[bank] = tok[(wn + "ev", sc, FT - 1)]
                if wn == "wWw":
                    ev = p.op("s", "activation", ewsb[:], ewsb[:], AF.Exp,
                              waits=[(sv, ev)], inc=ss)
                tok[("proj", wn, sc)] = ev
                tok[("wmm", sc, wi)] = t_mm
        t_gsp = qgt.load(vchunk(dr["g_sp"], sc * 512, 512, step=T, pitch=FT * T),
                         vchunk(gsb, 0, 512), waits=[(ss, tok[("proj", "wWg", sc)])])

        for cc in range(NCH):
            gi = sc * NCH + cc
            co0 = cc * C
            wK = (ss, tok[("proj", "wWk", sc)])
            wR = (ss, tok[("proj", "wWr", sc)])
            wE = (ss, tok[("proj", "wWw", sc)])
            wV = (ss, tok[("vtok", sc)])
            wF = (ss, tok[("vfe", sc)])
            tL0 = p.op("v", "memset", bass.AP(Lsb, 0, [[FT * 65, 128], [65, FT], [1, 1]]), 0.0,
                       waits=[(sv, tok.get(("Lfree", gi - 1), 0)),
                              (ss, tok.get(("Lfree_ss", gi - 1), 0))], inc=sv)
            tsc_ = 0
            for ct in range(FT):
                tsc_ = p.op("v", "tensor_tensor_scan", Lsb[:, ct * 65 + 1:ct * 65 + 65],
                            onesf[:, 0:C], ewsb[:, ct * 512 + co0:ct * 512 + co0 + C],
                            0.0, OP.mult, OP.subtract,
                            waits=[(sv, tL0), (sv, t_onesf), wE], inc=sv)
            t_qe = p.op("s", "activation", qe[:], bass.AP(Lsb, 0, [[FT * 65, 128], [65, FT], [1, C]]),
                        AF.Exp, waits=[(sv, tsc_), (sv, tok.get(("qefree", gi - 1), 0))], inc=ss)
            t_qh = p.op("v", "tensor_tensor", qh[:], qe[:], vchunk(rsb, co0, C), OP.mult,
                        waits=[(ss, t_qe), wR, (sm, tok.get(("qhfree", gi - 1), 0))], inc=sv)
            tok[("qefree", gi)] = t_qh
            t_dt8 = p.op("s", "activation", dt8[:], bass.AP(Lsb, 64, [[FT * 65, 128], [65, FT], [1, 1]]),
                         AF.Exp, waits=[(sv, tsc_), (sv, tok.get(("dt8free", gi - 1), 0))], inc=ss)
            tok[("Lfree_ss", gi)] = t_dt8
            t_qc = 0
            gqc = tok.get(("qcsp", gi - 1))
            for ft in range(FT):
                t_qc = p.op("v", "tensor_scalar", qcch[:, ft * C:(ft + 1) * C],
                            qh[:, ft * C:(ft + 1) * C], dcum[:, ft:ft + 1], None, OP.mult,
                            waits=([(sv, t_qh), (sv, t_dc)] + ([gqc] if gqc else []))
                            if ft == 0 else [(sv, t_qh)], inc=sv)
            t_ka = p.op("v", "tensor_scalar", ka[:], bass.AP(Lsb, 1, [[FT * 65, 128], [65, FT], [1, C]]),
                        -1.0, 60.0, OP.mult, OP.min, waits=[(sv, tsc_), (ss, tok.get(("kafree", gi - 1), 0))], inc=sv)
            t_ke = p.op("s", "activation", kt[:], ka[:], AF.Exp,
                        waits=[(sv, t_ka), (sm, tok.get(("ktfree", gi - 1), 0))], inc=ss)
            t_kt = p.op("v", "tensor_tensor", kt[:], kt[:], vchunk(ksb, co0, C), OP.mult,
                        waits=[(ss, t_ke), wK], inc=sv)
            t_kha = p.op("v", "tensor_tensor", ka[:],
                         bass.AP(Lsb, 64, [[FT * 65, 128], [65, FT], [0, C]]),
                         bass.AP(Lsb, 1, [[FT * 65, 128], [65, FT], [1, C]]), OP.subtract,
                         waits=[(sv, t_ka), (ss, t_ke)], inc=sv)
            tok[("kafree", gi)] = t_ke
            t_khe = p.op("s", "activation", kh[:], ka[:], AF.Exp,
                         waits=[(sv, t_kha), (sm, tok.get(("khfree", gi - 1), 0))], inc=ss)
            tok[("kafree", gi)] = t_khe
            t_kh = p.op("v", "tensor_tensor", kh[:], kh[:], vchunk(ksb, co0, C), OP.mult,
                        waits=[(ss, t_khe), wK], inc=sv)
            tok[("Lfree", gi)] = t_kha
            tok[("khfree2", gi)] = t_kh
            tok[("scanlast", gi)] = t_kha
            t_rk = p.op("v", "tensor_tensor", rku[:], vchunk(rsb, co0, C), vchunk(ksb, co0, C),
                        OP.mult, waits=[wR, wK, (sm, tok.get(("rkufree", gi - 1), 0))], inc=sv)
            tok[("rkufree2", gi)] = t_rk
            t_ru = 0
            for ft in range(FT):
                t_ru = p.op("v", "tensor_scalar", rku[:, ft * C:(ft + 1) * C],
                            rku[:, ft * C:(ft + 1) * C], uv[:, ft:ft + 1], None, OP.mult,
                            waits=[(sv, t_rk), t_vec] if ft == 0 else [(sv, t_rk)], inc=sv)
            ro = co0 % 128
            t_gr = 0
            for h in range(H):
                ct, hh = h // 2, h % 2
                bank = 2 + h // 8
                t_gr = p.op("t", "matmul", prow(pb[bank], ro, C, (h % 8) * C, C),
                            prow(kt, hh * 64, 64, ct * C, C),
                            prow(qh, hh * 64, 64, ct * C, C),
                            start=(h % 8 == 0), stop=(h % 8 == 7), skip_group_check=(ro == 64),
                            waits=[(sv, t_kt), (sv, t_qh), (sv, tok.get(("grfree", gi - 1), 0))],
                            inc=sm)
            tok[("ktfree", gi)] = t_gr
            tok[("qhfree", gi)] = 0
            t_am1 = p.op("v", "tensor_tensor", prow(amsb, ro, C, 0, 8 * C), prow(pb[2], ro, C, 0, 8 * C),
                         mk64[:, 0:8 * C], OP.mult,
                         waits=[(sm, t_gr), t_vec, (sm, tok.get(("amfree", gi - 1), 0))], inc=sv)
            t_am = p.op("v", "tensor_tensor", prow(amsb, ro, C, 8 * C, 8 * C), prow(pb[3], ro, C, 0, 8 * C),
                        mk64[:, 8 * C:16 * C], OP.mult, waits=[(sm, t_gr)], inc=sv)
            t_rho = 0
            for ct in range(FT):
                t_rho = p.op("t", "matmul", prow(pb[4], 0, 16, 0, C), hsel[:, ct * 16:(ct + 1) * 16],
                             rku[:, ct * C:(ct + 1) * C], start=(ct == 0), stop=(ct == FT - 1),
                             waits=[(sv, t_ru), t_vec, (ss, tok.get(("rhofree", gi - 1), 0))], inc=sm)
            tok[("rkufree", gi)] = t_rho
            t_rhoe = p.op("s", "copy", rhosb[:], prow(pb[4], 0, 16, 0, C),
                          waits=[(sm, t_rho)], inc=ss)
            tok[("rhofree", gi)] = t_rhoe
            t_rep = 0
            for ct in range(FT):
                t_rep = p.op("t", "matmul", pb[5][:, ct * C:(ct + 1) * C], rsel[:, ct * 128:(ct + 1) * 128], rhosb[:],
                             start=(ct == 0), stop=(ct == FT - 1),
                             waits=[(ss, t_rhoe), t_vec, (sv, tok.get(("repfree", gi - 1), 0))], inc=sm)
            t_y = 0
            for h in range(H):
                ct, hh = h // 2, h % 2
                tb = co0 // 128
                t_y = p.op("t", "matmul", prow(pb[6], hh * 64, 64, ct * C, C),
                           prow(vtok, ro, 64, tb * 1024 + h * 64, 64),
                           prow(amsb, ro, C, (h % 8) * C + (h // 8) * 8 * C, C),
                           start=(h <= 1), stop=False, skip_group_check=(hh == 1),
                           waits=[(sv, t_am), (sv, t_am1), wV,
                                  (sv, tok.get(("yfree", gi - 1), 0))], inc=sm)
            for h in range(H):
                ct, hh = h // 2, h % 2
                t_y = p.op("t", "matmul", prow(pb[6], hh * 64, 64, ct * C, C),
                           prow(Sbb[ct], hh * 64, 64, 0, NH),
                           prow(qh, hh * 64, 64, ct * C, C),
                           start=False, stop=(h >= H - 2), skip_group_check=(hh == 1),
                           waits=[(sv, t_qh), (sv, t_chain)], inc=sm)
            tok[("qhfree", gi)] = t_y
            tok[("amfree", gi)] = t_y
            for rnd in range(2):
                t_tr = 0
                for j in range(4):
                    ct = rnd * 4 + j
                    gv = tok.get(("khtev", gi, rnd - 1)) or tok.get(("khtuse", gi - 1))
                    t_tr = p.op("t", "transpose", prow(pbT, ro, C, j * 128, 128),
                                kh[:, ct * C:(ct + 1) * C], identb[:],
                                waits=[(sv, t_kh)] + ([gv] if (gv and j == 0) else []), inc=sm)
                tok[("khtev", gi, rnd)] = (ss, p.op("s", "copy", prow(khtok, ro, C, rnd * 512, 512),
                                                    prow(pbT, ro, C, 0, 512),
                                                    waits=[(sm, t_tr)], inc=ss))
            tok[("khfree", gi)] = t_tr
            t_khtok = tok[("khtev", gi, 1)]
            t_ds = 0
            for h in range(H):
                ct, hh = h // 2, h % 2
                bank = 2 + ct // 4
                tb = co0 // 128
                t_ds = p.op("t", "matmul", prow(pb[bank], hh * 64, 64, (ct % 4) * 64, 64),
                            prow(khtok, ro, C, ct * 128 + hh * 64, 64),
                            prow(vtok, ro, C, tb * 1024 + h * 64, 64),
                            start=(h % 8 <= 1), stop=(h % 8 >= 6), skip_group_check=(hh == 1),
                            waits=[t_khtok, wV, (sv, t_am), (sv, t_am1),
                                   (sv, tok.get(("dsfree", gi - 1), 0))], inc=sm)
            tok[("khtuse", gi)] = (sm, t_ds)
            t_su = 0
            for ct in range(FT):
                bank = 2 + ct // 4
                t_su = p.op("v", "scalar_tensor_tensor", Ssb[ct][:], Ssb[ct][:],
                            dt8[:, ct:ct + 1], prow(pb[bank], 0, 128, (ct % 4) * 64, 64),
                            OP.mult, OP.add,
                            waits=[(ss, t_dt8), (sm, t_ds), (sv, t_chain)] if ct == 0
                            else [(sm, t_ds)], inc=sv)
            for ct in range(FT):
                t_su = p.op("v", "tensor_copy", Sbb[ct][:], Ssb[ct][:],
                            waits=[(sv, t_su)] if ct == 0 else [], inc=sv)
            t_chain = t_su
            tok[("dsfree", gi)] = t_su
            tok[("grfree", gi)] = t_su
            t_dc = p.op("v", "tensor_tensor", dcum[:], dcum[:], dt8[:], OP.mult,
                        waits=[(sv, t_qc), (ss, t_dt8)], inc=sv)
            tok[("dt8free", gi)] = t_dc
            t_d1 = p.op("v", "tensor_tensor", dtmp[:], pb[5][:, 0:8 * C], vchunk(vfe, co0, C),
                        OP.mult, waits=[(sm, t_rep), wF, (sv, tok.get(("dtmpfree", gi - 1), 0))],
                        inc=sv)
            tok[("repfree", gi)] = t_d1
            gys = tok.get(("ysp", gi - 1))
            t_ye = p.op("v", "tensor_tensor", yraw[:], prow(pb[6], 0, 128, 0, 8 * C), dtmp[:],
                        OP.add, waits=[(sm, t_y), (sv, t_d1)] + ([gys] if gys else []), inc=sv)
            tok[("dtmpfree", gi)] = t_ye
            tok[("yfree", gi)] = t_ye
            t_sq = qct.load(vchunk(dr["qc_sp"], sc * 512 + co0, C, step=T, pitch=FT * T),
                            qcch[:], waits=[(sv, t_qc)])
            tok[("qcsp", gi)] = t_sq
            t_sy = qyt.load(vchunk(dr["yraw_sp"], sc * 512 + co0, C, step=T, pitch=FT * T),
                            yraw[:], waits=[(sv, t_ye)])
            tok[("ysp", gi)] = t_sy
        lg = sc * NCH + NCH - 1
        tok[("vtuse", sc)] = (sv, tok[("dsfree", lg)])
        tok[("vfuse", sc)] = (sv, tok[("dtmpfree", lg)])
        tok[("wWkuse", sc)] = (sv, tok[("khfree2", lg)])
        tok[("wWruse", sc)] = (sv, tok[("rkufree2", lg)])
        tok[("wWguse", sc)] = t_gsp
        tok[("wWwuse", sc)] = (sv, tok[("scanlast", lg)])

    sst = p.stream("sste", 1)
    for ft in range(FT):
        t_se = sst.load(dra["s_snd"][ft], Ssb[ft][:], eng="g",
                        waits=[(sv, t_chain)] if ft == 0 else (), adv=False)
    t_cc2 = p.op("g", "collective_compute", "AllGather", OP.bypass, replica_groups=RG_PAIRS,
                 ins=[dra["s_snd"]], outs=[dra["s_rcv"]], waits=[t_se], inc=sg)
    sinb = p.sb("sinb", [128, FT * NH], BF16)
    t_si = 0
    for ft in range(FT):
        t_si = p.dma("g", sinb[:, ft * NH:(ft + 1) * NH], dra["s_rcv"][0][ft],
                     waits=[(sg, t_cc2)] if ft == 0 else ())
    t_sin0 = 0
    for ft in range(FT):
        t_sin0 = p.op("v", "tensor_scalar", sinb[:, ft * NH:(ft + 1) * NH],
                      sinb[:, ft * NH:(ft + 1) * NH], hmv[:], None, OP.mult,
                      waits=[(p.gld, t_si), t_vec] if ft == 0 else [], inc=sv)
    t_sin = (sv, t_sin0)

    p.glob["t_sin"] = t_sin
    p.glob["qst_done"] = qgt.done() + qct.done() + qyt.done()
    p.glob["sinb"] = sinb

def rwkv_post(p, io, dbg):
    nc, T = p.nc, p.T
    FT = 8
    NSC = T // 512
    C = 64
    NCH = 512 // C
    acc, xn0, pb = p.glob["acc"], p.glob.get("xn0"), p.glob["pb"]
    pbT = p.glob["pbT"]
    identb = p.glob["identb"]
    sv, ss, sm, sg = p.sv, p.ss, p.sm, p.sg
    dr, dra = p.glob["dr"], p.glob["dra"]
    lnw, lnb, tdv, uv, hmv = p.glob["lnw"], p.glob["lnb"], p.glob["tdv"], p.glob["uv"], p.glob["hmv"]
    mk64, hsel, rsel, sgm = p.glob["mk64"], p.glob["hsel"], p.glob["rsel"], p.glob["sgm"]
    t_vec, t_sgm = p.glob["t_vec"], p.glob["t_sgm"]
    onesb, onesf, t_ones, t_onesf = p.glob["onesb"], p.glob["onesf"], p.glob["t_ones"], p.glob["t_onesf"]
    tok = p.glob["rtok"]

    t_sin = p.glob["t_sin"]
    qdone = p.glob["qst_done"]
    sinb = p.glob["sinb"]
    dtmp = p.sb("dtmp2", [128, 8 * C], BF16)
    Wo_sb = p.sb("Wo_sb", [128, FT * 1024], BF16)
    wost = p.stream("wost", 1)
    for ci in range(FT):
        t_wo = wost.load(Wo_sb[:, ci * 1024:(ci + 1) * 1024],
                         io["wWo"][ci * 128:(ci + 1) * 128, :], adv=False)
    ysc = p.sb("ysc", [128, FT * 512], BF16)
    qcsc = p.sb("qcsc", [128, FT * 512], BF16)
    gsc2 = p.sb("gsc2", [128, FT * 512], BF16)
    pst2 = p.stream("pp", 1)
    for sc in range(NSC):
        wprev = [(sv, tok[("y2done", sc - 1)])] if sc else []
        t_ly = pst2.load(ysc[:], vchunk(dr["yraw_sp"], sc * 512, 512, step=T, pitch=FT * T),
                         waits=(qdone + wprev) if sc == 0 else wprev, adv=False)
        t_lq = pst2.load(qcsc[:], vchunk(dr["qc_sp"], sc * 512, 512, step=T, pitch=FT * T),
                         adv=False)
        t_lg = pst2.load(gsc2[:], vchunk(dr["g_sp"], sc * 512, 512, step=T, pitch=FT * T),
                         adv=False)
        t_l = t_lg
        for cc in range(NCH):
            co0 = cc * C
            t_cm = 0
            for h in range(H):
                ct, hh = h // 2, h % 2
                t_cm = p.op("t", "matmul", prow(pb[6], hh * 64, 64, ct * C, C),
                            prow(sinb, hh * 64, 64, ct * NH, NH),
                            prow(qcsc, hh * 64, 64, ct * 512 + co0, C),
                            start=(h <= 1), stop=(h >= H - 2), skip_group_check=(hh == 1),
                            waits=[t_sin, t_l, (sv, tok.get(("y2e", sc, cc - 1),
                                                            tok.get(("pb6free", sc - 1), 0)))],
                            inc=sm)
            t_ad = p.op("v", "tensor_tensor", dtmp[:], prow(pb[6], 0, 128, 0, 8 * C),
                        vchunk(ysc, co0, C), OP.add,
                        waits=[(sm, t_cm), t_l, (sv, tok.get(("y2e", sc, cc - 1), 0))], inc=sv)
            t_y2 = p.op("v", "tensor_tensor", vchunk(ysc, co0, C), dtmp[:],
                        vchunk(gsc2, co0, C), OP.mult, waits=[(sv, t_ad)], inc=sv)
            tok[("y2e", sc, cc)] = t_y2
        tok[("pb6free", sc)] = tok[("y2e", sc, NCH - 1)]
        t_acc = 0
        for co in range(FT):
            t_mm = 0
            for ci in range(FT):
                t_mm = p.op("t", "matmul", pb[0][:, 0:512],
                            Wo_sb[:, ci * 1024 + co * 128:ci * 1024 + co * 128 + 128],
                            ysc[:, ci * 512:(ci + 1) * 512],
                            start=(ci == 0), stop=(ci == FT - 1),
                            waits=[(sv, tok[("y2e", sc, NCH - 1)]), t_wo,
                                   (sv, tok.get(("accadd", sc, co - 1),
                                                tok.get(("accadd", sc - 1, FT - 1), 0)))],
                            inc=sm)
            t_acc = p.op("v", "tensor_tensor", acc[co][:, sc * 512:(sc + 1) * 512],
                         acc[co][:, sc * 512:(sc + 1) * 512], pb[0][:, 0:512], OP.add,
                         waits=[(sm, t_mm)], inc=sv)
            tok[("accadd", sc, co)] = t_acc
        tok[("y2done", sc)] = t_acc
    p.glob["attn_done"] = (sv, sv.n)



def phase_ffn0(p, io, dbg):
    nc, T = p.nc, p.T
    FT = 8
    IT = DI // 128
    NTC = T // 512
    acc, pb, pbT = p.glob["acc"], p.glob["pb"], p.glob["pbT"]
    sv, ss, sm, sg = p.sv, p.ss, p.sm, p.sg
    dr, dra = p.glob["dr"], p.glob["dra"]
    identb = p.glob["identb"]
    onesf = p.glob["onesf"]
    t_onesf = p.glob["t_onesf"]
    tok = {}

    h_sp = nc.dram_tensor("h_sp", [IT, 128, T], BF16,
                          kind="ExternalOutput" if dbg else "Internal")
    hra = h_sp.ap()
    ar2_in = nc.dram_tensor("ar2_in", [1, 1], F32, kind="Internal").ap()
    ar2_out = nc.dram_tensor("ar2_out", [1, 1], F32, kind="Internal").ap()

    sc1 = p.sb("fsc", [1, 8], F32)
    scb = p.sb("fscb", [128, 8], F32)
    cst = p.stream("fcl", 1)
    t_cx = cst.load(sc1[0:1, 0:1], io["clip_x"], adv=False)
    t_cd = cst.load(sc1[0:1, 1:2], io["clip_down"], adv=False)
    t_am = cst.load(sc1[0:1, 2:3], dra["armax_out"], adv=False,
                    waits=[p.glob["armax_tok"]])
    t_s1 = p.op("s", "activation", sc1[0:1, 3:4], sc1[0:1, 0:1], AF.Sigmoid,
                waits=[t_am], inc=ss)
    t_s2 = p.op("v", "tensor_tensor", sc1[0:1, 4:5], sc1[0:1, 3:4], sc1[0:1, 2:3],
                OP.mult, waits=[(ss, t_s1)], inc=sv)
    t_s3 = p.op("v", "reciprocal", sc1[0:1, 5:6], sc1[0:1, 4:5], waits=[(sv, t_s2)], inc=sv)
    t_s4 = p.op("v", "tensor_scalar", sc1[0:1, 5:6], sc1[0:1, 5:6], 127.0, None, OP.mult,
                waits=[(sv, t_s3)], inc=sv)
    t_s5 = p.op("v", "tensor_scalar", sc1[0:1, 4:5], sc1[0:1, 4:5], 1.0 / 127.0, None,
                OP.mult, waits=[(sv, t_s4)], inc=sv)
    t_bc = p.op("t", "matmul", pb[3][:, 0:2],
                onesf[0:1, 0:128], sc1[0:1, 4:6], start=True, stop=True,
                waits=[(sv, t_s5), (sv, t_onesf), p.glob["attn_done"]], inc=sm)
    t_bce = p.op("s", "copy", scb[:, 0:2], pb[3][:, 0:2], waits=[(sm, t_bc)], inc=ss)

    wrow = [p.sb("fwr0", [128, max(D, DI)], F32), p.sb("fwr1", [128, max(D, DI)], F32)]
    wi32 = p.sb("fwi", [128, max(D, DI)], I32)
    wtmp = p.sb("fwt", [128, max(D, DI)], F32)
    wqrow = p.sb("fwq", [128, max(D, DI)], BF16)
    rmax = p.sb("frm", [128, 4], F32)
    wst = p.stream("fwl", 2)
    with p.scope():
        xq = p.sb("xq", [128, FT * T], BF16)
        xb0 = [p.sb(f"xb0_{j}", [128, T], BF16) for j in range(2)]
        xi32 = p.sb("xi32", [128, T], I32)
        xf1 = p.sb("xf1", [128, T], F32)
        xf2 = p.sb("xf2", [128, T], F32)
        xst = p.stream("fxl", 2)
        for ft in range(FT):
            t_lx = xst.load(xb0[ft % 2][:], dra["xn0_sp"][ft],
                            waits=[(sv, tok.get(("xq", ft - 2), 0))] if ft >= 2 else [])
            t1 = p.op("v", "tensor_scalar", xf2[:], xb0[ft % 2][:], scb[:, 1:2], 512.0, OP.mult, OP.add,
                      waits=[t_lx, (ss, t_bce), (sv, tok.get(("xq", ft - 1), 0))], inc=sv)
            t2 = p.op("v", "tensor_copy", xi32[:], xf2[:], waits=[(sv, t1)], inc=sv)
            t3 = p.op("v", "tensor_copy", xf1[:], xi32[:], waits=[(sv, t2)], inc=sv)
            t4 = p.op("v", "tensor_tensor", xf2[:], xf2[:], xf1[:], OP.subtract,
                      waits=[(sv, t3)], inc=sv)
            t5 = p.op("v", "scalar_tensor_tensor", xf1[:], xf2[:], 0.5, xf1[:], OP.is_ge,
                      OP.add, waits=[(sv, t4)], inc=sv)
            t6 = p.op("v", "tensor_scalar", xf1[:], xf1[:], -512.0, -128.0, OP.add, OP.max,
                      waits=[(sv, t5)], inc=sv)
            t7 = p.op("v", "tensor_scalar", xq[:, ft * T:(ft + 1) * T], xf1[:], 127.0, None,
                      OP.min, waits=[(sv, t6)], inc=sv)
            tok[("xq", ft)] = t7
        t_xq = tok[("xq", FT - 1)]


        def quant_row_tile(buf, ncols, t_lw, scale_col, gate_prev=None):
            tq0 = p.op("v", "reduce_max", rmax[:, scale_col:scale_col + 1], buf[:, 0:ncols],
                       axis=AX.X, apply_absolute_value=True,
                       waits=[t_lw], inc=sv)
            tq1 = p.op("v", "reciprocal", rmax[:, 3:4], rmax[:, scale_col:scale_col + 1],
                       waits=[(sv, tq0)], inc=sv)
            tq2 = p.op("v", "tensor_scalar", rmax[:, 3:4], rmax[:, 3:4], 127.0, None, OP.mult,
                       waits=[(sv, tq1)], inc=sv)
            tq3 = p.op("v", "tensor_scalar", wtmp[:, 0:ncols], buf[:, 0:ncols], rmax[:, 3:4],
                       512.0, OP.mult, OP.add, waits=[(sv, tq2)], inc=sv)
            tq4 = p.op("v", "tensor_copy", wi32[:, 0:ncols], wtmp[:, 0:ncols],
                       waits=[(sv, tq3)], inc=sv)
            tq5 = p.op("v", "tensor_copy", buf[:, 0:ncols], wi32[:, 0:ncols],
                       waits=[(sv, tq4)], inc=sv)
            tq6 = p.op("v", "tensor_tensor", wtmp[:, 0:ncols], wtmp[:, 0:ncols], buf[:, 0:ncols],
                       OP.subtract, waits=[(sv, tq5)], inc=sv)
            tq7 = p.op("v", "scalar_tensor_tensor", buf[:, 0:ncols], wtmp[:, 0:ncols], 0.5,
                       buf[:, 0:ncols], OP.is_ge, OP.add, waits=[(sv, tq6)], inc=sv)
            tq8 = p.op("v", "tensor_scalar", wqrow[:, 0:ncols], buf[:, 0:ncols], -512.0, None,
                       OP.add, waits=[(sv, tq7), (sm, tok.get("wqrow_free", 0))], inc=sv)
            return tq8

        def transpose_to_arr(dst, ncolt, t_q, arr_guard=None, dst_off=0):
            t_e = tok.get("pbT", 0)
            for jb in range(0, ncolt, 4):
                nb = min(4, ncolt - jb)
                t_tr = 0
                for j in range(nb):
                    t_tr = p.op("t", "transpose", pbT[:, j * 128:(j + 1) * 128],
                                wqrow[:, (jb + j) * 128:(jb + j + 1) * 128], identb[:],
                                waits=[(sv, t_q), (ss, t_e)], inc=sm)
                t_e = p.op("s", "copy", dst[:, dst_off + jb * 128:dst_off + (jb + nb) * 128],
                           pbT[:, 0:nb * 128],
                           waits=[(sm, t_tr)] + ([arr_guard] if (arr_guard and jb == 0) else []),
                           inc=ss)
                tok["wqrow_free"] = t_tr
            tok["pbT"] = t_e
            return t_e

        wqg2 = [p.sb(f"wqg{j}", [128, FT * 128], BF16) for j in range(2)]
        wqu2 = [p.sb(f"wqu{j}", [128, FT * 128], BF16) for j in range(2)]
        sxg = p.sb("sxg", [128, 2], F32)
        hsb2 = [p.sb(f"hsb{j}", [128, 512], BF16) for j in range(2)]
        sgb2 = [p.sb(f"sgb{j}", [128, 512], BF16) for j in range(2)]
        hmx = p.sb("hmx", [128, IT + 2], F32)
        hst = p.stream("fhs", 2)
        for it in range(IT):
            t_lg = wst.load(wrow[0][:, 0:D], io["w_gate_r"][it * 128:(it + 1) * 128, :],
                            waits=[(sv, tok.get(("gq", it - 1), 0))])
            t_lu = wst.load(wrow[1][:, 0:D], io["w_up_r"][it * 128:(it + 1) * 128, :],
                            waits=[(sv, tok.get(("uq", it - 1), 0))])
            wqg = wqg2[it % 2]
            wqu = wqu2[it % 2]
            tqg = quant_row_tile(wrow[0], D, t_lg, 0)
            tok[("gq", it)] = tqg
            teg = transpose_to_arr(wqg, FT, tqg,
                                   (sm, tok.get(("gmm", it - 2), 0)) if it >= 2 else None)
            ts1 = p.op("v", "tensor_scalar", sxg[:, 0:1], rmax[:, 0:1], 1.0 / 127.0, None,
                       OP.mult, waits=[(sv, tqg)], inc=sv)
            ts2 = p.op("v", "tensor_tensor", sxg[:, 0:1], sxg[:, 0:1], scb[:, 0:1], OP.mult,
                       waits=[(sv, ts1), (ss, t_bce)], inc=sv)
            tqu = quant_row_tile(wrow[1], D, t_lu, 1)
            tok[("uq", it)] = tqu
            teu = transpose_to_arr(wqu, FT, tqu,
                                   (sm, tok.get(("umm", it - 2), 0)) if it >= 2 else None)
            ts3 = p.op("v", "tensor_scalar", sxg[:, 1:2], rmax[:, 1:2], 1.0 / 127.0, None,
                       OP.mult, waits=[(sv, tqu)], inc=sv)
            ts4 = p.op("v", "tensor_tensor", sxg[:, 1:2], sxg[:, 1:2], scb[:, 0:1], OP.mult,
                       waits=[(sv, ts3)], inc=sv)

            for tc in range(NTC):
                t_mmg = 0
                for ci in range(FT):
                    t_mmg = p.op("t", "matmul", pb[0][:, 0:512],
                                 wqg[:, ci * 128:(ci + 1) * 128],
                                 xq[:, ci * T + tc * 512:ci * T + tc * 512 + 512],
                                 start=(ci == 0), stop=(ci == FT - 1),
                                 waits=[(ss, teg), (sv, t_xq), (sv, tok.get(("hev", it, tc - 1), tok.get(("hev", it - 1, NTC - 1), 0)))],
                                 inc=sm)
                t_mmu = 0
                for ci in range(FT):
                    t_mmu = p.op("t", "matmul", pb[1][:, 0:512],
                                 wqu[:, ci * 128:(ci + 1) * 128],
                                 xq[:, ci * T + tc * 512:ci * T + tc * 512 + 512],
                                 start=(ci == 0), stop=(ci == FT - 1),
                                 waits=[(ss, teu), (sv, t_xq), (sv, tok.get(("hev", it, tc - 1), tok.get(("hev", it - 1, NTC - 1), 0)))],
                                 inc=sm)
                par = (it * NTC + tc) % 2
                hsb, sgb = hsb2[par], sgb2[par]
                t_sg = p.op("s", "activation", sgb[:], pb[0][:, 0:512], AF.Silu,
                            scale=sxg[:, 0:1],
                            waits=[(sm, t_mmg), (sv, ts2),
                                   tok.get(("hq2", it * NTC + tc - 2), (sv, 0))], inc=ss)
                t_h = p.op("v", "scalar_tensor_tensor", hsb[:], pb[1][:, 0:512], sxg[:, 1:2],
                           sgb[:], OP.mult, OP.mult,
                           waits=[(sm, t_mmu), (ss, t_sg), (sv, ts4)] +
                                 ([tok[("hst", (it * NTC + tc - 2) // NTC, (it * NTC + tc - 2) % NTC)]]
                                  if it * NTC + tc >= 2 else []), inc=sv)
                t_hm = p.op("v", "reduce_max", hmx[:, it:it + 1], hsb[:], axis=AX.X,
                            apply_absolute_value=True, waits=[(sv, t_h)],
                            inc=sv) if tc == 0 else p.op(
                    "v", "tensor_tensor_reduce" if False else "reduce_max",
                    hmx[:, IT:IT + 1], hsb[:], axis=AX.X, apply_absolute_value=True,
                    waits=[(sv, t_h)], inc=sv)
                if tc != 0:
                    t_hm = p.op("v", "tensor_tensor", hmx[:, it:it + 1], hmx[:, it:it + 1],
                                hmx[:, IT:IT + 1], OP.max, waits=[(sv, t_hm)], inc=sv)
                t_hs = hst.load(hra[it][:, tc * 512:(tc + 1) * 512], hsb[:], waits=[(sv, t_h)])
                tok[("hev", it, tc)] = t_h
                tok[("hst", it, tc)] = t_hs
                tok[("gmm", it)] = t_mmg
                tok[("umm", it)] = t_mmu
                tok[("hq2", it * NTC + tc)] = (sv, t_h)
        t_hm2 = p.op("v", "reduce_max", hmx[:, IT + 1:IT + 2], hmx[:, 0:IT], axis=AX.X,
                     apply_absolute_value=True, waits=[(sv, sv.n)], inc=sv)
        mxr2 = p.sb("mxr2", [1, 129], F32)
        ttr2 = p.op("t", "transpose", pb[2][0:1, 0:128], hmx[:, IT + 1:IT + 2],
                    p.glob["ident"][:], waits=[(sv, t_hm2)], inc=sm)
        tev2 = p.op("s", "copy", mxr2[0:1, 0:128], pb[2][0:1, 0:128], waits=[(sm, ttr2)], inc=ss)
        tpr2 = p.op("v", "reduce_max", mxr2[0:1, 128:129], mxr2[0:1, 0:128], axis=AX.X,
                    waits=[(ss, tev2)], inc=sv)
        tdm2 = p.dma("g", ar2_in, mxr2[0:1, 128:129], waits=[(sv, tpr2)])
        tcc2 = p.op("g", "collective_compute", "AllReduce", OP.max, replica_groups=RG_ALL,
                    ins=[ar2_in], outs=[ar2_out], waits=[(p.gld, tdm2)], inc=sg)
        t_h1 = cst.load(sc1[0:1, 6:7], ar2_out, adv=False, waits=[(sg, tcc2)])
        t_h2 = p.op("s", "activation", sc1[0:1, 1:2], sc1[0:1, 1:2], AF.Sigmoid,
                    waits=[t_h1], inc=ss)
        t_h3 = p.op("v", "tensor_tensor", sc1[0:1, 6:7], sc1[0:1, 6:7], sc1[0:1, 1:2],
                    OP.mult, waits=[(ss, t_h2)], inc=sv)
        t_h4 = p.op("v", "reciprocal", sc1[0:1, 7:8], sc1[0:1, 6:7], waits=[(sv, t_h3)], inc=sv)
        t_h5 = p.op("v", "tensor_scalar", sc1[0:1, 7:8], sc1[0:1, 7:8], 127.0, None, OP.mult,
                    waits=[(sv, t_h4)], inc=sv)
        t_h6 = p.op("v", "tensor_scalar", sc1[0:1, 6:7], sc1[0:1, 6:7], 1.0 / 127.0, None,
                    OP.mult, waits=[(sv, t_h5)], inc=sv)
        t_b2 = p.op("t", "matmul", pb[3][:, 2:4], onesf[0:1, 0:128], sc1[0:1, 6:8],
                    start=True, stop=True, waits=[(sv, t_h6), (sv, t_onesf), (ss, t_bce)], inc=sm)
        t_b2e = p.op("s", "copy", scb[:, 2:4], pb[3][:, 2:4], waits=[(sm, t_b2)], inc=ss)

        hq_sp = nc.dram_tensor("hq_sp", [IT, 128, T], BF16, kind="Internal")
        hqa = hq_sp.ap()
        hq2 = [p.sb(f"fhq{j}", [128, T], BF16) for j in range(2)]
        hf1 = xf1[:, 0:T]
        hf2 = xf2[:, 0:T]
        hqi = xi32[:, 0:T]
        hlst = p.stream("fhl", 2)
        hqst = p.stream("fhq", 2)
        for it in range(IT):
            t_lh = hlst.load(hq2[it % 2][:], hra[it][:, 0:T],
                             waits=[tok[("hst", it, NTC - 1)],
                                    tok.get(("hqs", it - 2), (sv, 0)),
                                    (sv, tok.get(("hqq", it - 2), 0))])
            q1 = p.op("v", "tensor_scalar", hf2, hq2[it % 2][:], scb[:, 3:4], 512.0,
                      OP.mult, OP.add, waits=[t_lh, (ss, t_b2e)], inc=sv)
            q2 = p.op("v", "tensor_copy", hqi, hf2, waits=[(sv, q1)], inc=sv)
            q3 = p.op("v", "tensor_copy", hf1, hqi, waits=[(sv, q2)], inc=sv)
            q4 = p.op("v", "tensor_tensor", hf2, hf2, hf1, OP.subtract,
                      waits=[(sv, q3)], inc=sv)
            q5 = p.op("v", "scalar_tensor_tensor", hf1, hf2, 0.5, hf1,
                      OP.is_ge, OP.add, waits=[(sv, q4)], inc=sv)
            q6 = p.op("v", "tensor_scalar", hf1, hf1, -512.0, -128.0, OP.add,
                      OP.max, waits=[(sv, q5)], inc=sv)
            q7 = p.op("v", "tensor_scalar", hq2[it % 2][:], hf1, 127.0, None, OP.min,
                      waits=[(sv, q6), tok.get(("hqs", it - 2), (sv, 0))], inc=sv)
            tok[("hqq", it)] = q7
            tok[("hqs", it)] = hqst.load(hqa[it][:, 0:T], hq2[it % 2][:],
                                         waits=[(sv, q7)])
        hq_done = hqst.done()

    wqd = p.sb("wqd", [128, FT * IT * 128], BF16)
    sdh = p.sb("sdh", [128, 8], F32)
    for co in range(FT):
        wg0 = ([(sv, tok[("gq", IT - 1)])] if co == 0 else
               [(sv, tok[("uq", IT - 1)])] if co == 1 else [])
        t_ld = wst.load(wrow[co % 2][:, 0:DI], io["w_down_r"][co * 128:(co + 1) * 128, :],
                        waits=[(sv, tok.get(("dq", co - 2), 0))] + wg0)
        tqd = quant_row_tile(wrow[co % 2], DI, t_ld, 0)
        tok[("dq", co)] = tqd
        ted = transpose_to_arr(wqd, IT, tqd, dst_off=co * IT * 128)
        td1 = p.op("v", "tensor_scalar", sdh[:, co:co + 1], rmax[:, 0:1], 1.0 / 127.0,
                   None, OP.mult, waits=[(sv, tqd)], inc=sv)
        td2 = p.op("v", "tensor_tensor", sdh[:, co:co + 1], sdh[:, co:co + 1], scb[:, 2:3],
                   OP.mult, waits=[(sv, td1), (ss, t_b2e)], inc=sv)
        tok[("dsc", co)] = td2
        tok[("dte", co)] = ted
    hrd = [p.sb("fhr0", [128, IT * 512], BF16)]
    hrst = p.stream("fhr", 2)
    for tc in range(NTC):
        wfree = ([(sm, tok[("dmm", tc - 1, FT - 1)])] if tc >= 1 else [])
        t_lq = 0
        for it in range(IT):
            t_lq = hrst.load(hrd[0][:, it * 512:(it + 1) * 512],
                             hqa[it][:, tc * 512:(tc + 1) * 512],
                             waits=(wfree if it == 0 else [])
                             + (hq_done if (tc == 0 and it == 0) else []),
                             adv=(it == IT - 1))
        for co in range(FT):
            bank = co % 4
            t_mm = 0
            for it in range(IT):
                t_mm = p.op("t", "matmul", pb[bank][:, 0:512],
                            wqd[:, co * IT * 128 + it * 128:co * IT * 128 + (it + 1) * 128],
                            hrd[0][:, it * 512:(it + 1) * 512],
                            start=(it == 0), stop=(it == IT - 1),
                            waits=[(ss, tok[("dte", co)]), t_lq]
                            + ([(sv, tok.get(("dacc", tc, co - 4),
                                             tok.get(("dacc", tc - 1, co + 4), 0)))]
                               if it == 0 else []),
                            inc=sm)
            tok[("dmm", tc, co)] = t_mm
            t_ac = p.op("v", "scalar_tensor_tensor", acc[co][:, tc * 512:(tc + 1) * 512],
                        pb[bank][:, 0:512], sdh[:, co:co + 1],
                        acc[co][:, tc * 512:(tc + 1) * 512],
                        OP.mult, OP.add,
                        waits=[(sm, t_mm), (sv, tok[("dsc", co)]), p.glob["attn_done"]],
                        inc=sv)
            tok[("dacc", tc, co)] = t_ac
    p.glob["ffn0_done"] = (sv, sv.n)


def phase_l1pre(p, io, dbg):
    nc, T = p.nc, p.T
    FT = 8
    NTC = T // 512
    acc, pb, pbT = p.glob["acc"], p.glob["pb"], p.glob["pbT"]
    sv, ss, sm, sg = p.sv, p.ss, p.sm, p.sg
    identb, ident = p.glob["identb"], p.glob["ident"]
    onesb = p.glob["onesb"]
    t_ones = p.glob["t_ones"]
    hmv = p.glob["hmv"]
    tok = {}

    xc_sp = nc.dram_tensor("xc_int", [16, 128, T], BF16, kind="Internal")
    xca = xc_sp.ap()
    if dbg:
        xc_dbg = nc.dram_tensor("xc_sp", [16, 128, T], BF16, kind="ExternalOutput")
    xc_ag = nc.dram_tensor("xc_ag", [2, 16, 128, T], BF16, kind="Internal")
    res_sp = nc.dram_tensor("res_sp", [16, 128, T], BF16,
                            kind="ExternalOutput" if dbg else "Internal")
    resa = res_sp.ap()
    p.glob["res_sp"] = res_sp
    x13_snd_h = nc.dram_tensor("x13_snd", [3, D], F32, kind="Internal")
    x13_snd = x13_snd_h.ap()
    x13_rcv = nc.dram_tensor("x13_rcv", [2, 3, D], F32, kind="Internal").ap()
    p.glob["xc_ag"] = xc_ag
    p.glob["xc_sp"] = xc_sp
    if dbg:
        dbg_xn1 = nc.dram_tensor("dbg_xn1", [FT, 128, T], BF16, kind="ExternalOutput").ap()
        dbg_xm = nc.dram_tensor("dbg_xm", [FT, 128, T], BF16, kind="ExternalOutput").ap()
    ffn0_done = p.glob["ffn0_done"]

    hst = p.stream("l1h", 1)
    h3s = p.stream("l1h3", 1)
    dbx = p.stream("l1dx", 1)
    dbm = p.stream("l1dm", 1)
    for ft in range(FT):
        p.op("g", "dma_start", bass.AP(x13_snd_h, ft * 128, [[1, 128], [D, 3]]),
             bass.AP(acc[ft], T - 3, [[T, 128], [1, 3]]),
             waits=[ffn0_done] if ft == 0 else (), inc=h3s.sems[0], inck=16,
             allow_slow_non_contiguous=True)
    t_halo = (h3s.sems[0], h3s.sems[0].n)
    t_cc3 = p.op("g", "collective_compute", "AllGather", OP.bypass, replica_groups=RG_PAIRS,
                 ins=[x13_snd], outs=[x13_rcv], waits=[t_halo], inc=sg)
    xh = p.sb("xh", [3, D], F32)
    xhs = p.stream("l1xh", 1)
    t_lh3 = xhs.load(xh[:], x13_rcv[0], eng="g", waits=[(sg, t_cc3)], adv=False)

    xm = [p.sb(f"xm_{i}", [128, T], BF16) for i in range(FT)]
    wwin = p.sb("l1win", [128, FT * NH], BF16)
    wwout = p.sb("l1wo", [NH, D], BF16)
    binv = p.sb("l1bin", [NH, 1], F32)
    b2v = p.sb("l1b2", [NH, 1], F32)
    boutv = p.sb("l1bo", [128, 8], F32)
    nwv = p.sb("l1nw", [128, 8], F32)
    w1f = p.sb("l1w1f", [NH, 16], F32)
    w2f = p.sb("l1w2f", [NH, 16], F32)
    b1f = p.sb("l1b1f", [NH, 16], F32)
    W1b = p.sb("l1w1", [NH, 16], BF16)
    W2b = p.sb("l1w2", [NH, 16], BF16)
    b1b = p.sb("l1b1", [NH, 16], BF16)
    cwv = p.sb("l1cw", [128, 16 * 4], F32)
    cbv = p.sb("l1cb", [128, 16], F32)
    vst = p.stream("l1v", 1)
    for ci in range(FT):
        vst.load(wwin[:, ci * NH:(ci + 1) * NH],
                 io["w_win"][ci * 128:(ci + 1) * 128, :], adv=False)
    vst.load(wwout[:], io["w_wout"], adv=False)
    vst.load(binv[:], io["bin_"], adv=False)
    vst.load(b2v[:], io["b2"], adv=False)
    vst.load(boutv[:], io["bout"], adv=False)
    vst.load(nwv[:], io["norm_w"], adv=False)
    vst.load(w1f[:], io["W1"], adv=False)
    vst.load(w2f[:], io["W2"], adv=False)
    vst.load(b1f[:], io["b1"], adv=False)
    for ct in range(16):
        vst.load(cwv[:, ct * 4:(ct + 1) * 4], io["conv_w"][ct], adv=False)
    t_vl = vst.load(cbv[:], io["conv_b"], adv=False)
    t_c1 = p.op("v", "tensor_copy", W1b[:], w1f[:], waits=[t_vl], inc=sv)
    t_c2 = p.op("v", "tensor_copy", W2b[:], w2f[:], inc=sv)
    t_cv = p.op("v", "tensor_copy", b1b[:], b1f[:], inc=sv)

    xn1 = [p.sb(f"xn1_{i}", [128, 512], BF16) for i in range(FT)]
    xbb = p.sb("l1xb", [128, 512], BF16)
    sqb = p.sb("l1sq", [128, 512], BF16)
    meanb = p.sb("l1mn", [128, 512], F32)
    rstdb = p.sb("l1rs", [128, 512], F32)
    vtmp = p.sb("l1vt", [128, 512], F32)
    rrb = p.sb("l1rr", [128, 512], F32)
    sffb = p.sb("l1sf", [NH, 512], BF16)
    hfb = p.sb("l1hf", [NH, 512 * 16], BF16)
    g1b = p.sb("l1g1", [NH, 512], BF16)
    g1f = p.sb("l1g1f", [NH, 512], F32)
    for tc in range(NTC):
        c0 = tc * 512
        gprev = (sv, tok.get(("xmd", tc - 1), 0))
        t_mm1 = 0
        t_mm2 = 0
        t_sq = 0
        for ft in range(FT):
            t_xb = p.op("v", "tensor_copy", xbb[:], acc[ft][:, c0:c0 + 512],
                        waits=[ffn0_done, (sm, t_mm1), (ss, t_sq), gprev], inc=sv)
            t_sq = p.op("s", "activation", sqb[:], xbb[:], AF.Square,
                        waits=[(sv, t_xb), (sm, t_mm2)], inc=ss)
            t_mm1 = p.op("t", "matmul", pb[4][:, 0:512], onesb[:], xbb[:],
                         start=(ft == 0), stop=(ft == FT - 1),
                         waits=[(sv, t_xb), (sv, t_ones), (ss, tok.get(("st1", tc - 1), 0))],
                         inc=sm)
            t_mm2 = p.op("t", "matmul", pb[5][:, 0:512], onesb[:], sqb[:],
                         start=(ft == 0), stop=(ft == FT - 1),
                         waits=[(ss, t_sq), (sv, tok.get(("st2", tc - 1), 0))], inc=sm)
        t_s1 = p.op("s", "activation", meanb[:], pb[4][:, 0:512], AF.Copy,
                    scale=1.0 / D, waits=[(sm, t_mm1), (sv, tok.get(("mnu", tc - 1), 0))],
                    inc=ss)
        tok[("st1", tc)] = t_s1
        t_s2 = p.op("s", "activation", vtmp[:], meanb[:], AF.Square,
                    waits=[(ss, t_s1)], inc=ss)
        t_s3 = p.op("v", "scalar_tensor_tensor", rstdb[:], pb[5][:, 0:512], 1.0 / D,
                    vtmp[:], OP.mult, OP.subtract, waits=[(ss, t_s2), (sm, t_mm2)], inc=sv)
        tok[("st2", tc)] = t_s3
        t_s4 = p.op("s", "activation", rstdb[:], rstdb[:], AF.Sqrt, bias=p.const(1e-5),
                    waits=[(sv, t_s3)], inc=ss)
        t_s5 = p.op("v", "reciprocal", rstdb[:], rstdb[:], waits=[(ss, t_s4)], inc=sv)
        t_x1 = 0
        for ft in range(FT):
            t_d = p.op("v", "tensor_tensor", vtmp[:], acc[ft][:, c0:c0 + 512], meanb[:],
                       OP.subtract, waits=[(sv, t_s5), (sv, t_x1), (ss, 0 if ft else t_s2)],
                       inc=sv)
            t_x1 = p.op("v", "tensor_tensor", xn1[ft][:], vtmp[:], rstdb[:], OP.mult,
                        waits=[(sv, t_d)] + ([tok[("dbx", tc - 1)]] if (dbg and tc and ft == 0)
                                             else []), inc=sv)
            tok[("xn1t", tc, ft)] = t_x1
        tok[("mnu", tc)] = t_x1
        if dbg:
            for ft in range(FT):
                tok[("dbx", tc)] = dbx.load(dbg_xn1[ft][:, c0:c0 + 512], xn1[ft][:], eng="g",
                                            waits=[(sv, tok[("xn1t", tc, ft)])], adv=False)
        t_mm3 = 0
        for ft in range(FT):
            t_sq2 = p.op("s", "activation", sqb[:], xn1[ft][:], AF.Square,
                         waits=[(sv, tok[("xn1t", tc, ft)]), (sm, t_mm3)], inc=ss)
            t_mm3 = p.op("t", "matmul", pb[6][:, 0:512], onesb[:], sqb[:],
                         start=(ft == 0), stop=(ft == FT - 1),
                         waits=[(ss, t_sq2), (sv, tok.get(("rru", tc - 1), 0))], inc=sm)
        t_r1 = p.op("s", "activation", rrb[:], pb[6][:, 0:512], AF.Sqrt,
                    bias=p.const(1e-6), scale=1.0 / D, waits=[(sm, t_mm3)], inc=ss)
        t_r2 = p.op("v", "reciprocal", rrb[:], rrb[:], waits=[(ss, t_r1)], inc=sv)
        t_xm = 0
        for ft in range(FT):
            t_m1 = p.op("v", "tensor_tensor", vtmp[:], xn1[ft][:], rrb[:], OP.mult,
                        waits=[(sv, t_r2), (sv, t_xm)], inc=sv)
            t_xm = p.op("v", "tensor_scalar", xm[ft][:, c0:c0 + 512], vtmp[:],
                        nwv[:, ft:ft + 1], None, OP.mult, waits=[(sv, t_m1), t_vl], inc=sv)
            tok[("xmt", tc, ft)] = t_xm
        tok[("rru", tc)] = t_xm
        tok[("xmd", tc)] = t_xm
        if dbg:
            for ft in range(FT):
                tok[("dbm", tc)] = dbm.load(dbg_xm[ft][:, c0:c0 + 512], xm[ft][:, c0:c0 + 512],
                                            eng="g", waits=[(sv, tok[("xmt", tc, ft)])], adv=False)
        t_sm1 = 0
        for ci in range(FT):
            t_sm1 = p.op("t", "matmul", pb[0][0:NH, 0:512],
                         wwin[:, ci * NH:(ci + 1) * NH], xn1[ci][:],
                         start=(ci == 0), stop=(ci == FT - 1),
                         waits=[(sv, tok[("xn1t", tc, ci)]), t_vl,
                                (sv, tok.get(("sevac", tc - 1), 0))], inc=sm)
        t_se = p.op("v", "tensor_scalar", sffb[:], pb[0][0:NH, 0:512], binv[:, 0:1], None,
                    OP.add, waits=[(sm, t_sm1), t_vl, (sv, tok.get(("hfu", tc - 1), 0))],
                    inc=sv)
        tok[("sevac", tc)] = t_se
        t_h1 = p.op("v", "tensor_tensor", hfb[:],
                    bass.AP(sffb, 0, [[512, NH], [1, 512], [0, 16]]),
                    bass.AP(W1b, 0, [[16, NH], [0, 512], [1, 16]]), OP.mult,
                    waits=[(sv, t_se), (sv, t_c1), (sv, tok.get(("g1u", tc - 1), 0))], inc=sv)
        t_h2 = p.op("v", "tensor_tensor", hfb[:], hfb[:],
                    bass.AP(b1b, 0, [[16, NH], [0, 512], [1, 16]]), OP.add,
                    waits=[(sv, t_h1)], inc=sv)
        t_h3 = p.op("v", "tensor_relu", hfb[:], hfb[:], waits=[(sv, t_h2)], inc=sv)
        t_g1 = p.op("v", "tensor_tensor", hfb[:], hfb[:],
                    bass.AP(W2b, 0, [[16, NH], [0, 512], [1, 16]]), OP.mult,
                    waits=[(sv, t_h3)], inc=sv)
        tok[("hfu", tc)] = t_g1
        t_g2 = p.op("v", "reduce_sum", g1f[:], bass.AP(hfb, 0, [[8192, NH], [16, 512], [1, 16]]),
                    axis=AX.X, waits=[(sv, t_g1), (sv, tok.get(("g1fu", tc - 1), 0))], inc=sv)
        t_g3 = p.op("v", "tensor_scalar", g1b[:], g1f[:], b2v[:, 0:1], None, OP.add,
                    waits=[(sv, t_g2), t_vl], inc=sv)
        tok[("g1fu", tc)] = t_g3
        tok[("g1u", tc)] = t_g3
        for ft in range(FT):
            t_sk = p.op("v", "tensor_tensor", acc[ft][:, c0:c0 + 512],
                        acc[ft][:, c0:c0 + 512], xn1[ft][:], OP.add,
                        waits=[(sv, tok[("xmt", tc, ft)]), (sm, t_mm3), (sm, t_sm1)]
                        + ([t_halo] if tc == NTC - 1 else []), inc=sv)
            tok[("skip", tc, ft)] = t_sk
        for co in range(FT):
            t_om = p.op("t", "matmul", pb[1][:, 0:512], wwout[0:NH, co * 128:(co + 1) * 128],
                        g1b[:], start=True, stop=True,
                        waits=[(sv, t_g3), t_vl, (sv, tok.get(("oacc", tc, co - 1),
                                                              tok.get(("oacc", tc - 1, FT - 1), 0)))],
                        inc=sm)
            t_oa = p.op("v", "scalar_tensor_tensor", acc[co][:, c0:c0 + 512],
                        pb[1][:, 0:512], boutv[:, co:co + 1], acc[co][:, c0:c0 + 512],
                        OP.add, OP.add, waits=[(sm, t_om), (sv, tok[("skip", tc, co)]), t_vl],
                        inc=sv)
            tok[("oacc", tc, co)] = t_oa
    p.glob["l1acc_done"] = (sv, sv.n)

    st3 = p.sb("l1s3", [3, 8], F32)
    sq3 = p.sb("l1q3", [3, D], F32)
    xmh = p.sb("l1xmh", [128, FT * 4], BF16)
    a_ = p.op("v", "reduce_sum", st3[:, 0:1], xh[:], axis=AX.X, waits=[t_lh3], inc=sv)
    b_ = p.op("s", "activation", sq3[:], xh[:], AF.Square, waits=[t_lh3], inc=ss)
    c_ = p.op("v", "reduce_sum", st3[:, 1:2], sq3[:], axis=AX.X, waits=[(ss, b_)], inc=sv)
    d_ = p.op("v", "tensor_scalar", st3[:, 2:3], st3[:, 0:1], 1.0 / D, None, OP.mult,
              waits=[(sv, c_)], inc=sv)
    e_ = p.op("s", "activation", st3[:, 3:4], st3[:, 2:3], AF.Square, waits=[(sv, d_)], inc=ss)
    f_ = p.op("v", "scalar_tensor_tensor", st3[:, 4:5], st3[:, 1:2], 1.0 / D, st3[:, 3:4],
              OP.mult, OP.subtract, waits=[(ss, e_)], inc=sv)
    g_ = p.op("s", "activation", st3[:, 5:6], st3[:, 4:5], AF.Sqrt, bias=p.const(1e-5, rows=3),
              waits=[(sv, f_)], inc=ss)
    h_ = p.op("v", "reciprocal", st3[:, 6:7], st3[:, 5:6], waits=[(ss, g_)], inc=sv)
    i_ = p.op("v", "tensor_scalar", xh[:], xh[:], st3[:, 2:3], st3[:, 6:7],
              OP.subtract, OP.mult, waits=[(sv, h_)], inc=sv)
    j_ = p.op("s", "activation", sq3[:], xh[:], AF.Square, waits=[(sv, i_)], inc=ss)
    k_ = p.op("v", "reduce_sum", st3[:, 1:2], sq3[:], axis=AX.X, waits=[(ss, j_)], inc=sv)
    l_ = p.op("s", "activation", st3[:, 5:6], st3[:, 1:2], AF.Sqrt, bias=p.const(1e-6, rows=3),
              scale=1.0 / D, waits=[(sv, k_)], inc=ss)
    m_ = p.op("v", "reciprocal", st3[:, 6:7], st3[:, 5:6], waits=[(ss, l_)], inc=sv)
    n_ = p.op("v", "tensor_scalar", xh[:], xh[:], st3[:, 6:7], None, OP.mult,
              waits=[(sv, m_)], inc=sv)
    t_pe = 0
    for ft in range(FT):
        t_tr = p.op("t", "transpose", pb[2][0:128, ft * 4:ft * 4 + 3],
                    xh[:, ft * 128:(ft + 1) * 128], ident[0:3, 0:3],
                    waits=[(sv, n_), (ss, t_pe) if t_pe else (sv, n_)], inc=sm)
        t_pe = p.op("s", "copy", xmh[:, ft * 4:ft * 4 + 3], pb[2][0:128, ft * 4:ft * 4 + 3],
                    waits=[(sm, t_tr)], inc=ss)
    t_xm1 = 0
    for ft in range(FT):
        t_xm1 = p.op("v", "tensor_scalar", xmh[:, ft * 4:ft * 4 + 3], xmh[:, ft * 4:ft * 4 + 3],
                     nwv[:, ft:ft + 1], None, OP.mult, waits=[(ss, t_pe), t_vl], inc=sv)
    t_xmh = 0
    for ft in range(FT):
        t_xmh = p.op("v", "tensor_scalar", xmh[:, ft * 4:ft * 4 + 3], xmh[:, ft * 4:ft * 4 + 3],
                     hmv[:], None, OP.mult, waits=[(sv, t_xm1)], inc=sv)

    wist = p.stream("l1wi", 2)
    xist = p.stream("l1xs", 2)
    wtile = [p.sb(f"l1wt{j}", [128, FT * 128], BF16) for j in range(2)]
    xib = [p.sb(f"l1xi{j}", [128, 3 + T], BF16) for j in range(2)]
    rstage = p.sb("l1rst", [128, 512], BF16)
    ctmp = p.sb("l1ct", [128, T], F32)
    xcst = p.sb("l1xc", [128, T], BF16)
    for ro in range(32):
        wb = wtile[ro % 2]
        t_lw = 0
        for ci in range(FT):
            t_lw = wist.load(wb[:, ci * 128:(ci + 1) * 128],
                             io["w_inproj"][ci * 128:(ci + 1) * 128, ro * 128:(ro + 1) * 128],
                             waits=[(sm, tok.get(("ipm", ro - 2), 0))] if ci == 0 else (),
                             adv=(ci == FT - 1))
        is_xi = ro < 16
        if is_xi:
            t_hm = 0
            for ci in range(FT):
                t_hm = p.op("t", "matmul", pb[2][:, 0:3], wb[:, ci * 128:(ci + 1) * 128],
                            xmh[:, ci * 4:ci * 4 + 3], start=(ci == 0), stop=(ci == FT - 1),
                            waits=[(sv, t_xmh), t_lw, (sv, tok.get(("xiev", ro - 2), 0)),
                                   (ss, t_pe)], inc=sm)
        t_evp = None
        for tc in range(NTC):
            t_mm = 0
            for ci in range(FT):
                w_mm = [(sv, tok[("xmt", tc, ci)]), t_lw]
                if ci == 0 and t_evp is not None:
                    w_mm.append(t_evp)
                if is_xi:
                    w_mm.append((sv, tok.get(("xiev", ro - 2), 0)))
                elif tok.get(("rsev", ro - 2)):
                    w_mm.append((ss, tok[("rsev", ro - 2)]))
                t_mm = p.op("t", "matmul", pb[0 if ro % 2 == 0 else 1][:, 0:512],
                            wb[:, ci * 128:(ci + 1) * 128],
                            xm[ci][:, tc * 512:(tc + 1) * 512],
                            start=(ci == 0), stop=(ci == FT - 1), waits=w_mm, inc=sm)
            if is_xi:
                t_ev = p.op("v", "tensor_copy", xib[ro % 2][:, 3 + tc * 512:3 + (tc + 1) * 512],
                            pb[0 if ro % 2 == 0 else 1][:, 0:512], waits=[(sm, t_mm)], inc=sv)
                t_evp = (sv, t_ev)
            else:
                t_ev = p.op("s", "activation", rstage[:], pb[0 if ro % 2 == 0 else 1][:, 0:512],
                            AF.Silu,
                            waits=[(sm, t_mm)] + ([tok[("rspl", ro, tc - 1)]] if tc else
                                                  ([tok[("rspl", ro - 1, NTC - 1)]] if ro > 16 else [])),
                            inc=ss)
                tok[("rsev", ro)] = t_ev
                t_evp = (ss, t_ev)
                tok[("rspl", ro, tc)] = xist.load(resa[ro - 16][:, tc * 512:(tc + 1) * 512],
                                                  rstage[:], waits=[(ss, t_ev)])
            tok[("ipm", ro)] = t_mm
        if is_xi:
            t_he = p.op("v", "tensor_copy", xib[ro % 2][:, 0:3], pb[2][:, 0:3],
                        waits=[(sm, t_hm)], inc=sv)
            ct = ro
            t_cv1 = p.op("v", "tensor_scalar", ctmp[:], xib[ro % 2][:, 0:T],
                         cwv[:, ct * 4:ct * 4 + 1], None, OP.mult,
                         waits=[(sv, t_ev), (sv, t_he), t_vl,
                                (ss, tok.get(("xcs", ro - 1), 0))], inc=sv)
            for kk in range(1, 4):
                t_cv1 = p.op("v", "scalar_tensor_tensor", ctmp[:],
                             xib[ro % 2][:, kk:kk + T], cwv[:, ct * 4 + kk:ct * 4 + kk + 1],
                             ctmp[:], OP.mult, OP.add, waits=[(sv, t_cv1)], inc=sv)
            t_si = p.op("s", "activation", xcst[:], ctmp[:], AF.Silu,
                        bias=cbv[:, ct:ct + 1],
                        waits=[(sv, t_cv1)] + ([tok[("xcspl", ct - 1)]] if ct else []),
                        inc=ss)
            tok[("xcs", ro)] = t_si
            tok[("xiev", ro)] = t_cv1
            tok[("xcspl", ct)] = xist.load(xca[ct], xcst[:], waits=[(ss, t_si)])
    spdone = xist.done()
    if dbg:
        p.op("y", "dma_start", xc_dbg.ap(), xc_sp.ap(), waits=spdone, inc=hst.sems[0], inck=16)
    t_cc4 = p.op("g", "collective_compute", "AllGather", OP.bypass, replica_groups=RG_PAIRS,
                 ins=[xc_sp.ap()], outs=[xc_ag.ap()], waits=spdone, inc=sg)
    p.glob["xc_ag_tok"] = (sg, t_cc4)
    p.glob["res_done"] = spdone


def phase_acc_spill(p, io):
    T = p.T
    nc = p.nc
    acc = p.glob["acc"]
    acc_sp = nc.dram_tensor("acc_sp", [8, 128, T], F32, kind="Internal")
    p.glob["acc_sp"] = acc_sp
    asp = p.stream("asp", 1)
    ad = p.glob["acc_done"]
    toks = []
    for ft in range(8):
        toks.append(asp.load(acc_sp.ap()[ft], acc[ft][:], waits=[ad] if ft == 0 else (),
                             adv=False))
    p.glob["acc_sp_tok"] = toks[-1]


def phase_mamba(p, io, dbg):
    T = p.T
    T2 = 2 * T
    NT2 = T2 // 512
    nc = p.nc
    sv, ss, sm, sg = p.sv, p.ss, p.sm, p.sg
    big = p.glob["big"]
    ofs = p.glob["ofs"]
    rg = p.glob["rg"]
    tok = {}

    dt_sp = nc.dram_tensor("dt_sp", [8, 128, T2], BF16, kind="Internal")
    y_sp = nc.dram_tensor("y_sp", [8, 128, T2], BF16, kind="Internal")
    p.glob["y_sp"] = y_sp
    xc_ag = p.glob["xc_ag"]
    t_xcag = p.glob["xc_ag_tok"]

    vst = p.stream("mvs", 1)
    ind16 = p.sb("mind", [128, 16], BF16)
    dtbv = p.sb("mdtb", [128, 8], F32)
    A_sb = p.sb("mA", [128, 512], F32)
    vst.load(ind16[:], io["ind16"], adv=False)
    vst.load(dtbv[:], io["dt_bh"], adv=False)
    vst.load(ofs[:], io["offs"], adv=False)
    t_va = vst.load(A_sb[:], io["A_log_h"], adv=False)
    p.glob["t_ofs"] = t_va
    t_ae = p.op("s", "activation", A_sb[:], A_sb[:], AF.Exp, waits=[t_va], inc=ss)
    t_an = p.op("v", "tensor_scalar", A_sb[:], A_sb[:], -1.0, None, OP.mult,
                waits=[(ss, t_ae)], inc=sv)

    B_sb = p.sb("mB", [NS, T2], BF16)
    with p.scope():
        wdt = p.sb("mwdt", [128, 16 * 1024], BF16)
        wxp = p.sb("mwxp", [128, 16 * NS], BF16)
        wst = p.stream("mws", 1)
        for ci in range(16):
            wst.load(wdt[:, ci * 1024:(ci + 1) * 1024],
                     io["w_dt_h"][ci * 128:(ci + 1) * 128, :], adv=False)
        for ci in range(16):
            t_w = wst.load(wxp[:, ci * NS:(ci + 1) * NS],
                           io["w_xproj"][ci * 128:(ci + 1) * 128, :], adv=False)
        xk = [p.sb(f"mxk{j}", [128, 16 * 512], BF16) for j in range(2)]
        dstg = [p.sb(f"mds{j}", [128, 512], F32) for j in range(4)]
        dbf = [p.sb(f"mdb{j}", [128, 512], BF16) for j in range(4)]
        sps = [p.sb(f"msp{j}", [128, 512], F32) for j in range(4)]
        bstg = [p.sb(f"mbs{j}", [NS, 512], F32) for j in range(2)]
        c1 = p.const(1.0)
        xkst = p.stream("mxk", 2)
        dsp = p.stream("mdsp", 2)
        for tch in range(NT2):
            slot, lc = tch // (NT2 // 2), (tch % (NT2 // 2)) * 512
            buf = tch % 2
            wx = [t_xcag] if tch == 0 else []
            if tch >= 2:
                wx.append((sm, tok[("bmm", tch - 2)]))
            for ci in range(16):
                t_xk = xkst.load(xk[buf][:, ci * 512:(ci + 1) * 512],
                                 bass.AP(xc_ag, slot * (16 * 128 * T) + ci * 128 * T + lc,
                                         [[T, 128], [1, 512]]),
                                 waits=wx if ci == 0 else (), adv=(ci == 15))
            t_mm = 0
            for co in range(8):
                for ki in range(16):
                    t_mm = p.op("t", "matmul", big[:, co * 512:(co + 1) * 512],
                                wdt[:, ki * 1024 + co * 128:ki * 1024 + co * 128 + 128],
                                xk[buf][:, ki * 512:(ki + 1) * 512],
                                start=(ki == 0), stop=(ki == 15),
                                waits=([t_xk, t_w] if (co == 0 and ki == 0) else [])
                                + (([(ss, tok.get(("dtev", tch - 1, co), 0))]
                                    + ([(sv, tok.get(("bev", tch - 1), 0))] if co == 0 else []))
                                   if ki == 0 else []), inc=sm)
                tok[("comm", tch, co)] = t_mm
            tok[("dtmm", tch)] = t_mm
            for cg in range(2):
                cos = range(cg * 4, cg * 4 + 4)
                te1, te3 = {}, {}
                for co in cos:
                    te1[co] = p.op("s", "activation", sps[co % 4][:],
                                   big[:, co * 512:(co + 1) * 512],
                                   AF.Abs, bias=dtbv[:, co:co + 1],
                                   waits=[(sm, tok[("comm", tch, co)]), t_va,
                                          (sv, tok.get(("dtadd", tch, co - 4),
                                                       tok.get(("dtadd", tch - 1, 4 + co), 0)))],
                                   inc=ss)
                for co in cos:
                    p.op("s", "activation", sps[co % 4][:], sps[co % 4][:], AF.Exp,
                         scale=-1.0, waits=[(ss, te1[co])], inc=ss)
                for co in cos:
                    te3[co] = p.op("s", "activation", sps[co % 4][:], sps[co % 4][:], AF.Ln,
                                   bias=c1, waits=[p._cinit], inc=ss)
                for co in cos:
                    wsp = [tok.get(("dtsp", tch, co - 4), tok.get(("dtsp", tch - 1, 4 + co),
                                                                  (sv, 0)))]
                    t_ev = p.op("s", "activation", dstg[co % 4][:],
                                big[:, co * 512:(co + 1) * 512],
                                AF.Relu, bias=dtbv[:, co:co + 1],
                                waits=[(ss, te3[co])] + wsp, inc=ss)
                    tok[("dtev", tch, co)] = t_ev
                for co in cos:
                    t_add = p.op("v", "tensor_tensor", dbf[co % 4][:], dstg[co % 4][:],
                                 sps[co % 4][:], OP.add,
                                 waits=[(ss, tok[("dtev", tch, co)])], inc=sv)
                    tok[("dtadd", tch, co)] = t_add
                    tok[("dtsp", tch, co)] = dsp.load(
                        bass.AP(dt_sp, co * 128 * T2 + tch * 512, [[T2, 128], [1, 512]]),
                        dbf[co % 4][:], waits=[(sv, t_add)])
            t_bm = 0
            for ki in range(16):
                t_bm = p.op("t", "matmul", big[0:NS, 0:512],
                            wxp[:, ki * NS:(ki + 1) * NS], xk[buf][:, ki * 512:(ki + 1) * 512],
                            start=(ki == 0), stop=(ki == 15),
                            waits=[(ss, tok[("dtev", tch, 0)])] if ki == 0 else [], inc=sm)
            t_be = p.op("v", "tensor_copy", bstg[buf][:], big[0:NS, 0:512],
                        waits=[(sm, t_bm), (sv, tok.get(("bcp", tch - 2), 0))], inc=sv)
            tok[("bev", tch)] = t_be
            t_bc = p.op("v", "tensor_copy", B_sb[:, tch * 512:(tch + 1) * 512], bstg[buf][:],
                        waits=[(sv, t_be)], inc=sv)
            tok[("bcp", tch)] = t_bc
            tok[("bmm", tch)] = t_bm
        t_ball = tok[("bcp", NT2 - 1)]
        dt_done = dsp.done()
        p.glob["m_dt_done"] = dt_done
        if dbg:
            dt_dbg = nc.dram_tensor("dt_dbg", [8, 128, T2], BF16, kind="ExternalOutput")
            p.op("y", "dma_start", dt_dbg.ap(), dt_sp.ap(), waits=dt_done, inc=p.ld, inck=16)
            B_dbg = nc.dram_tensor("B_dbg", [NS, T2], BF16, kind="ExternalOutput")
            p.op("y", "dma_start", B_dbg.ap(), B_sb[:], waits=[(sv, t_ball)],
                 inc=p.ld, inck=16)

    Brep = [p.sb(f"mBr{k}", [128, T2], BF16) for k in range(8)]
    brs = p.stream("mbrs", 1)
    for k in range(8):
        for c in range(16):
            t_br = brs.load(bass.AP(Brep[k], c * 8 * T2, [[T2, 8], [1, T2]]),
                            bass.AP(B_sb, 8 * k * T2, [[T2, 8], [1, T2]]),
                            waits=[(sv, t_ball)] if (k == 0 and c == 0) else (), adv=False)
    dt_done = p.glob["m_dt_done"]

    xcsm = p.sb("mxcs", [128, T2], BF16)
    xcrep = [p.sb(f"mxc{j}", [128, T2], BF16) for j in range(2)]
    dtrep = [p.sb(f"mdt{j}", [128, T2], BF16) for j in range(2)]
    af = [p.sb(f"maf{j}", [128, T2], F32) for j in range(2)]
    bxb = [p.sb(f"mbx{j}", [128, T2], BF16) for j in range(2)]
    ystg = [p.sb(f"mys{j}", [16, T2], BF16) for j in range(2)]
    mrd = p.stream("mrd", 2)
    mrd2 = p.stream("mrd2", 2)
    mrx = p.stream("mrx", 2)
    mrx2 = p.stream("mrx2", 2)
    mrg = p.stream("mrg", 1)
    ysp = p.stream("mysp", 2)
    zc = p.const(0.0)
    EV = T2 // 512

    def emit_yevac(cbe):
        bufe = cbe % 2
        cte, rbe = cbe // 8, cbe % 8
        for hh in range(2):
            t_ye = p.op("v", "tensor_copy", ystg[bufe][:, hh * (T2 // 2):(hh + 1) * (T2 // 2)],
                        big[0:16, hh * (T2 // 2):(hh + 1) * (T2 // 2)],
                        waits=[(sm, tok[("sum", cbe, 7, EV - 1)])]
                        + ([tok[("yspl", cbe - 2)]] if (hh == 0 and cbe >= 2) else []), inc=sv)
            tok[("yev", cbe, hh)] = t_ye
        tok[("yspl", cbe)] = ysp.load(
            bass.AP(y_sp, cte * 128 * T2 + 16 * rbe * T2, [[T2, 16], [1, T2]]),
            ystg[bufe][:], waits=[(sv, tok[("yev", cbe, 1)])])

    for cb in range(64):
        buf = cb % 2
        ct, rb = cb // 8, cb % 8
        if rb == 0:
            gfirst = ([t_va, t_xcag, t_br] if cb == 0 else [])
            gfree = [tok[("mlx", cb - 1)], tok[("mlx2", cb - 1)]] if cb else []
            p.op("g", "reg_load", rg, ofs[0:1, ct:ct + 1], waits=gfree + gfirst)
            tok[("xcs", ct)] = mrg.load(
                xcsm[:], bass.AP(xc_ag, rg, [[T, 128], [16 * 128 * T, 2], [1, T]]),
                eng="g", adv=False)
        yfirst = (dt_done + [(sv, t_an), tok[("xcs", 0)]]) if cb == 0 else []
        yfree = ([(ss, tok[("aexp", cb - 2, 7)]),
                  (sv, tok[("bx", cb - 2, 7)])] if cb >= 2 else [])
        dfree = [(ss, tok[("aexp", cb - 2, 7)])] if cb >= 2 else []
        xfree = [(sv, tok[("bx", cb - 2, 7)])] if cb >= 2 else []
        wxc = [tok[("xcs", ct)]] if rb == 0 else []
        t_d1 = mrd.load(dtrep[buf][0:64, :],
                        bass.AP(dt_sp, ct * 128 * T2 + 16 * rb * T2,
                                [[T2, 8], [0, 8], [1, T2]]),
                        waits=dfree + (yfirst if cb == 0 else []))
        t_d2 = mrd2.load(bass.AP(dtrep[buf], 64 * T2, [[T2, 64], [1, T2]]),
                         bass.AP(dt_sp, ct * 128 * T2 + (16 * rb + 8) * T2,
                                 [[T2, 8], [0, 8], [1, T2]]),
                         eng="s", waits=dfree)
        t_x1 = mrx.load(xcrep[buf][0:64, :],
                        bass.AP(xcsm, 16 * rb * T2, [[T2, 8], [0, 8], [1, T2]]),
                        waits=xfree + wxc)
        t_x2 = mrx2.load(bass.AP(xcrep[buf], 64 * T2, [[T2, 64], [1, T2]]),
                         bass.AP(xcsm, (16 * rb + 8) * T2, [[T2, 8], [0, 8], [1, T2]]),
                         eng="s", waits=xfree + wxc)
        tok[("mld", cb)] = t_d1
        tok[("mld2", cb)] = t_d2
        tok[("mlx", cb)] = t_x1
        tok[("mlx2", cb)] = t_x2
        for k in range(8):
            kb = k % 2
            t_a = p.op("s", "activation", af[kb][:], dtrep[buf][:], AF.Exp,
                       scale=A_sb[:, cb * 8 + k:cb * 8 + k + 1],
                       waits=[tok[("mld", cb)], tok[("mld2", cb)],
                              (sv, tok.get(("scan", cb, k - 2),
                                           tok.get(("scan", cb - 1, 6 + k), 0)))],
                       inc=ss)
            tok[("aexp", cb, k)] = t_a
            bfree = tok.get(("sum", cb, k - 2, EV - 1),
                            tok.get(("sum", cb - 1, 6 + k, EV - 1), 0))
            t_bx = p.op("v", "tensor_tensor", bxb[kb][:], Brep[k][:], xcrep[buf][:], OP.mult,
                        waits=[tok[("mlx", cb)], tok[("mlx2", cb)], t_br, (sm, bfree)],
                        inc=sv)
            tok[("bx", cb, k)] = t_bx
            t_sc = p.op("v", "tensor_tensor_scan", bxb[kb][:], af[kb][:], bxb[kb][:],
                        zc, OP.mult, OP.add,
                        waits=[(ss, t_a), (sv, t_bx)], inc=sv)
            tok[("scan", cb, k)] = t_sc
            if k == 0 and cb > 0:
                emit_yevac(cb - 1)
            for tc in range(EV):
                t_sum = p.op("t", "matmul", big[0:16, tc * 512:(tc + 1) * 512], ind16[:],
                             bxb[kb][:, tc * 512:(tc + 1) * 512],
                             start=(k == 0), stop=(k == 7),
                             waits=([(sv, t_sc)] if tc == 0 else [])
                             + ([(sv, tok.get(("yev", cb - 1, 1), 0))]
                                if (k == 0 and tc == 0) else [])
                             + ([t_va] if (cb == 0 and k == 0 and tc == 0) else []), inc=sm)
                tok[("sum", cb, k, tc)] = t_sum
    emit_yevac(63)
    p.glob["m_y_done"] = ysp.done()
    if dbg:
        y_dbg = nc.dram_tensor("y_dbg", [8, 128, T2], BF16, kind="ExternalOutput")
        p.op("y", "dma_start", y_dbg.ap(), y_sp.ap(), waits=p.glob["m_y_done"],
             inc=p.ld, inck=16)


def phase_epilogue(p, io, dbg):
    T = p.T
    T2 = 2 * T
    nc = p.nc
    sv, ss, sm, sg = p.sv, p.ss, p.sm, p.sg
    big = p.glob["big"]
    ofs = p.glob["ofs"]
    rg = p.glob["rg"]
    xc_sp = p.glob["xc_sp"]
    y_sp = p.glob["y_sp"]
    res_sp = p.glob["res_sp"]
    accE = p.glob["acc"]
    tok = {}

    y_ag = nc.dram_tensor("y_ag", [2, 8, 128, T2], BF16, kind="Internal")
    t_cc = p.op("g", "collective_compute", "AllGather", OP.bypass, replica_groups=RG_PAIRS,
                ins=[y_sp.ap()], outs=[y_ag.ap()], waits=p.glob["m_y_done"], inc=sg)

    ast = p.stream("ast2", 1)
    acc_sp = p.glob["acc_sp"]
    for ft in range(8):
        t_ar = ast.load(accE[ft][:], acc_sp.ap()[ft],
                        waits=[p.glob["acc_sp_tok"]] if ft == 0 else (), adv=False)

    vst = p.stream("evs", 1)
    Dv = p.sb("eDv", [128, 16], F32)
    t_dv = vst.load(Dv[:], io["D_all"], adv=False)
    wo = p.sb("ewo", [128, 16 * 1024], BF16)
    wst = p.stream("ews", 1)
    for ci in range(16):
        t_wo = wst.load(wo[:, ci * 1024:(ci + 1) * 1024],
                        io["w_outproj"][ci * 128:(ci + 1) * 128, :], adv=False)

    y2 = p.sb("ey2", [128, 16 * T], BF16)
    xstg = [p.sb(f"exs{j}", [128, T], BF16) for j in range(2)]
    rstg = [p.sb(f"ers{j}", [128, T], BF16) for j in range(2)]
    lstg = p.stream("elg", 1)
    lsty = p.stream("els", 2)
    res_done = p.glob["res_done"]
    for s in range(2):
        p.op("g", "reg_load", rg, ofs[0:1, 8 + s:8 + s + 1],
             waits=[(sg, t_cc), p.glob["t_ofs"]] if s == 0 else [])
        t_ly = lstg.load(bass.AP(y2, s * 8 * T, [[16 * T, 128], [T, 8], [1, T]]),
                         bass.AP(y_ag, rg, [[T2, 128], [128 * T2, 8], [1, T]]),
                         eng="g", adv=False)
    for ct in range(16):
        buf = ct % 2
        wfree = [(sv, tok[("y2", ct - 2)])] if ct >= 2 else []
        t_lx = lsty.load(xstg[buf][:], xc_sp.ap()[ct],
                         waits=wfree + (res_done if ct == 0 else []), adv=False)
        t_l = lsty.load(rstg[buf][:], res_sp.ap()[ct])
        t_t = p.op("v", "scalar_tensor_tensor", y2[:, ct * T:(ct + 1) * T], xstg[buf][:],
                   Dv[:, ct:ct + 1], y2[:, ct * T:(ct + 1) * T], OP.mult, OP.add,
                   waits=[t_l, t_ly, t_dv, (sv, tok.get(("y2", ct - 2), 0))], inc=sv)
        t_y2 = p.op("v", "tensor_tensor", y2[:, ct * T:(ct + 1) * T],
                    y2[:, ct * T:(ct + 1) * T], rstg[buf][:], OP.mult,
                    waits=[(sv, t_t)], inc=sv)
        tok[("y2", ct)] = t_y2
    NTCH = T // 512
    for co in range(8):
        for tch in range(NTCH):
            t_mm = 0
            for ci in range(16):
                t_mm = p.op("t", "matmul", big[:, co * 512:(co + 1) * 512],
                            wo[:, ci * 1024 + co * 128:ci * 1024 + co * 128 + 128],
                            y2[:, ci * T + tch * 512:ci * T + (tch + 1) * 512],
                            start=(ci == 0), stop=(ci == 15),
                            waits=[(sv, tok[("y2", ci)]), t_wo]
                            + ([(sv, tok.get(("oa", co, tch - 1),
                                             tok.get(("oa", co - 1, NTCH - 1), 0)))]
                               if ci == 0 else []), inc=sm)
            t_oa = p.op("v", "tensor_tensor", accE[co][:, tch * 512:(tch + 1) * 512],
                        accE[co][:, tch * 512:(tch + 1) * 512], big[:, co * 512:(co + 1) * 512],
                        OP.add, waits=[(sm, t_mm), t_ar], inc=sv)
            tok[("oa", co, tch)] = t_oa
    p.glob["acc_done"] = (sv, sv.n)


class ColView:

    def __init__(self, t, base):
        self.t, self.base = t, base

    def __getitem__(self, idx):
        s0, s1 = idx
        return self.t[s0, self.base + s1.start:self.base + s1.stop]


def phase_out(p, io):
    T = p.T
    FT = D // 128
    TT = T // 128
    acc = p.glob["acc"]
    ident = p.glob["ident"]
    pbk = p.glob["pb"]
    otok = [p.sb(f"otok{j}", [128, D], F32) for j in range(2)]
    ost = p.stream("ost", 2)
    done = {}
    acc_done = p.glob["acc_done"]
    for tt in range(TT):
        buf = otok[tt % 2]
        evs = 0
        for grp in range(2):
            bank = pbk[grp]
            tms = 0
            for j in range(4):
                ft = grp * 4 + j
                w = [acc_done, (p.sv, done.get(("e", grp), 0))]
                if (tt - 2) in done:
                    w.append(done[tt - 2])
                tms = p.op("t", "transpose", bank[:, j * 128:(j + 1) * 128],
                           acc[ft][:, tt * 128:(tt + 1) * 128], ident[:], waits=w, inc=p.sm)
            for j in range(4):
                ft = grp * 4 + j
                evs = p.op("v", "tensor_copy", buf[:, ft * 128:(ft + 1) * 128],
                           bank[:, j * 128:(j + 1) * 128],
                           waits=[(p.sm, tms)], inc=p.sv)
            done[("e", grp)] = evs
        done[tt] = ost.load(io["x_out"][tt * 128:(tt + 1) * 128, :], buf[:],
                            waits=[(p.sv, evs)])
    p.wait("y", [done[TT - 1], done[TT - 2]])


def build(Tloc=TFULL // 2, dbg=False, ncores=8):
    set_ncores(ncores)
    nc = bass.Bass("TRN2", target_bir_lowering=False)
    io = declare_io(nc, Tloc, dbg)
    es = contextlib.ExitStack()
    with es:
        block = es.enter_context(nc.Block())
        p = Prog(nc, es, block, Tloc)
        for cv in (1e-5, 1e-6, 0.0, 1.0):
            p.const(cv)
        p.glob["ofs"] = p.sb("ofs", [1, 16], I32)
        p.glob["rg"] = es.enter_context(nc.gpsimd.register("rg0"))
        rwkv_prelude(p, io, dbg)
        with p.scope():
            p.glob["pb"] = [p.ps(f"pb{i}", 512) for i in range(7)]
            p.glob["pbT"] = p.ps("pbT", 1024, BF16)
            p.glob["acc"] = [p.sb(f"acc{i}", [128, Tloc], F32) for i in range(D // 128)]
            with p.scope(), nc.named_scope("ph_ln"):
                phase_load_ln(p, io, dbg)
                rwkv_stageA(p, io, dbg)
            with p.scope(), nc.named_scope("ph_rwkv"):
                rwkv_main(p, io, dbg)
            with p.scope(), nc.named_scope("ph_rwkvpost"):
                rwkv_post(p, io, dbg)
            p.glob["attn_done"] = (p.sv, p.sv.n)
            with p.scope(), nc.named_scope("ph_ffn0"):
                phase_ffn0(p, io, dbg)
            with p.scope(), nc.named_scope("ph_l1pre"):
                phase_l1pre(p, io, dbg)
            p.glob["acc_done"] = (p.sv, p.sv.n)
            phase_acc_spill(p, io)
        big = p.ps("big", 4096)
        p.glob["big"] = big
        with p.scope(), nc.named_scope("ph_mamba"):
            phase_mamba(p, io, dbg)
        p.glob["acc"] = [p.sb(f"accE{i}", [128, Tloc], F32) for i in range(D // 128)]
        with p.scope(), nc.named_scope("ph_epi"):
            phase_epilogue(p, io, dbg)
        p.glob["pb"] = [ColView(big, i * 512) for i in range(7)]
        with nc.named_scope("ph_out"):
            phase_out(p, io)
    return nc


def kernel(**inputs):
    Tloc = TFULL // 2
    per_core = prep_inputs(inputs, Tloc)
    nc = build(Tloc)
    res = run_bass_kernel_spmd(nc, per_core, core_ids=list(range(8)))
    out = np.zeros((NB, TFULL, D), np.float32)
    for core in range(8):
        b, half = core // 2, core % 2
        out[b, half * Tloc:(half + 1) * Tloc] = res.results[core]["x_out"]
    return out


# version 07
# speedup vs baseline: 1.0428x; 1.0279x over previous
import contextlib
import numpy as np
import ml_dtypes
import concourse.bass as bass
import concourse.mybir as mybir
from concourse.bass_utils import run_bass_kernel_spmd
from concourse.bass_isa_rust import ReduceOp

RED_MAX = ReduceOp.max

F32 = mybir.dt.float32
BF16 = mybir.dt.bfloat16
I32 = mybir.dt.int32
AF = mybir.ActivationFunctionType
OP = mybir.AluOpType
AX = mybir.AxisListType

D = 1024
NB = 4
TFULL = 4096
H, NH = 16, 64
DI = 2816
DM = 2048
NS = 64
CH = 32

ENG = dict(v="vector", s="scalar", t="tensor", g="gpsimd", y="sync")


class Sem:
    __slots__ = ("h", "n")

    def __init__(self, nc, es, name):
        self.h = es.enter_context(nc.semaphore(name))
        self.n = 0


class Prog:
    def __init__(self, nc, es, block, Tloc):
        self.nc, self.es, self.block = nc, es, block
        self.es0 = es
        self.T = Tloc
        self.ld = Sem(nc, es, "ld")
        self.gld = Sem(nc, es, "gld")
        self.sv = Sem(nc, es, "sv")
        self.ss = Sem(nc, es, "ss")
        self.sm = Sem(nc, es, "sm")
        self.sg = Sem(nc, es, "sg")
        self.sy = Sem(nc, es, "sy")
        self.sbm = Sem(nc, es, "sbm")
        self.glob = {}
        self._streams = []
        self._cpool = es.enter_context(nc.sbuf_tensor("cpool", [128, 16], F32))
        self._cvals = {}
        self._cinit = None

    def const(self, val, rows=128):
        if val not in self._cvals:
            assert len(self._cvals) < 16
            j = len(self._cvals)
            self._cvals[val] = j
            tok = self.op("v", "memset", self._cpool[:, j:j + 1], float(val), inc=self.sv)
            self._cinit = (self.sv, tok)
        j = self._cvals[val]
        return self._cpool[0:rows, j:j + 1]

    def op(self, eng, meth, *args, waits=(), inc=None, inck=1, **kw):
        waits = tuple(waits)

        def f(e):
            for sem, val in waits:
                if val > 0:
                    e.wait_ge(sem.h, val)
            ins = getattr(e, meth)(*args, **kw)
            if inc is not None:
                ins.then_inc(inc.h, inck)
        getattr(self.block, ENG[eng])(f)
        if inc is not None:
            inc.n += inck
            return inc.n
        return None

    def wait(self, eng, waits):
        waits = tuple(waits)

        def f(e):
            for sem, val in waits:
                e.wait_ge(sem.h, val)
        getattr(self.block, ENG[eng])(f)

    def dma(self, eng, dst, src, waits=()):
        sem = self.gld if eng == "g" else self.ld
        return self.op(eng, "dma_start", dst, src, waits=waits, inc=sem, inck=16)

    def stream(self, name, nbuf=2):
        s = DStream(self, name, nbuf)
        self._streams.append(s)
        return s

    @contextlib.contextmanager
    def scope(self):
        old_es = self.es
        es2 = contextlib.ExitStack()
        self.es = es2
        try:
            yield
        finally:
            self.barrier()
            es2.close()
            self.es = old_es

    def sb(self, name, shape, dtype):
        return self.es.enter_context(self.nc.sbuf_tensor(name, shape, dtype))

    def ps(self, name, cols, dtype=F32):
        return self.es.enter_context(self.nc.psum_tensor(name, [128, cols], dtype))

    def barrier(self):
        dws = [(self.ld, self.ld.n), (self.gld, self.gld.n)]
        for st_ in self._streams:
            dws += st_.done()
        self.wait("y", dws)
        tgt = self.sbm.n + 5
        for eng in "vstgy":
            self.op(eng, "nop", inc=self.sbm)
        for eng in "vstgy":
            self.wait(eng, [(self.sbm, tgt)])


class DStream:

    def __init__(self, p, name, nbuf):
        self.p, self.nbuf = p, nbuf
        self.sems = [Sem(p.nc, p.es0, f"{name}{i}") for i in range(nbuf)]
        self.i = 0

    def load(self, dst, src, eng="y", waits=(), adv=True, **kw):
        s = self.sems[self.i % self.nbuf]
        self.p.op(eng, "dma_start", dst, src, waits=waits, inc=s, inck=16, **kw)
        if adv:
            self.i += 1
        return (s, s.n)

    def adv(self):
        self.i += 1

    def done(self):
        return [(s, s.n) for s in self.sems if s.n > 0]



def _bf16(a):
    return np.ascontiguousarray(np.asarray(a).astype(ml_dtypes.bfloat16))


def _f32(a):
    return np.ascontiguousarray(np.asarray(a, dtype=np.float32))


def _tiles(vec):
    v = _f32(vec).reshape(-1)
    return np.ascontiguousarray(v.reshape(v.shape[0] // 128, 128).T)


def prep_inputs(inputs, Tloc):
    x = _f32(inputs["L0_attn_x"])
    sh = {}
    for nm in ["Wr", "Wk", "Wv", "Ww", "Wg", "Wo"]:
        sh["w" + nm] = _bf16(np.asarray(inputs["L0_attn_" + nm]).T)
    for nm in ["ln_w", "ln_b", "mix_k", "mix_v", "mix_r", "mix_w", "time_decay"]:
        sh[nm] = _tiles(inputs["L0_attn_" + nm])
    sh["u_vec"] = _tiles(np.asarray(inputs["L0_attn_time_first"]).reshape(-1))
    sh["clip_x"] = _f32(inputs["L0_ffn_clip_x"]).reshape(1, 1)
    sh["clip_down"] = _f32(inputs["L0_ffn_clip_down"]).reshape(1, 1)
    sh["w_gate_r"] = _f32(inputs["L0_ffn_w_gate"])
    sh["w_up_r"] = _f32(inputs["L0_ffn_w_up"])
    sh["w_down_r"] = _f32(inputs["L0_ffn_w_down"])
    sh["w_gate_t"] = _f32(np.asarray(inputs["L0_ffn_w_gate"]).T)
    sh["w_up_t"] = _f32(np.asarray(inputs["L0_ffn_w_up"]).T)
    sh["w_down_t"] = _f32(np.asarray(inputs["L0_ffn_w_down"]).T)
    sh["norm_w"] = _tiles(inputs["L1_attn_norm_w"])
    sh["w_inproj"] = _bf16(np.asarray(inputs["L1_attn_in_proj_w"]).T)
    sh["conv_w"] = np.ascontiguousarray(_f32(inputs["L1_attn_conv_w"]).reshape(16, 128, 4))
    sh["conv_b"] = _tiles(inputs["L1_attn_conv_b"])
    sh["w_xproj"] = _bf16(np.asarray(inputs["L1_attn_x_proj_w"])[64:128].T)
    sh["w_outproj"] = _bf16(np.asarray(inputs["L1_attn_out_proj_w"]).T)
    sh["D_all"] = _tiles(inputs["L1_attn_D_param"])
    ind = np.zeros((128, 16), np.float32)
    for pp in range(128):
        ind[pp, pp // 8] = 1.0
    sh["ind16"] = _bf16(ind)
    sh["w_win"] = _bf16(inputs["L1_ffn_Win"])
    sh["bin_"] = _f32(inputs["L1_ffn_bin_"]).reshape(NH, 1)
    sh["W1"] = _f32(inputs["L1_ffn_W1"])
    sh["b1"] = _f32(inputs["L1_ffn_b1"])
    sh["W2"] = _f32(inputs["L1_ffn_W2"])
    sh["b2"] = _f32(inputs["L1_ffn_b2"]).reshape(NH, 1)
    sh["w_wout"] = _bf16(inputs["L1_ffn_Wout"])
    sh["bout"] = _tiles(inputs["L1_ffn_bout"])
    sh["ident"] = _f32(np.eye(128))
    sh["identb"] = _bf16(np.eye(128))
    m = np.tril(np.ones((CH, CH), np.float32), -1).T
    sh["mask32"] = _bf16(np.tile(m, (1, H)))
    m64 = np.tril(np.ones((64, 64), np.float32), -1).T
    sh["mask64"] = _bf16(np.tile(m64, (1, H)))
    hs = np.zeros((8, 128, 16), np.float32)
    rs = np.zeros((8, 16, 128), np.float32)
    for ct in range(8):
        hs[ct, 0:64, 2 * ct] = 1.0
        hs[ct, 64:128, 2 * ct + 1] = 1.0
        rs[ct, 2 * ct, 0:64] = 1.0
        rs[ct, 2 * ct + 1, 64:128] = 1.0
    sh["headsel"] = _bf16(hs.transpose(1, 0, 2).reshape(128, 8 * 16))
    sh["rowsel"] = _bf16(rs.transpose(1, 0, 2).reshape(16, 8 * 128))
    bind = np.zeros((NS, NS * 128), np.float32)
    for n in range(NS):
        bind[n, n * 128:(n + 1) * 128] = 1.0
    sh["bind"] = _bf16(bind)

    per_core = []
    for core in range(8):
        b, half = core // 2, core % 2
        d = dict(sh)
        d["x_in"] = np.ascontiguousarray(x[b, half * Tloc:(half + 1) * Tloc])
        d["x_prev"] = (np.zeros((1, D), np.float32) if half == 0
                       else np.ascontiguousarray(x[b, Tloc - 1:Tloc]))
        d["halo_mask"] = np.full((1, 1), float(half), np.float32)
        d["halo_maskv"] = np.full((128, 1), float(half), np.float32)
        sl = slice(half * (DM // 2), (half + 1) * (DM // 2))
        al = _f32(inputs["L1_attn_A_log"])[sl]
        als = np.empty((128, 512), np.float32)
        for cb in range(64):
            blk = al[128 * (cb // 8) + 16 * (cb % 8):][:16]
            als[:, cb * 8:(cb + 1) * 8] = (
                blk.reshape(16, 8, 8).transpose(0, 2, 1).reshape(128, 8))
        d["A_log_h"] = np.ascontiguousarray(als)
        d["w_dt_h"] = np.ascontiguousarray(
            _bf16(np.asarray(inputs["L1_attn_dt_w"]).T)[:, sl])
        d["dt_bh"] = _tiles(np.asarray(inputs["L1_attn_dt_b"])[sl])
        T2 = 2 * Tloc
        offs = [(half * 8 + ct) * (128 * Tloc) for ct in range(8)]
        offs += [s * (8 * 128 * T2) + half * Tloc for s in range(2)]
        offs += [0] * 6
        d["offs"] = np.asarray(offs, np.int32).reshape(1, 16)
        per_core.append(d)
    return per_core


def declare_io(nc, Tloc, dbg):
    io = {}

    def inp(name, shape, dtype=F32):
        io[name] = nc.dram_tensor(name, shape, dtype, kind="ExternalInput").ap()

    def outp(name, shape, dtype=F32):
        io[name] = nc.dram_tensor(name, shape, dtype, kind="ExternalOutput").ap()

    inp("x_in", [Tloc, D]); inp("x_prev", [1, D]); inp("halo_mask", [1, 1])
    for nm in ["Wr", "Wk", "Wv", "Ww", "Wg", "Wo"]:
        inp("w" + nm, [D, D], BF16)
    for nm in ["ln_w", "ln_b", "mix_k", "mix_v", "mix_r", "mix_w", "time_decay", "u_vec",
               "norm_w", "bout"]:
        inp(nm, [128, 8])
    inp("conv_b", [128, 16])
    inp("halo_maskv", [128, 1])
    inp("mask64", [64, H * 64], BF16)
    inp("headsel", [128, 8 * 16], BF16)
    inp("rowsel", [16, 8 * 128], BF16)
    inp("clip_x", [1, 1]); inp("clip_down", [1, 1])
    inp("w_gate_r", [DI, D]); inp("w_up_r", [DI, D]); inp("w_down_r", [D, DI])
    inp("w_gate_t", [D, DI]); inp("w_up_t", [D, DI]); inp("w_down_t", [DI, D])
    inp("w_inproj", [D, 2 * DM], BF16)
    inp("conv_w", [16, 128, 4])
    inp("w_xproj", [DM, NS], BF16); inp("w_dt_h", [DM, DM // 2], BF16)
    inp("w_outproj", [DM, D], BF16)
    inp("D_all", [128, 16]); inp("ind16", [128, 16], BF16); inp("dt_bh", [128, 8])
    io["offs"] = nc.dram_tensor("offs", [1, 16], mybir.dt.int32, kind="ExternalInput").ap()
    inp("w_win", [D, NH], BF16); inp("bin_", [NH, 1])
    inp("W1", [NH, 16]); inp("b1", [NH, 16]); inp("W2", [NH, 16]); inp("b2", [NH, 1])
    inp("w_wout", [NH, D], BF16)
    inp("ident", [128, 128]); inp("identb", [128, 128], BF16)
    inp("mask32", [CH, H * CH], BF16)
    inp("bind", [NS, NS * 128], BF16)
    inp("A_log_h", [128, 512])
    outp("x_out", [Tloc, D])
    if dbg:
        outp("dbg_xn0", [8, 128, Tloc], BF16)
    return io


def phase_load_ln(p, io, dbg):
    T = p.T
    FT = D // 128
    TT = T // 128
    acc = p.glob["acc"]
    xn0 = [p.sb(f"xn0_{i}", [128, 1 + T], BF16) for i in range(FT)]
    p.glob["xn0"] = xn0

    ident = p.glob["ident"]
    identb = p.glob["identb"]
    t_c = p.glob["t_ident"]

    xst = p.stream("ldx", 2)
    xtok = [p.sb(f"xtok{j}", [128, D], F32) for j in range(2)]
    xntok = p.sb("xntok", [128, D], F32)
    sq = p.sb("sq_ln", [128, D], F32)
    st8 = p.sb("st8", [128, 8], F32)
    pbk = p.glob["pb"]

    sv, ss, sm = p.sv, p.ss, p.sm
    gates = {}
    for tt in range(TT):
        buf = xtok[tt % 2]
        w = [(sv, gates[tt - 2])] if (tt - 2) in gates else ()
        tok_ld = xst.load(buf[:], io["x_in"][tt * 128:(tt + 1) * 128, :], waits=w)
        p.op("v", "reduce_sum", st8[:, 0:1], buf[:], axis=AX.X, waits=[tok_ld], inc=sv)
        tok_sq = p.op("s", "activation", sq[:], buf[:], AF.Square, waits=[tok_ld], inc=ss)
        t1 = p.op("v", "reduce_sum", st8[:, 1:2], sq[:], axis=AX.X, waits=[(ss, tok_sq)], inc=sv)
        t2 = p.op("v", "tensor_scalar", st8[:, 2:3], st8[:, 0:1], 1.0 / D, None, OP.mult,
                  waits=[(sv, t1)], inc=sv)
        t3 = p.op("s", "activation", st8[:, 3:4], st8[:, 2:3], AF.Square, waits=[(sv, t2)], inc=ss)
        t4 = p.op("v", "scalar_tensor_tensor", st8[:, 4:5], st8[:, 1:2], 1.0 / D, st8[:, 3:4],
                  OP.mult, OP.subtract, waits=[(ss, t3)], inc=sv)
        t5 = p.op("s", "activation", st8[:, 5:6], st8[:, 4:5], AF.Sqrt, bias=p.const(1e-5),
                  waits=[(sv, t4)], inc=ss)
        t6 = p.op("v", "reciprocal", st8[:, 6:7], st8[:, 5:6], waits=[(ss, t5)], inc=sv)
        t7 = p.op("v", "tensor_scalar", xntok[:], buf[:], st8[:, 2:3], st8[:, 6:7],
                  OP.subtract, OP.mult, waits=[(sv, t6)], inc=sv)
        for grp in range(2):
            bank = pbk[grp]
            tms = 0
            for j in range(4):
                ft = grp * 4 + j
                tms = p.op("t", "transpose", bank[:, j * 128:(j + 1) * 128],
                           buf[:, ft * 128:(ft + 1) * 128], ident[:],
                           waits=[tok_ld, (sv, gates.get(("psx", grp), 0)), t_c], inc=sm)
            for j in range(4):
                ft = grp * 4 + j
                e1 = p.op("v", "tensor_copy", acc[ft][:, tt * 128:(tt + 1) * 128],
                          bank[:, j * 128:(j + 1) * 128],
                          waits=[(sm, tms)], inc=sv)
            gates[("psx", grp)] = e1
        for grp in range(2):
            bank = pbk[2 + grp]
            tms = 0
            for j in range(4):
                ft = grp * 4 + j
                tms = p.op("t", "transpose", bank[:, j * 128:(j + 1) * 128],
                           xntok[:, ft * 128:(ft + 1) * 128], ident[:],
                           waits=[(sv, t7), (sv, gates.get(("psn", grp), 0))], inc=sm)
            for j in range(4):
                ft = grp * 4 + j
                e2 = p.op("v", "tensor_copy", xn0[ft][:, 1 + tt * 128:1 + (tt + 1) * 128],
                          bank[:, j * 128:(j + 1) * 128],
                          waits=[(sm, tms)], inc=sv)
            gates[("psn", grp)] = e2
        gates[tt] = e2
    xprev = p.sb("xprev", [1, D], F32)
    hm = p.sb("hm", [1, 1], F32)
    hst = p.stream("ldh", 1)
    hst.load(xprev[:], io["x_prev"])
    t_h = hst.load(hm[:], io["halo_mask"])
    st1 = p.sb("st1", [1, 8], F32)
    sq1 = p.sb("sq1", [1, D], F32)
    a = p.op("v", "reduce_sum", st1[:, 0:1], xprev[:], axis=AX.X, waits=[t_h], inc=sv)
    b_ = p.op("s", "activation", sq1[:], xprev[:], AF.Square, waits=[t_h], inc=ss)
    c = p.op("v", "reduce_sum", st1[:, 1:2], sq1[:], axis=AX.X, waits=[(ss, b_)], inc=sv)
    d_ = p.op("v", "tensor_scalar", st1[:, 2:3], st1[:, 0:1], 1.0 / D, None, OP.mult,
              waits=[(sv, c)], inc=sv)
    e_ = p.op("s", "activation", st1[:, 3:4], st1[:, 2:3], AF.Square, waits=[(sv, d_)], inc=ss)
    f_ = p.op("v", "scalar_tensor_tensor", st1[:, 4:5], st1[:, 1:2], 1.0 / D, st1[:, 3:4],
              OP.mult, OP.subtract, waits=[(ss, e_)], inc=sv)
    g_ = p.op("s", "activation", st1[:, 5:6], st1[:, 4:5], AF.Sqrt, bias=p.const(1e-5, rows=1),
              waits=[(sv, f_)], inc=ss)
    h_ = p.op("v", "reciprocal", st1[:, 6:7], st1[:, 5:6], waits=[(ss, g_)], inc=sv)
    i_ = p.op("v", "tensor_tensor", st1[:, 6:7], st1[:, 6:7], hm[:, 0:1], OP.mult,
              waits=[(sv, h_)], inc=sv)
    j_ = p.op("v", "tensor_scalar", sq1[:], xprev[:], st1[:, 2:3], st1[:, 6:7],
              OP.subtract, OP.mult, waits=[(sv, i_)], inc=sv)
    prev_ev = gates[("psx", 0)]
    for ft in range(FT):
        tm = p.op("t", "transpose", pbk[0][:, 0:1], sq1[:, ft * 128:(ft + 1) * 128],
                  ident[0:1, 0:1], waits=[(sv, j_), (sv, prev_ev)], inc=sm)
        prev_ev = p.op("v", "tensor_copy", xn0[ft][:, 0:1], pbk[0][:, 0:1],
                       waits=[(sm, tm)], inc=sv)
    p.glob["xn0_done"] = (sv, sv.n)

    if dbg:
        dst = p.stream("dbgs", 1)
        for ft in range(FT):
            dst.load(io["dbg_xn0"][ft], xn0[ft][:, 1:1 + T], eng="y",
                     waits=[(sv, sv.n)])
        p.glob["dbg_st"] = dst


RG_PAIRS = [[0, 1], [2, 3], [4, 5], [6, 7]]
RG_ALL = [[0, 1, 2, 3, 4, 5, 6, 7]]


def set_ncores(n):
    global RG_PAIRS, RG_ALL
    RG_PAIRS = [[i, i + 1] for i in range(0, n, 2)]
    RG_ALL = [list(range(n))]


def vchunk(tens, coff, n, step=512, groups=8, pitch=None):
    pitch = pitch if pitch is not None else tens.shape[1]
    return bass.AP(tens, coff, [[pitch, 128], [step, groups], [1, n]])


def prow(tens, row0, nrow, coff, ncol):
    pitch = tens.shape[1]
    return bass.AP(tens, row0 * pitch + coff, [[pitch, nrow], [1, ncol]])


def rwkv_prelude(p, io, dbg):
    nc, T = p.nc, p.T
    FT = 8
    C = 64
    sv, ss, sm, sg = p.sv, p.ss, p.sm, p.sg
    ident = p.sb("identS", [128, 128], F32)
    identb = p.sb("identbS", [128, 128], BF16)
    p.glob["ident"] = ident
    p.glob["identb"] = identb
    cst = p.stream("ldc", 1)
    cst.load(ident[:], io["ident"], adv=False)
    p.glob["t_ident"] = cst.load(identb[:], io["identb"], adv=False)
    dr = {}
    dbg_vis = {"xn2_sp", "qc_sp", "yraw_sp", "g_sp"}
    for nm, shape, dt_ in [("xn0_sp", [FT, 128, T], BF16), ("xn2_sp", [FT, 128, 1 + T], BF16),
                           ("qc_sp", [128, FT * T], BF16), ("yraw_sp", [128, FT * T], BF16),
                           ("g_sp", [128, FT * T], BF16),
                           ("s_snd", [FT, 128, NH], F32), ("s_rcv", [2, FT, 128, NH], F32),
                           ("armax_in", [1, 1], F32), ("armax_out", [1, 1], F32)]:
        kind = "ExternalOutput" if (dbg and nm in dbg_vis) else "Internal"
        dr[nm] = nc.dram_tensor(nm, shape, dt_, kind=kind)
    dra = {nm: h.ap() for nm, h in dr.items()}
    p.glob["dr"] = dr
    p.glob["dra"] = dra

    vst = p.stream("vld", 1)
    lnw = p.sb("lnw", [128, 8], F32); vst.load(lnw[:], io["ln_w"], adv=False)
    lnb = p.sb("lnb", [128, 8], F32); vst.load(lnb[:], io["ln_b"], adv=False)
    mixv = p.sb("mixv", [128, 32], F32)
    for i, nm in enumerate(["mix_k", "mix_v", "mix_r", "mix_w"]):
        vst.load(mixv[:, i * 8:(i + 1) * 8], io[nm], adv=False)
    tdv = p.sb("tdv", [128, 8], F32); vst.load(tdv[:], io["time_decay"], adv=False)
    uv = p.sb("uv", [128, 8], F32); vst.load(uv[:], io["u_vec"], adv=False)
    hmv = p.sb("hmv", [128, 1], F32); vst.load(hmv[:], io["halo_maskv"], adv=False)
    mk64 = p.sb("mk64", [C, H * C], BF16); vst.load(mk64[:], io["mask64"], adv=False)
    hsel = p.sb("hsel", [128, 8 * 16], BF16); vst.load(hsel[:], io["headsel"], adv=False)
    rsel = p.sb("rsel", [16, 8 * 128], BF16)
    t_vec = vst.load(rsel[:], io["rowsel"], adv=False)
    sgm = p.sb("sgm", [128, 32], F32)
    t_sgm = p.op("s", "activation", sgm[:], mixv[:], AF.Sigmoid, waits=[t_vec], inc=ss)

    onesb = p.sb("onesb", [128, 128], BF16)
    t_ones = p.op("v", "memset", onesb[:], 1.0, inc=sv)
    onesf = p.sb("onesf", [128, 128], F32)
    t_onesf = p.op("v", "memset", onesf[:], 1.0, inc=sv)

    for _k in ("dr", "dra", "lnw", "lnb", "mixv", "tdv", "uv", "hmv", "mk64", "hsel",
               "rsel", "sgm", "t_vec", "t_sgm", "onesb", "onesf", "t_ones", "t_onesf"):
        p.glob[_k] = locals()[_k]
    p.glob["rtok"] = {}


def rwkv_stageA(p, io, dbg):
    nc, T = p.nc, p.T
    FT = 8
    NSC = T // 512
    C = 64
    NCH = 512 // C
    acc, xn0, pb = p.glob["acc"], p.glob.get("xn0"), p.glob["pb"]
    pbT = p.glob["pbT"]
    identb = p.glob["identb"]
    sv, ss, sm, sg = p.sv, p.ss, p.sm, p.sg
    dr, dra = p.glob["dr"], p.glob["dra"]
    lnw, lnb, tdv, uv, hmv = p.glob["lnw"], p.glob["lnb"], p.glob["tdv"], p.glob["uv"], p.glob["hmv"]
    mk64, hsel, rsel, sgm = p.glob["mk64"], p.glob["hsel"], p.glob["rsel"], p.glob["sgm"]
    t_vec, t_sgm = p.glob["t_vec"], p.glob["t_sgm"]
    onesb, onesf, t_ones, t_onesf = p.glob["onesb"], p.glob["onesf"], p.glob["t_ones"], p.glob["t_onesf"]
    tok = p.glob["rtok"]

    xn2f = [p.sb(f"xn2f{i}", [128, 1 + T], BF16) for i in range(FT)]
    sqb = p.sb("sqb", [128, 512], BF16)
    rstd2 = p.sb("rstd2", [128, 512], F32)
    xdone = p.glob["xn0_done"]
    bf6 = 0
    ncol = 1 + T
    xn2_tok = 0
    for ck in range((ncol + 511) // 512):
        c0 = ck * 512
        w_ = min(ncol, c0 + 512) - c0
        mm_t = 0
        for ft in range(FT):
            tsq = p.op("s", "activation", sqb[:, 0:w_], xn0[ft][:, c0:c0 + w_], AF.Square,
                       waits=[xdone, (sm, mm_t), (sv, bf6)], inc=ss)
            mm_t = p.op("t", "matmul", pb[6][:, 0:w_], onesb[:], sqb[:, 0:w_],
                        start=(ft == 0), stop=(ft == FT - 1),
                        waits=[(ss, tsq), (sv, t_ones), (sv, bf6)], inc=sm)
        t1 = p.op("s", "activation", rstd2[:, 0:w_], pb[6][:, 0:w_], AF.Sqrt,
                  bias=p.const(1e-5), scale=1.0 / D, waits=[(sm, mm_t), (sv, xn2_tok)], inc=ss)
        bf6 = p.op("v", "reciprocal", rstd2[:, 0:w_], rstd2[:, 0:w_], waits=[(ss, t1)], inc=sv)
        for ft in range(FT):
            t2 = p.op("v", "tensor_tensor", xn2f[ft][:, c0:c0 + w_], xn0[ft][:, c0:c0 + w_],
                      rstd2[:, 0:w_], OP.mult, waits=[(sv, bf6)], inc=sv)
            xn2_tok = p.op("v", "tensor_scalar", xn2f[ft][:, c0:c0 + w_], xn2f[ft][:, c0:c0 + w_],
                           lnw[:, ft:ft + 1], lnb[:, ft:ft + 1], OP.mult, OP.add,
                           waits=[(sv, t2), t_vec], inc=sv)
    for ft in range(FT):
        xn2_tok = p.op("v", "tensor_tensor", xn2f[ft][:, 0:1], xn2f[ft][:, 0:1], hmv[:],
                       OP.mult, waits=[(sv, xn2_tok), t_vec], inc=sv)

    mx8 = p.sb("mx8", [128, 9], F32)
    tmx = 0
    for ft in range(FT):
        tmx = p.op("v", "reduce_max", mx8[:, ft:ft + 1], xn0[ft][:, 1:1 + T], axis=AX.X,
                   apply_absolute_value=True, waits=[xdone], inc=sv)
    tmx = p.op("v", "reduce_max", mx8[:, 8:9], mx8[:, 0:8], axis=AX.X,
               apply_absolute_value=True, waits=[(sv, tmx)], inc=sv)
    mxr = p.sb("mxr", [1, 129], F32)
    ttr = p.op("t", "transpose", pb[0][0:1, 0:128], mx8[:, 8:9], p.glob["ident"][:],
               waits=[(sv, tmx), xdone], inc=sm)
    tev = p.op("s", "copy", mxr[0:1, 0:128], pb[0][0:1, 0:128], waits=[(sm, ttr)], inc=ss)
    tpr = p.op("v", "reduce_max", mxr[0:1, 128:129], mxr[0:1, 0:128], axis=AX.X,
               waits=[(ss, tev)], inc=sv)
    tdm = p.dma("g", dra["armax_in"], mxr[0:1, 128:129], waits=[(sv, tpr)])
    tcc = p.op("g", "collective_compute", "AllReduce", OP.max, replica_groups=RG_ALL,
               ins=[dra["armax_in"]], outs=[dra["armax_out"]],
               waits=[(p.gld, tdm)], inc=sg)
    p.glob["armax_tok"] = (sg, tcc)

    spst = p.stream("spl", 2)
    for ft in range(FT):
        spst.load(dra["xn0_sp"][ft], xn0[ft][:, 1:1 + T], waits=[xdone])
        spst.load(dra["xn2_sp"][ft], xn2f[ft][:], waits=[(sv, xn2_tok)])
    sp_done = spst.done()
    p.glob["xn0_spilled"] = sp_done

    p.glob["sp_done"] = sp_done

def rwkv_main(p, io, dbg):
    nc, T = p.nc, p.T
    FT = 8
    NSC = T // 512
    C = 64
    NCH = 512 // C
    acc, xn0, pb = p.glob["acc"], p.glob.get("xn0"), p.glob["pb"]
    pbT = p.glob["pbT"]
    identb = p.glob["identb"]
    sv, ss, sm, sg = p.sv, p.ss, p.sm, p.sg
    dr, dra = p.glob["dr"], p.glob["dra"]
    lnw, lnb, tdv, uv, hmv = p.glob["lnw"], p.glob["lnb"], p.glob["tdv"], p.glob["uv"], p.glob["hmv"]
    mk64, hsel, rsel, sgm = p.glob["mk64"], p.glob["hsel"], p.glob["rsel"], p.glob["sgm"]
    t_vec, t_sgm = p.glob["t_vec"], p.glob["t_sgm"]
    onesb, onesf, t_ones, t_onesf = p.glob["onesb"], p.glob["onesf"], p.glob["t_ones"], p.glob["t_onesf"]
    tok = p.glob["rtok"]

    sp_done = p.glob["sp_done"]
    xn2sc = p.sb("xn2sc", [128, FT * 513], BF16)
    dsb = p.sb("dsb", [128, FT * 512], BF16)
    xmix = p.sb("xmix", [128, FT * 512], BF16)
    Wsb = [p.sb(f"Wsb{j}", [128, FT * 1024], BF16) for j in range(2)]
    ksb = p.sb("ksb", [128, FT * 512], BF16)
    vtok = p.sb("vtok", [128, 4 * 1024], BF16)
    vfe = p.sb("vfe", [128, FT * 512], BF16)
    rsb = p.sb("rsb", [128, FT * 512], BF16)
    gsb = p.sb("gsb", [128, FT * 512], BF16)
    ewsb = p.sb("ewsb", [128, FT * 512], BF16)
    Lsb = p.sb("Lsb", [128, FT * 65], F32)
    qh = p.sb("qh", [128, FT * C], BF16)
    qcch = p.sb("qcch", [128, FT * C], BF16)
    kt = p.sb("kt", [128, FT * C], BF16)
    kh = p.sb("kh", [128, FT * C], BF16)
    ka = p.sb("ka", [128, FT * C], F32)
    qe = p.sb("qe", [128, FT * C], BF16)
    dt8 = p.sb("dt8", [128, 8], F32)
    dcum = p.sb("dcum", [128, 8], F32)
    rku = p.sb("rku", [128, FT * C], BF16)
    rhosb = p.sb("rhosb", [16, C], BF16)
    amsb = p.sb("amsb", [128, H * C], BF16)
    khtok = p.sb("khtok", [128, FT * 128], BF16)
    dtmp = p.sb("dtmp", [128, FT * C], BF16)
    yraw = p.sb("yraw", [128, FT * C], BF16)
    Ssb = [p.sb(f"Ssb{i}", [128, NH], F32) for i in range(FT)]
    Sbb = [p.sb(f"Sbb{i}", [128, NH], BF16) for i in range(FT)]
    t_s0 = 0
    for i in range(FT):
        p.op("v", "memset", Ssb[i][:], 0.0, inc=sv)
        t_s0 = p.op("v", "memset", Sbb[i][:], 0.0, inc=sv)
    t_dc = p.op("v", "memset", dcum[:], 1.0, inc=sv)

    wst = p.stream("wld", 2)
    xst2 = p.stream("x2ld", 1)
    qst = p.stream("qsp", 2)
    qgt = p.stream("qg", 1)
    qct = p.stream("qc", 1)
    qyt = p.stream("qy", 1)
    bankfree = {0: None, 1: None, 2: None}
    tok = {}
    t_chain = t_s0
    WNAMES = ["wWk", "wWv", "wWr", "wWg", "wWw"]
    MIXI = {"wWk": 0, "wWv": 1, "wWr": 2, "wWw": 3}

    for sc in range(NSC):
        wv = [(sv, tok[("mix", "wWw", sc - 1)])] if sc else []
        t_x2 = 0
        for ft in range(FT):
            t_x2 = xst2.load(xn2sc[:, ft * 513:(ft + 1) * 513],
                             dra["xn2_sp"][ft][:, sc * 512:sc * 512 + 513],
                             waits=(sp_done + wv) if ft == 0 else (), adv=(ft == FT - 1))
        t_d = p.op("v", "tensor_tensor", vchunk(dsb, 0, 512), vchunk(xn2sc, 1, 512, step=513),
                   vchunk(xn2sc, 0, 512, step=513), OP.subtract,
                   waits=[t_x2, (sv, tok.get(("dfree", sc - 1), 0))], inc=sv)
        for wi, wn in enumerate(WNAMES):
            wbuf = Wsb[wi % 2]
            gtok = tok.get(("wmm", sc, wi - 2)) or tok.get(("wmm", sc - 1, wi + 3)) or 0
            t_w = 0
            for ci in range(FT):
                t_w = wst.load(wbuf[:, ci * 1024:(ci + 1) * 1024],
                               io[wn][ci * 128:(ci + 1) * 128, :],
                               waits=[(sm, gtok)] if (gtok and ci == 0) else (),
                               adv=(ci == FT - 1))
            if wn != "wWg":
                mi = MIXI[wn]
                wmprev = [(sm, tok[("wmm", sc, wi - 1)])] if wi else (
                    [(sm, tok[("wmm", sc - 1, 4)])] if sc else [])
                t_m = 0
                for ft in range(FT):
                    t_m = p.op("v", "scalar_tensor_tensor", xmix[:, ft * 512:(ft + 1) * 512],
                               dsb[:, ft * 512:(ft + 1) * 512], sgm[:, mi * 8 + ft:mi * 8 + ft + 1],
                               xn2sc[:, ft * 513:ft * 513 + 512], OP.mult, OP.add,
                               waits=([(sv, t_d), (ss, t_sgm)] + wmprev) if ft == 0 else [(sv, t_d)],
                               inc=sv)
                tok[("mix", wn, sc)] = t_m
            else:
                t_m = tok[("mix", "wWr", sc)]
            if wn == "wWw":
                tok[("dfree", sc)] = t_m
            if wn == "wWv":
                t_mm = 0
                for tb in range(4):
                    for colh in range(2):
                        gv = tok.get(("vtev", sc, tb - 1, colh)) or bankfree[colh]
                        guse = tok.get(("vtuse", sc - 1))
                        for ci in range(FT):
                            t_mm = p.op("t", "matmul", pb[colh][:, 0:512],
                                        prow(xmix, 0, 128, ci * 512 + tb * 128, 128),
                                        wbuf[:, ci * 1024 + colh * 512:ci * 1024 + colh * 512 + 512],
                                        start=(ci == 0), stop=(ci == FT - 1),
                                        waits=([(sv, t_m), t_w]
                                               if (tb == 0 and colh == 0 and ci == 0) else [])
                                        + ([gv] if (gv and ci == 0) else []), inc=sm)
                        ev = p.op("s", "copy",
                                  vtok[:, tb * 1024 + colh * 512:tb * 1024 + colh * 512 + 512],
                                  pb[colh][:, 0:512],
                                  waits=[(sm, t_mm)] + ([guse] if (guse and tb == 0 and colh == 0) else []),
                                  inc=ss)
                        tok[("vtev", sc, tb, colh)] = (ss, ev)
                bankfree[0] = tok[("vtev", sc, 3, 0)]
                bankfree[1] = tok[("vtev", sc, 3, 1)]
                tok[("vtok", sc)] = ev
                for co in range(FT):
                    gv = tok.get(("vfev", sc, co - 1))
                    gbank = bankfree[2]
                    guse = tok.get(("vfuse", sc - 1))
                    for ci in range(FT):
                        t_mm = p.op("t", "matmul", pb[2][:, 0:512],
                                    wbuf[:, ci * 1024 + co * 128:ci * 1024 + co * 128 + 128],
                                    xmix[:, ci * 512:(ci + 1) * 512],
                                    start=(ci == 0), stop=(ci == FT - 1),
                                    waits=([(sv, t_m), t_w] if (co == 0 and ci == 0) else [])
                                    + ([gv] if (gv and ci == 0) else [])
                                    + ([gbank] if (gbank and co == 0 and ci == 0) else []),
                                    inc=sm)
                    ev = p.op("s", "copy", vfe[:, co * 512:(co + 1) * 512], pb[2][:, 0:512],
                              waits=[(sm, t_mm)] + ([guse] if (guse and co == 0) else []), inc=ss)
                    tok[("vfev", sc, co)] = (ss, ev)
                tok[("vfe", sc)] = ev
                bankfree[2] = tok[("vfev", sc, FT - 1)]
                tok[("wmm", sc, wi)] = t_mm
            else:
                dst = {"wWk": ksb, "wWr": rsb, "wWg": gsb, "wWw": ewsb}[wn]
                bank = wi % 2
                t_mm = 0
                ev = 0
                guse = tok.get((wn + "use", sc - 1))
                if isinstance(guse, Sem):
                    guse = (guse, guse.n)
                for co in range(FT):
                    gv = tok.get((wn + "ev", sc, co - 1)) or bankfree[bank]
                    for ci in range(FT):
                        t_mm = p.op("t", "matmul", pb[bank][:, 0:512],
                                    wbuf[:, ci * 1024 + co * 128:ci * 1024 + co * 128 + 128],
                                    xmix[:, ci * 512:(ci + 1) * 512],
                                    start=(ci == 0), stop=(ci == FT - 1),
                                    waits=([(sv, t_m), t_w] if (co == 0 and ci == 0) else [])
                                    + ([gv] if (gv and ci == 0) else []), inc=sm)
                    dguard = [guse] if (guse and co == 0) else []
                    if wn == "wWk":
                        ev = p.op("s", "copy", dst[:, co * 512:(co + 1) * 512],
                                  pb[bank][:, 0:512], waits=[(sm, t_mm)] + dguard, inc=ss)
                        evt = (ss, ev)
                    elif wn in ("wWr", "wWg"):
                        ev = p.op("s", "activation", dst[:, co * 512:(co + 1) * 512],
                                  pb[bank][:, 0:512], AF.Sigmoid,
                                  waits=[(sm, t_mm)] + dguard, inc=ss)
                        evt = (ss, ev)
                    else:
                        tw1 = p.op("v", "tensor_scalar", ewsb[:, co * 512:(co + 1) * 512],
                                   pb[bank][:, 0:512], tdv[:, co:co + 1], -10.0, OP.add, OP.max,
                                   waits=[(sm, t_mm), t_vec] + dguard, inc=sv)
                        ev = p.op("v", "tensor_scalar", ewsb[:, co * 512:(co + 1) * 512],
                                  ewsb[:, co * 512:(co + 1) * 512], 10.0, None, OP.min,
                                  waits=[(sv, tw1)], inc=sv)
                        evt = (sv, ev)
                    tok[(wn + "ev", sc, co)] = evt
                bankfree[bank] = tok[(wn + "ev", sc, FT - 1)]
                if wn == "wWw":
                    ev = p.op("s", "activation", ewsb[:], ewsb[:], AF.Exp,
                              waits=[(sv, ev)], inc=ss)
                tok[("proj", wn, sc)] = ev
                tok[("wmm", sc, wi)] = t_mm
        t_gsp = qgt.load(vchunk(dr["g_sp"], sc * 512, 512, step=T, pitch=FT * T),
                         vchunk(gsb, 0, 512), waits=[(ss, tok[("proj", "wWg", sc)])])

        for cc in range(NCH):
            gi = sc * NCH + cc
            co0 = cc * C
            wK = (ss, tok[("proj", "wWk", sc)])
            wR = (ss, tok[("proj", "wWr", sc)])
            wE = (ss, tok[("proj", "wWw", sc)])
            wV = (ss, tok[("vtok", sc)])
            wF = (ss, tok[("vfe", sc)])
            tL0 = p.op("v", "memset", bass.AP(Lsb, 0, [[FT * 65, 128], [65, FT], [1, 1]]), 0.0,
                       waits=[(sv, tok.get(("Lfree", gi - 1), 0)),
                              (ss, tok.get(("Lfree_ss", gi - 1), 0))], inc=sv)
            tsc_ = 0
            for ct in range(FT):
                tsc_ = p.op("v", "tensor_tensor_scan", Lsb[:, ct * 65 + 1:ct * 65 + 65],
                            onesf[:, 0:C], ewsb[:, ct * 512 + co0:ct * 512 + co0 + C],
                            0.0, OP.mult, OP.subtract,
                            waits=[(sv, tL0), (sv, t_onesf), wE], inc=sv)
            t_qe = p.op("s", "activation", qe[:], bass.AP(Lsb, 0, [[FT * 65, 128], [65, FT], [1, C]]),
                        AF.Exp, waits=[(sv, tsc_), (sv, tok.get(("qefree", gi - 1), 0))], inc=ss)
            t_qh = p.op("v", "tensor_tensor", qh[:], qe[:], vchunk(rsb, co0, C), OP.mult,
                        waits=[(ss, t_qe), wR, (sm, tok.get(("qhfree", gi - 1), 0))], inc=sv)
            tok[("qefree", gi)] = t_qh
            t_dt8 = p.op("s", "activation", dt8[:], bass.AP(Lsb, 64, [[FT * 65, 128], [65, FT], [1, 1]]),
                         AF.Exp, waits=[(sv, tsc_), (sv, tok.get(("dt8free", gi - 1), 0))], inc=ss)
            tok[("Lfree_ss", gi)] = t_dt8
            t_qc = 0
            gqc = tok.get(("qcsp", gi - 1))
            for ft in range(FT):
                t_qc = p.op("v", "tensor_scalar", qcch[:, ft * C:(ft + 1) * C],
                            qh[:, ft * C:(ft + 1) * C], dcum[:, ft:ft + 1], None, OP.mult,
                            waits=([(sv, t_qh), (sv, t_dc)] + ([gqc] if gqc else []))
                            if ft == 0 else [(sv, t_qh)], inc=sv)
            t_ka = p.op("v", "tensor_scalar", ka[:], bass.AP(Lsb, 1, [[FT * 65, 128], [65, FT], [1, C]]),
                        -1.0, 60.0, OP.mult, OP.min, waits=[(sv, tsc_), (ss, tok.get(("kafree", gi - 1), 0))], inc=sv)
            t_ke = p.op("s", "activation", kt[:], ka[:], AF.Exp,
                        waits=[(sv, t_ka), (sm, tok.get(("ktfree", gi - 1), 0))], inc=ss)
            t_kt = p.op("v", "tensor_tensor", kt[:], kt[:], vchunk(ksb, co0, C), OP.mult,
                        waits=[(ss, t_ke), wK], inc=sv)
            t_kha = p.op("v", "tensor_tensor", ka[:],
                         bass.AP(Lsb, 64, [[FT * 65, 128], [65, FT], [0, C]]),
                         bass.AP(Lsb, 1, [[FT * 65, 128], [65, FT], [1, C]]), OP.subtract,
                         waits=[(sv, t_ka), (ss, t_ke)], inc=sv)
            tok[("kafree", gi)] = t_ke
            t_khe = p.op("s", "activation", kh[:], ka[:], AF.Exp,
                         waits=[(sv, t_kha), (sm, tok.get(("khfree", gi - 1), 0))], inc=ss)
            tok[("kafree", gi)] = t_khe
            t_kh = p.op("v", "tensor_tensor", kh[:], kh[:], vchunk(ksb, co0, C), OP.mult,
                        waits=[(ss, t_khe), wK], inc=sv)
            tok[("Lfree", gi)] = t_kha
            tok[("khfree2", gi)] = t_kh
            tok[("scanlast", gi)] = t_kha
            t_rk = p.op("v", "tensor_tensor", rku[:], vchunk(rsb, co0, C), vchunk(ksb, co0, C),
                        OP.mult, waits=[wR, wK, (sm, tok.get(("rkufree", gi - 1), 0))], inc=sv)
            tok[("rkufree2", gi)] = t_rk
            t_ru = 0
            for ft in range(FT):
                t_ru = p.op("v", "tensor_scalar", rku[:, ft * C:(ft + 1) * C],
                            rku[:, ft * C:(ft + 1) * C], uv[:, ft:ft + 1], None, OP.mult,
                            waits=[(sv, t_rk), t_vec] if ft == 0 else [(sv, t_rk)], inc=sv)
            ro = co0 % 128
            t_gr = 0
            for h in range(H):
                ct, hh = h // 2, h % 2
                bank = 2 + h // 8
                t_gr = p.op("t", "matmul", prow(pb[bank], ro, C, (h % 8) * C, C),
                            prow(kt, hh * 64, 64, ct * C, C),
                            prow(qh, hh * 64, 64, ct * C, C),
                            start=(h % 8 == 0), stop=(h % 8 == 7), skip_group_check=(ro == 64),
                            waits=[(sv, t_kt), (sv, t_qh), (sv, tok.get(("grfree", gi - 1), 0))],
                            inc=sm)
            tok[("ktfree", gi)] = t_gr
            tok[("qhfree", gi)] = 0
            t_am1 = p.op("v", "tensor_tensor", prow(amsb, ro, C, 0, 8 * C), prow(pb[2], ro, C, 0, 8 * C),
                         mk64[:, 0:8 * C], OP.mult,
                         waits=[(sm, t_gr), t_vec, (sm, tok.get(("amfree", gi - 1), 0))], inc=sv)
            t_am = p.op("v", "tensor_tensor", prow(amsb, ro, C, 8 * C, 8 * C), prow(pb[3], ro, C, 0, 8 * C),
                        mk64[:, 8 * C:16 * C], OP.mult, waits=[(sm, t_gr)], inc=sv)
            t_rho = 0
            for ct in range(FT):
                t_rho = p.op("t", "matmul", prow(pb[4], 0, 16, 0, C), hsel[:, ct * 16:(ct + 1) * 16],
                             rku[:, ct * C:(ct + 1) * C], start=(ct == 0), stop=(ct == FT - 1),
                             waits=[(sv, t_ru), t_vec, (ss, tok.get(("rhofree", gi - 1), 0))], inc=sm)
            tok[("rkufree", gi)] = t_rho
            t_rhoe = p.op("s", "copy", rhosb[:], prow(pb[4], 0, 16, 0, C),
                          waits=[(sm, t_rho)], inc=ss)
            tok[("rhofree", gi)] = t_rhoe
            t_rep = 0
            for ct in range(FT):
                t_rep = p.op("t", "matmul", pb[5][:, ct * C:(ct + 1) * C], rsel[:, ct * 128:(ct + 1) * 128], rhosb[:],
                             start=(ct == 0), stop=(ct == FT - 1),
                             waits=[(ss, t_rhoe), t_vec, (sv, tok.get(("repfree", gi - 1), 0))], inc=sm)
            t_y = 0
            for h in range(H):
                ct, hh = h // 2, h % 2
                tb = co0 // 128
                t_y = p.op("t", "matmul", prow(pb[6], hh * 64, 64, ct * C, C),
                           prow(vtok, ro, 64, tb * 1024 + h * 64, 64),
                           prow(amsb, ro, C, (h % 8) * C + (h // 8) * 8 * C, C),
                           start=(h <= 1), stop=False, skip_group_check=(hh == 1),
                           waits=[(sv, t_am), (sv, t_am1), wV,
                                  (sv, tok.get(("yfree", gi - 1), 0))], inc=sm)
            for h in range(H):
                ct, hh = h // 2, h % 2
                t_y = p.op("t", "matmul", prow(pb[6], hh * 64, 64, ct * C, C),
                           prow(Sbb[ct], hh * 64, 64, 0, NH),
                           prow(qh, hh * 64, 64, ct * C, C),
                           start=False, stop=(h >= H - 2), skip_group_check=(hh == 1),
                           waits=[(sv, t_qh), (sv, t_chain)], inc=sm)
            tok[("qhfree", gi)] = t_y
            tok[("amfree", gi)] = t_y
            for rnd in range(2):
                t_tr = 0
                for j in range(4):
                    ct = rnd * 4 + j
                    gv = tok.get(("khtev", gi, rnd - 1)) or tok.get(("khtuse", gi - 1))
                    t_tr = p.op("t", "transpose", prow(pbT, ro, C, j * 128, 128),
                                kh[:, ct * C:(ct + 1) * C], identb[:],
                                waits=[(sv, t_kh)] + ([gv] if (gv and j == 0) else []), inc=sm)
                tok[("khtev", gi, rnd)] = (ss, p.op("s", "copy", prow(khtok, ro, C, rnd * 512, 512),
                                                    prow(pbT, ro, C, 0, 512),
                                                    waits=[(sm, t_tr)], inc=ss))
            tok[("khfree", gi)] = t_tr
            t_khtok = tok[("khtev", gi, 1)]
            t_ds = 0
            for h in range(H):
                ct, hh = h // 2, h % 2
                bank = 2 + ct // 4
                tb = co0 // 128
                t_ds = p.op("t", "matmul", prow(pb[bank], hh * 64, 64, (ct % 4) * 64, 64),
                            prow(khtok, ro, C, ct * 128 + hh * 64, 64),
                            prow(vtok, ro, C, tb * 1024 + h * 64, 64),
                            start=(h % 8 <= 1), stop=(h % 8 >= 6), skip_group_check=(hh == 1),
                            waits=[t_khtok, wV, (sv, t_am), (sv, t_am1),
                                   (sv, tok.get(("dsfree", gi - 1), 0))], inc=sm)
            tok[("khtuse", gi)] = (sm, t_ds)
            t_su = 0
            for ct in range(FT):
                bank = 2 + ct // 4
                t_su = p.op("v", "scalar_tensor_tensor", Ssb[ct][:], Ssb[ct][:],
                            dt8[:, ct:ct + 1], prow(pb[bank], 0, 128, (ct % 4) * 64, 64),
                            OP.mult, OP.add,
                            waits=[(ss, t_dt8), (sm, t_ds), (sv, t_chain)] if ct == 0
                            else [(sm, t_ds)], inc=sv)
            for ct in range(FT):
                t_su = p.op("v", "tensor_copy", Sbb[ct][:], Ssb[ct][:],
                            waits=[(sv, t_su)] if ct == 0 else [], inc=sv)
            t_chain = t_su
            tok[("dsfree", gi)] = t_su
            tok[("grfree", gi)] = t_su
            t_dc = p.op("v", "tensor_tensor", dcum[:], dcum[:], dt8[:], OP.mult,
                        waits=[(sv, t_qc), (ss, t_dt8)], inc=sv)
            tok[("dt8free", gi)] = t_dc
            t_d1 = p.op("v", "tensor_tensor", dtmp[:], pb[5][:, 0:8 * C], vchunk(vfe, co0, C),
                        OP.mult, waits=[(sm, t_rep), wF, (sv, tok.get(("dtmpfree", gi - 1), 0))],
                        inc=sv)
            tok[("repfree", gi)] = t_d1
            gys = tok.get(("ysp", gi - 1))
            t_ye = p.op("v", "tensor_tensor", yraw[:], prow(pb[6], 0, 128, 0, 8 * C), dtmp[:],
                        OP.add, waits=[(sm, t_y), (sv, t_d1)] + ([gys] if gys else []), inc=sv)
            tok[("dtmpfree", gi)] = t_ye
            tok[("yfree", gi)] = t_ye
            t_sq = qct.load(vchunk(dr["qc_sp"], sc * 512 + co0, C, step=T, pitch=FT * T),
                            qcch[:], waits=[(sv, t_qc)])
            tok[("qcsp", gi)] = t_sq
            t_sy = qyt.load(vchunk(dr["yraw_sp"], sc * 512 + co0, C, step=T, pitch=FT * T),
                            yraw[:], waits=[(sv, t_ye)])
            tok[("ysp", gi)] = t_sy
        lg = sc * NCH + NCH - 1
        tok[("vtuse", sc)] = (sv, tok[("dsfree", lg)])
        tok[("vfuse", sc)] = (sv, tok[("dtmpfree", lg)])
        tok[("wWkuse", sc)] = (sv, tok[("khfree2", lg)])
        tok[("wWruse", sc)] = (sv, tok[("rkufree2", lg)])
        tok[("wWguse", sc)] = t_gsp
        tok[("wWwuse", sc)] = (sv, tok[("scanlast", lg)])

    sst = p.stream("sste", 1)
    for ft in range(FT):
        t_se = sst.load(dra["s_snd"][ft], Ssb[ft][:], eng="g",
                        waits=[(sv, t_chain)] if ft == 0 else (), adv=False)
    t_cc2 = p.op("g", "collective_compute", "AllGather", OP.bypass, replica_groups=RG_PAIRS,
                 ins=[dra["s_snd"]], outs=[dra["s_rcv"]], waits=[t_se], inc=sg)
    sinb = p.sb("sinb", [128, FT * NH], BF16)
    t_si = 0
    for ft in range(FT):
        t_si = p.dma("g", sinb[:, ft * NH:(ft + 1) * NH], dra["s_rcv"][0][ft],
                     waits=[(sg, t_cc2)] if ft == 0 else ())
    t_sin0 = 0
    for ft in range(FT):
        t_sin0 = p.op("v", "tensor_scalar", sinb[:, ft * NH:(ft + 1) * NH],
                      sinb[:, ft * NH:(ft + 1) * NH], hmv[:], None, OP.mult,
                      waits=[(p.gld, t_si), t_vec] if ft == 0 else [], inc=sv)
    t_sin = (sv, t_sin0)

    p.glob["t_sin"] = t_sin
    p.glob["qst_done"] = qgt.done() + qct.done() + qyt.done()
    p.glob["sinb"] = sinb

def rwkv_post(p, io, dbg):
    nc, T = p.nc, p.T
    FT = 8
    NSC = T // 512
    C = 64
    NCH = 512 // C
    acc, xn0, pb = p.glob["acc"], p.glob.get("xn0"), p.glob["pb"]
    pbT = p.glob["pbT"]
    identb = p.glob["identb"]
    sv, ss, sm, sg = p.sv, p.ss, p.sm, p.sg
    dr, dra = p.glob["dr"], p.glob["dra"]
    lnw, lnb, tdv, uv, hmv = p.glob["lnw"], p.glob["lnb"], p.glob["tdv"], p.glob["uv"], p.glob["hmv"]
    mk64, hsel, rsel, sgm = p.glob["mk64"], p.glob["hsel"], p.glob["rsel"], p.glob["sgm"]
    t_vec, t_sgm = p.glob["t_vec"], p.glob["t_sgm"]
    onesb, onesf, t_ones, t_onesf = p.glob["onesb"], p.glob["onesf"], p.glob["t_ones"], p.glob["t_onesf"]
    tok = p.glob["rtok"]

    t_sin = p.glob["t_sin"]
    qdone = p.glob["qst_done"]
    sinb = p.glob["sinb"]
    dtmp = p.sb("dtmp2", [128, 8 * C], BF16)
    Wo_sb = p.sb("Wo_sb", [128, FT * 1024], BF16)
    wost = p.stream("wost", 1)
    for ci in range(FT):
        t_wo = wost.load(Wo_sb[:, ci * 1024:(ci + 1) * 1024],
                         io["wWo"][ci * 128:(ci + 1) * 128, :], adv=False)
    ysc = p.sb("ysc", [128, FT * 512], BF16)
    qcsc = p.sb("qcsc", [128, FT * 512], BF16)
    gsc2 = p.sb("gsc2", [128, FT * 512], BF16)
    pst2 = p.stream("pp", 1)
    for sc in range(NSC):
        wprev = [(sv, tok[("y2done", sc - 1)])] if sc else []
        t_ly = pst2.load(ysc[:], vchunk(dr["yraw_sp"], sc * 512, 512, step=T, pitch=FT * T),
                         waits=(qdone + wprev) if sc == 0 else wprev, adv=False)
        t_lq = pst2.load(qcsc[:], vchunk(dr["qc_sp"], sc * 512, 512, step=T, pitch=FT * T),
                         adv=False)
        t_lg = pst2.load(gsc2[:], vchunk(dr["g_sp"], sc * 512, 512, step=T, pitch=FT * T),
                         adv=False)
        t_l = t_lg
        for cc in range(NCH):
            co0 = cc * C
            t_cm = 0
            for h in range(H):
                ct, hh = h // 2, h % 2
                t_cm = p.op("t", "matmul", prow(pb[6], hh * 64, 64, ct * C, C),
                            prow(sinb, hh * 64, 64, ct * NH, NH),
                            prow(qcsc, hh * 64, 64, ct * 512 + co0, C),
                            start=(h <= 1), stop=(h >= H - 2), skip_group_check=(hh == 1),
                            waits=[t_sin, t_l, (sv, tok.get(("y2e", sc, cc - 1),
                                                            tok.get(("pb6free", sc - 1), 0)))],
                            inc=sm)
            t_ad = p.op("v", "tensor_tensor", dtmp[:], prow(pb[6], 0, 128, 0, 8 * C),
                        vchunk(ysc, co0, C), OP.add,
                        waits=[(sm, t_cm), t_l, (sv, tok.get(("y2e", sc, cc - 1), 0))], inc=sv)
            t_y2 = p.op("v", "tensor_tensor", vchunk(ysc, co0, C), dtmp[:],
                        vchunk(gsc2, co0, C), OP.mult, waits=[(sv, t_ad)], inc=sv)
            tok[("y2e", sc, cc)] = t_y2
        tok[("pb6free", sc)] = tok[("y2e", sc, NCH - 1)]
        t_acc = 0
        for co in range(FT):
            t_mm = 0
            for ci in range(FT):
                t_mm = p.op("t", "matmul", pb[0][:, 0:512],
                            Wo_sb[:, ci * 1024 + co * 128:ci * 1024 + co * 128 + 128],
                            ysc[:, ci * 512:(ci + 1) * 512],
                            start=(ci == 0), stop=(ci == FT - 1),
                            waits=[(sv, tok[("y2e", sc, NCH - 1)]), t_wo,
                                   (sv, tok.get(("accadd", sc, co - 1),
                                                tok.get(("accadd", sc - 1, FT - 1), 0)))],
                            inc=sm)
            t_acc = p.op("v", "tensor_tensor", acc[co][:, sc * 512:(sc + 1) * 512],
                         acc[co][:, sc * 512:(sc + 1) * 512], pb[0][:, 0:512], OP.add,
                         waits=[(sm, t_mm)], inc=sv)
            tok[("accadd", sc, co)] = t_acc
        tok[("y2done", sc)] = t_acc
    p.glob["attn_done"] = (sv, sv.n)



def phase_ffn0(p, io, dbg):
    nc, T = p.nc, p.T
    FT = 8
    IT = DI // 128
    NTC = T // 512
    acc, pb, pbT = p.glob["acc"], p.glob["pb"], p.glob["pbT"]
    sv, ss, sm, sg = p.sv, p.ss, p.sm, p.sg
    dr, dra = p.glob["dr"], p.glob["dra"]
    identb = p.glob["identb"]
    onesf = p.glob["onesf"]
    t_onesf = p.glob["t_onesf"]
    tok = {}

    h_sp = nc.dram_tensor("h_sp", [IT, 128, T], BF16,
                          kind="ExternalOutput" if dbg else "Internal")
    hra = h_sp.ap()
    ar2_in = nc.dram_tensor("ar2_in", [1, 1], F32, kind="Internal").ap()
    ar2_out = nc.dram_tensor("ar2_out", [1, 1], F32, kind="Internal").ap()

    sc1 = p.sb("fsc", [1, 8], F32)
    scb = p.sb("fscb", [128, 8], F32)
    cst = p.stream("fcl", 1)
    t_cx = cst.load(sc1[0:1, 0:1], io["clip_x"], adv=False)
    t_cd = cst.load(sc1[0:1, 1:2], io["clip_down"], adv=False)
    t_am = cst.load(sc1[0:1, 2:3], dra["armax_out"], adv=False,
                    waits=[p.glob["armax_tok"]])
    t_s1 = p.op("s", "activation", sc1[0:1, 3:4], sc1[0:1, 0:1], AF.Sigmoid,
                waits=[t_am], inc=ss)
    t_s2 = p.op("v", "tensor_tensor", sc1[0:1, 4:5], sc1[0:1, 3:4], sc1[0:1, 2:3],
                OP.mult, waits=[(ss, t_s1)], inc=sv)
    t_s3 = p.op("v", "reciprocal", sc1[0:1, 5:6], sc1[0:1, 4:5], waits=[(sv, t_s2)], inc=sv)
    t_s4 = p.op("v", "tensor_scalar", sc1[0:1, 5:6], sc1[0:1, 5:6], 127.0, None, OP.mult,
                waits=[(sv, t_s3)], inc=sv)
    t_s5 = p.op("v", "tensor_scalar", sc1[0:1, 4:5], sc1[0:1, 4:5], 1.0 / 127.0, None,
                OP.mult, waits=[(sv, t_s4)], inc=sv)
    t_bc = p.op("t", "matmul", pb[3][:, 0:2],
                onesf[0:1, 0:128], sc1[0:1, 4:6], start=True, stop=True,
                waits=[(sv, t_s5), (sv, t_onesf), p.glob["attn_done"]], inc=sm)
    t_bce = p.op("s", "copy", scb[:, 0:2], pb[3][:, 0:2], waits=[(sm, t_bc)], inc=ss)

    wrow = [p.sb("fwr0", [128, max(D, DI)], F32), p.sb("fwr1", [128, max(D, DI)], F32)]
    wi32 = p.sb("fwi", [128, max(D, DI)], I32)
    wtmp = p.sb("fwt", [128, max(D, DI)], F32)
    wqrow = p.sb("fwq", [128, max(D, DI)], BF16)
    rmax = p.sb("frm", [128, 4], F32)
    wst = p.stream("fwl", 2)
    with p.scope():
        xq = p.sb("xq", [128, FT * T], BF16)
        xb0 = [p.sb(f"xb0_{j}", [128, T], BF16) for j in range(2)]
        xi32 = p.sb("xi32", [128, T], I32)
        xf1 = p.sb("xf1", [128, T], F32)
        xf2 = p.sb("xf2", [128, T], F32)
        xst = p.stream("fxl", 2)
        for ft in range(FT):
            t_lx = xst.load(xb0[ft % 2][:], dra["xn0_sp"][ft],
                            waits=[(sv, tok.get(("xq", ft - 2), 0))] if ft >= 2 else [])
            t1 = p.op("v", "tensor_scalar", xf2[:], xb0[ft % 2][:], scb[:, 1:2], 512.0, OP.mult, OP.add,
                      waits=[t_lx, (ss, t_bce), (sv, tok.get(("xq", ft - 1), 0))], inc=sv)
            t2 = p.op("v", "tensor_copy", xi32[:], xf2[:], waits=[(sv, t1)], inc=sv)
            t3 = p.op("v", "tensor_copy", xf1[:], xi32[:], waits=[(sv, t2)], inc=sv)
            t4 = p.op("v", "tensor_tensor", xf2[:], xf2[:], xf1[:], OP.subtract,
                      waits=[(sv, t3)], inc=sv)
            t5 = p.op("v", "scalar_tensor_tensor", xf1[:], xf2[:], 0.5, xf1[:], OP.is_ge,
                      OP.add, waits=[(sv, t4)], inc=sv)
            t6 = p.op("v", "tensor_scalar", xf1[:], xf1[:], -512.0, -128.0, OP.add, OP.max,
                      waits=[(sv, t5)], inc=sv)
            t7 = p.op("v", "tensor_scalar", xq[:, ft * T:(ft + 1) * T], xf1[:], 127.0, None,
                      OP.min, waits=[(sv, t6)], inc=sv)
            tok[("xq", ft)] = t7
        t_xq = tok[("xq", FT - 1)]


        def quant_row_tile(buf, ncols, t_lw, scale_col, gate_prev=None):
            tq0 = p.op("v", "reduce_max", rmax[:, scale_col:scale_col + 1], buf[:, 0:ncols],
                       axis=AX.X, apply_absolute_value=True,
                       waits=[t_lw], inc=sv)
            tq1 = p.op("v", "reciprocal", rmax[:, 3:4], rmax[:, scale_col:scale_col + 1],
                       waits=[(sv, tq0)], inc=sv)
            tq2 = p.op("v", "tensor_scalar", rmax[:, 3:4], rmax[:, 3:4], 127.0, None, OP.mult,
                       waits=[(sv, tq1)], inc=sv)
            tq3 = p.op("v", "tensor_scalar", wtmp[:, 0:ncols], buf[:, 0:ncols], rmax[:, 3:4],
                       512.0, OP.mult, OP.add, waits=[(sv, tq2)], inc=sv)
            tq4 = p.op("v", "tensor_copy", wi32[:, 0:ncols], wtmp[:, 0:ncols],
                       waits=[(sv, tq3)], inc=sv)
            tq5 = p.op("v", "tensor_copy", buf[:, 0:ncols], wi32[:, 0:ncols],
                       waits=[(sv, tq4)], inc=sv)
            tq6 = p.op("v", "tensor_tensor", wtmp[:, 0:ncols], wtmp[:, 0:ncols], buf[:, 0:ncols],
                       OP.subtract, waits=[(sv, tq5)], inc=sv)
            tq7 = p.op("v", "scalar_tensor_tensor", buf[:, 0:ncols], wtmp[:, 0:ncols], 0.5,
                       buf[:, 0:ncols], OP.is_ge, OP.add, waits=[(sv, tq6)], inc=sv)
            tq8 = p.op("v", "tensor_scalar", wqrow[:, 0:ncols], buf[:, 0:ncols], -512.0, None,
                       OP.add, waits=[(sv, tq7), (sm, tok.get("wqrow_free", 0))], inc=sv)
            return tq8

        def transpose_to_arr(dst, ncolt, t_q, arr_guard=None, dst_off=0):
            t_e = tok.get("pbT", 0)
            for jb in range(0, ncolt, 4):
                nb = min(4, ncolt - jb)
                t_tr = 0
                for j in range(nb):
                    t_tr = p.op("t", "transpose", pbT[:, j * 128:(j + 1) * 128],
                                wqrow[:, (jb + j) * 128:(jb + j + 1) * 128], identb[:],
                                waits=[(sv, t_q), (ss, t_e)], inc=sm)
                t_e = p.op("s", "copy", dst[:, dst_off + jb * 128:dst_off + (jb + nb) * 128],
                           pbT[:, 0:nb * 128],
                           waits=[(sm, t_tr)] + ([arr_guard] if (arr_guard and jb == 0) else []),
                           inc=ss)
                tok["wqrow_free"] = t_tr
            tok["pbT"] = t_e
            return t_e

        wqg2 = [p.sb(f"wqg{j}", [128, FT * 128], BF16) for j in range(2)]
        wqu2 = [p.sb(f"wqu{j}", [128, FT * 128], BF16) for j in range(2)]
        sxg = p.sb("sxg", [128, 2], F32)
        hsb2 = [p.sb(f"hsb{j}", [128, 512], BF16) for j in range(2)]
        sgb2 = [p.sb(f"sgb{j}", [128, 512], BF16) for j in range(2)]
        hmx = p.sb("hmx", [128, IT + 2], F32)
        hst = p.stream("fhs", 2)
        for it in range(IT):
            t_lg = wst.load(wrow[0][:, 0:D], io["w_gate_r"][it * 128:(it + 1) * 128, :],
                            waits=[(sv, tok.get(("gq", it - 1), 0))])
            t_lu = wst.load(wrow[1][:, 0:D], io["w_up_r"][it * 128:(it + 1) * 128, :],
                            waits=[(sv, tok.get(("uq", it - 1), 0))])
            wqg = wqg2[it % 2]
            wqu = wqu2[it % 2]
            tqg = quant_row_tile(wrow[0], D, t_lg, 0)
            tok[("gq", it)] = tqg
            teg = transpose_to_arr(wqg, FT, tqg,
                                   (sm, tok.get(("gmm", it - 2), 0)) if it >= 2 else None)
            ts1 = p.op("v", "tensor_scalar", sxg[:, 0:1], rmax[:, 0:1], 1.0 / 127.0, None,
                       OP.mult, waits=[(sv, tqg)], inc=sv)
            ts2 = p.op("v", "tensor_tensor", sxg[:, 0:1], sxg[:, 0:1], scb[:, 0:1], OP.mult,
                       waits=[(sv, ts1), (ss, t_bce)], inc=sv)
            tqu = quant_row_tile(wrow[1], D, t_lu, 1)
            tok[("uq", it)] = tqu
            teu = transpose_to_arr(wqu, FT, tqu,
                                   (sm, tok.get(("umm", it - 2), 0)) if it >= 2 else None)
            ts3 = p.op("v", "tensor_scalar", sxg[:, 1:2], rmax[:, 1:2], 1.0 / 127.0, None,
                       OP.mult, waits=[(sv, tqu)], inc=sv)
            ts4 = p.op("v", "tensor_tensor", sxg[:, 1:2], sxg[:, 1:2], scb[:, 0:1], OP.mult,
                       waits=[(sv, ts3)], inc=sv)

            for tc in range(NTC):
                t_mmg = 0
                for ci in range(FT):
                    t_mmg = p.op("t", "matmul", pb[0][:, 0:512],
                                 wqg[:, ci * 128:(ci + 1) * 128],
                                 xq[:, ci * T + tc * 512:ci * T + tc * 512 + 512],
                                 start=(ci == 0), stop=(ci == FT - 1),
                                 waits=[(ss, teg), (sv, t_xq), (sv, tok.get(("hev", it, tc - 1), tok.get(("hev", it - 1, NTC - 1), 0)))],
                                 inc=sm)
                t_mmu = 0
                for ci in range(FT):
                    t_mmu = p.op("t", "matmul", pb[1][:, 0:512],
                                 wqu[:, ci * 128:(ci + 1) * 128],
                                 xq[:, ci * T + tc * 512:ci * T + tc * 512 + 512],
                                 start=(ci == 0), stop=(ci == FT - 1),
                                 waits=[(ss, teu), (sv, t_xq), (sv, tok.get(("hev", it, tc - 1), tok.get(("hev", it - 1, NTC - 1), 0)))],
                                 inc=sm)
                par = (it * NTC + tc) % 2
                hsb, sgb = hsb2[par], sgb2[par]
                t_sg = p.op("s", "activation", sgb[:], pb[0][:, 0:512], AF.Silu,
                            scale=sxg[:, 0:1],
                            waits=[(sm, t_mmg), (sv, ts2),
                                   tok.get(("hq2", it * NTC + tc - 2), (sv, 0))], inc=ss)
                t_h = p.op("v", "scalar_tensor_tensor", hsb[:], pb[1][:, 0:512], sxg[:, 1:2],
                           sgb[:], OP.mult, OP.mult,
                           waits=[(sm, t_mmu), (ss, t_sg), (sv, ts4)] +
                                 ([tok[("hst", (it * NTC + tc - 2) // NTC, (it * NTC + tc - 2) % NTC)]]
                                  if it * NTC + tc >= 2 else []), inc=sv)
                t_hm = p.op("v", "reduce_max", hmx[:, it:it + 1], hsb[:], axis=AX.X,
                            apply_absolute_value=True, waits=[(sv, t_h)],
                            inc=sv) if tc == 0 else p.op(
                    "v", "tensor_tensor_reduce" if False else "reduce_max",
                    hmx[:, IT:IT + 1], hsb[:], axis=AX.X, apply_absolute_value=True,
                    waits=[(sv, t_h)], inc=sv)
                if tc != 0:
                    t_hm = p.op("v", "tensor_tensor", hmx[:, it:it + 1], hmx[:, it:it + 1],
                                hmx[:, IT:IT + 1], OP.max, waits=[(sv, t_hm)], inc=sv)
                t_hs = hst.load(hra[it][:, tc * 512:(tc + 1) * 512], hsb[:], waits=[(sv, t_h)])
                tok[("hev", it, tc)] = t_h
                tok[("hst", it, tc)] = t_hs
                tok[("gmm", it)] = t_mmg
                tok[("umm", it)] = t_mmu
                tok[("hq2", it * NTC + tc)] = (sv, t_h)
        t_hm2 = p.op("v", "reduce_max", hmx[:, IT + 1:IT + 2], hmx[:, 0:IT], axis=AX.X,
                     apply_absolute_value=True, waits=[(sv, sv.n)], inc=sv)
        mxr2 = p.sb("mxr2", [1, 129], F32)
        ttr2 = p.op("t", "transpose", pb[2][0:1, 0:128], hmx[:, IT + 1:IT + 2],
                    p.glob["ident"][:], waits=[(sv, t_hm2)], inc=sm)
        tev2 = p.op("s", "copy", mxr2[0:1, 0:128], pb[2][0:1, 0:128], waits=[(sm, ttr2)], inc=ss)
        tpr2 = p.op("v", "reduce_max", mxr2[0:1, 128:129], mxr2[0:1, 0:128], axis=AX.X,
                    waits=[(ss, tev2)], inc=sv)
        tdm2 = p.dma("g", ar2_in, mxr2[0:1, 128:129], waits=[(sv, tpr2)])
        tcc2 = p.op("g", "collective_compute", "AllReduce", OP.max, replica_groups=RG_ALL,
                    ins=[ar2_in], outs=[ar2_out], waits=[(p.gld, tdm2)], inc=sg)
        t_h1 = cst.load(sc1[0:1, 6:7], ar2_out, adv=False, waits=[(sg, tcc2)])
        t_h2 = p.op("s", "activation", sc1[0:1, 1:2], sc1[0:1, 1:2], AF.Sigmoid,
                    waits=[t_h1], inc=ss)
        t_h3 = p.op("v", "tensor_tensor", sc1[0:1, 6:7], sc1[0:1, 6:7], sc1[0:1, 1:2],
                    OP.mult, waits=[(ss, t_h2)], inc=sv)
        t_h4 = p.op("v", "reciprocal", sc1[0:1, 7:8], sc1[0:1, 6:7], waits=[(sv, t_h3)], inc=sv)
        t_h5 = p.op("v", "tensor_scalar", sc1[0:1, 7:8], sc1[0:1, 7:8], 127.0, None, OP.mult,
                    waits=[(sv, t_h4)], inc=sv)
        t_h6 = p.op("v", "tensor_scalar", sc1[0:1, 6:7], sc1[0:1, 6:7], 1.0 / 127.0, None,
                    OP.mult, waits=[(sv, t_h5)], inc=sv)
        t_b2 = p.op("t", "matmul", pb[3][:, 2:4], onesf[0:1, 0:128], sc1[0:1, 6:8],
                    start=True, stop=True, waits=[(sv, t_h6), (sv, t_onesf), (ss, t_bce)], inc=sm)
        t_b2e = p.op("s", "copy", scb[:, 2:4], pb[3][:, 2:4], waits=[(sm, t_b2)], inc=ss)

        hq_sp = nc.dram_tensor("hq_sp", [IT, 128, T], BF16, kind="Internal")
        hqa = hq_sp.ap()
        hq2 = [p.sb(f"fhq{j}", [128, T], BF16) for j in range(2)]
        hf1 = xf1[:, 0:T]
        hf2 = xf2[:, 0:T]
        hqi = xi32[:, 0:T]
        hlst = p.stream("fhl", 2)
        hqst = p.stream("fhq", 2)
        for it in range(IT):
            t_lh = hlst.load(hq2[it % 2][:], hra[it][:, 0:T],
                             waits=[tok[("hst", it, NTC - 1)],
                                    tok.get(("hqs", it - 2), (sv, 0)),
                                    (sv, tok.get(("hqq", it - 2), 0))])
            q1 = p.op("v", "tensor_scalar", hf2, hq2[it % 2][:], scb[:, 3:4], 512.0,
                      OP.mult, OP.add, waits=[t_lh, (ss, t_b2e)], inc=sv)
            q2 = p.op("v", "tensor_copy", hqi, hf2, waits=[(sv, q1)], inc=sv)
            q3 = p.op("v", "tensor_copy", hf1, hqi, waits=[(sv, q2)], inc=sv)
            q4 = p.op("v", "tensor_tensor", hf2, hf2, hf1, OP.subtract,
                      waits=[(sv, q3)], inc=sv)
            q5 = p.op("v", "scalar_tensor_tensor", hf1, hf2, 0.5, hf1,
                      OP.is_ge, OP.add, waits=[(sv, q4)], inc=sv)
            q6 = p.op("v", "tensor_scalar", hf1, hf1, -512.0, -128.0, OP.add,
                      OP.max, waits=[(sv, q5)], inc=sv)
            q7 = p.op("v", "tensor_scalar", hq2[it % 2][:], hf1, 127.0, None, OP.min,
                      waits=[(sv, q6), tok.get(("hqs", it - 2), (sv, 0))], inc=sv)
            tok[("hqq", it)] = q7
            tok[("hqs", it)] = hqst.load(hqa[it][:, 0:T], hq2[it % 2][:],
                                         waits=[(sv, q7)])
        hq_done = hqst.done()

    wqd = p.sb("wqd", [128, FT * IT * 128], BF16)
    sdh = p.sb("sdh", [128, 8], F32)
    for co in range(FT):
        wg0 = ([(sv, tok[("gq", IT - 1)])] if co == 0 else
               [(sv, tok[("uq", IT - 1)])] if co == 1 else [])
        t_ld = wst.load(wrow[co % 2][:, 0:DI], io["w_down_r"][co * 128:(co + 1) * 128, :],
                        waits=[(sv, tok.get(("dq", co - 2), 0))] + wg0)
        tqd = quant_row_tile(wrow[co % 2], DI, t_ld, 0)
        tok[("dq", co)] = tqd
        ted = transpose_to_arr(wqd, IT, tqd, dst_off=co * IT * 128)
        td1 = p.op("v", "tensor_scalar", sdh[:, co:co + 1], rmax[:, 0:1], 1.0 / 127.0,
                   None, OP.mult, waits=[(sv, tqd)], inc=sv)
        td2 = p.op("v", "tensor_tensor", sdh[:, co:co + 1], sdh[:, co:co + 1], scb[:, 2:3],
                   OP.mult, waits=[(sv, td1), (ss, t_b2e)], inc=sv)
        tok[("dsc", co)] = td2
        tok[("dte", co)] = ted
    hrd = [p.sb("fhr0", [128, IT * 512], BF16)]
    hrst = p.stream("fhr", 2)
    for tc in range(NTC):
        wfree = ([(sm, tok[("dmm", tc - 1, FT - 1)])] if tc >= 1 else [])
        t_lq = 0
        for it in range(IT):
            t_lq = hrst.load(hrd[0][:, it * 512:(it + 1) * 512],
                             hqa[it][:, tc * 512:(tc + 1) * 512],
                             waits=(wfree if it == 0 else [])
                             + (hq_done if (tc == 0 and it == 0) else []),
                             adv=(it == IT - 1))
        for co in range(FT):
            bank = co % 4
            t_mm = 0
            for it in range(IT):
                t_mm = p.op("t", "matmul", pb[bank][:, 0:512],
                            wqd[:, co * IT * 128 + it * 128:co * IT * 128 + (it + 1) * 128],
                            hrd[0][:, it * 512:(it + 1) * 512],
                            start=(it == 0), stop=(it == IT - 1),
                            waits=[(ss, tok[("dte", co)]), t_lq]
                            + ([(sv, tok.get(("dacc", tc, co - 4),
                                             tok.get(("dacc", tc - 1, co + 4), 0)))]
                               if it == 0 else []),
                            inc=sm)
            tok[("dmm", tc, co)] = t_mm
            t_ac = p.op("v", "scalar_tensor_tensor", acc[co][:, tc * 512:(tc + 1) * 512],
                        pb[bank][:, 0:512], sdh[:, co:co + 1],
                        acc[co][:, tc * 512:(tc + 1) * 512],
                        OP.mult, OP.add,
                        waits=[(sm, t_mm), (sv, tok[("dsc", co)]), p.glob["attn_done"]],
                        inc=sv)
            tok[("dacc", tc, co)] = t_ac
    p.glob["ffn0_done"] = (sv, sv.n)


def phase_l1pre(p, io, dbg):
    nc, T = p.nc, p.T
    FT = 8
    NTC = T // 512
    acc, pb, pbT = p.glob["acc"], p.glob["pb"], p.glob["pbT"]
    sv, ss, sm, sg = p.sv, p.ss, p.sm, p.sg
    identb, ident = p.glob["identb"], p.glob["ident"]
    onesb = p.glob["onesb"]
    t_ones = p.glob["t_ones"]
    hmv = p.glob["hmv"]
    tok = {}

    xc_sp = nc.dram_tensor("xc_int", [16, 128, T], BF16, kind="Internal")
    xca = xc_sp.ap()
    if dbg:
        xc_dbg = nc.dram_tensor("xc_sp", [16, 128, T], BF16, kind="ExternalOutput")
    xc_ag = nc.dram_tensor("xc_ag", [2, 16, 128, T], BF16, kind="Internal")
    res_sp = nc.dram_tensor("res_sp", [16, 128, T], BF16,
                            kind="ExternalOutput" if dbg else "Internal")
    resa = res_sp.ap()
    p.glob["res_sp"] = res_sp
    x13_snd_h = nc.dram_tensor("x13_snd", [3, D], F32, kind="Internal")
    x13_snd = x13_snd_h.ap()
    x13_rcv = nc.dram_tensor("x13_rcv", [2, 3, D], F32, kind="Internal").ap()
    p.glob["xc_ag"] = xc_ag
    p.glob["xc_sp"] = xc_sp
    if dbg:
        dbg_xn1 = nc.dram_tensor("dbg_xn1", [FT, 128, T], BF16, kind="ExternalOutput").ap()
        dbg_xm = nc.dram_tensor("dbg_xm", [FT, 128, T], BF16, kind="ExternalOutput").ap()
    ffn0_done = p.glob["ffn0_done"]

    hst = p.stream("l1h", 1)
    h3s = p.stream("l1h3", 1)
    dbx = p.stream("l1dx", 1)
    dbm = p.stream("l1dm", 1)
    for ft in range(FT):
        p.op("g", "dma_start", bass.AP(x13_snd_h, ft * 128, [[1, 128], [D, 3]]),
             bass.AP(acc[ft], T - 3, [[T, 128], [1, 3]]),
             waits=[ffn0_done] if ft == 0 else (), inc=h3s.sems[0], inck=16,
             allow_slow_non_contiguous=True)
    t_halo = (h3s.sems[0], h3s.sems[0].n)
    t_cc3 = p.op("g", "collective_compute", "AllGather", OP.bypass, replica_groups=RG_PAIRS,
                 ins=[x13_snd], outs=[x13_rcv], waits=[t_halo], inc=sg)
    xh = p.sb("xh", [3, D], F32)
    xhs = p.stream("l1xh", 1)
    t_lh3 = xhs.load(xh[:], x13_rcv[0], eng="g", waits=[(sg, t_cc3)], adv=False)

    xm = [p.sb(f"xm_{i}", [128, T], BF16) for i in range(FT)]
    wwin = p.sb("l1win", [128, FT * NH], BF16)
    wwout = p.sb("l1wo", [NH, D], BF16)
    binv = p.sb("l1bin", [NH, 1], F32)
    b2v = p.sb("l1b2", [NH, 1], F32)
    boutv = p.sb("l1bo", [128, 8], F32)
    nwv = p.sb("l1nw", [128, 8], F32)
    w1f = p.sb("l1w1f", [NH, 16], F32)
    w2f = p.sb("l1w2f", [NH, 16], F32)
    b1f = p.sb("l1b1f", [NH, 16], F32)
    W1b = p.sb("l1w1", [NH, 16], BF16)
    W2b = p.sb("l1w2", [NH, 16], BF16)
    b1b = p.sb("l1b1", [NH, 16], BF16)
    cwv = p.sb("l1cw", [128, 16 * 4], F32)
    cbv = p.sb("l1cb", [128, 16], F32)
    vst = p.stream("l1v", 1)
    for ci in range(FT):
        vst.load(wwin[:, ci * NH:(ci + 1) * NH],
                 io["w_win"][ci * 128:(ci + 1) * 128, :], adv=False)
    vst.load(wwout[:], io["w_wout"], adv=False)
    vst.load(binv[:], io["bin_"], adv=False)
    vst.load(b2v[:], io["b2"], adv=False)
    vst.load(boutv[:], io["bout"], adv=False)
    vst.load(nwv[:], io["norm_w"], adv=False)
    vst.load(w1f[:], io["W1"], adv=False)
    vst.load(w2f[:], io["W2"], adv=False)
    vst.load(b1f[:], io["b1"], adv=False)
    for ct in range(16):
        vst.load(cwv[:, ct * 4:(ct + 1) * 4], io["conv_w"][ct], adv=False)
    t_vl = vst.load(cbv[:], io["conv_b"], adv=False)
    t_c1 = p.op("v", "tensor_copy", W1b[:], w1f[:], waits=[t_vl], inc=sv)
    t_c2 = p.op("v", "tensor_copy", W2b[:], w2f[:], inc=sv)
    t_cv = p.op("v", "tensor_copy", b1b[:], b1f[:], inc=sv)

    xn1 = [p.sb(f"xn1_{i}", [128, 512], BF16) for i in range(FT)]
    xbb = p.sb("l1xb", [128, 512], BF16)
    sqb = p.sb("l1sq", [128, 512], BF16)
    meanb = p.sb("l1mn", [128, 512], F32)
    rstdb = p.sb("l1rs", [128, 512], F32)
    vtmp = p.sb("l1vt", [128, 512], F32)
    rrb = p.sb("l1rr", [128, 512], F32)
    sffb = p.sb("l1sf", [NH, 512], BF16)
    hfb = p.sb("l1hf", [NH, 512 * 16], BF16)
    g1b = p.sb("l1g1", [NH, 512], BF16)
    g1f = p.sb("l1g1f", [NH, 512], F32)
    for tc in range(NTC):
        c0 = tc * 512
        gprev = (sv, tok.get(("xmd", tc - 1), 0))
        t_mm1 = 0
        t_mm2 = 0
        t_sq = 0
        for ft in range(FT):
            t_xb = p.op("v", "tensor_copy", xbb[:], acc[ft][:, c0:c0 + 512],
                        waits=[ffn0_done, (sm, t_mm1), (ss, t_sq), gprev], inc=sv)
            t_sq = p.op("s", "activation", sqb[:], xbb[:], AF.Square,
                        waits=[(sv, t_xb), (sm, t_mm2)], inc=ss)
            t_mm1 = p.op("t", "matmul", pb[4][:, 0:512], onesb[:], xbb[:],
                         start=(ft == 0), stop=(ft == FT - 1),
                         waits=[(sv, t_xb), (sv, t_ones), (ss, tok.get(("st1", tc - 1), 0))],
                         inc=sm)
            t_mm2 = p.op("t", "matmul", pb[5][:, 0:512], onesb[:], sqb[:],
                         start=(ft == 0), stop=(ft == FT - 1),
                         waits=[(ss, t_sq), (sv, tok.get(("st2", tc - 1), 0))], inc=sm)
        t_s1 = p.op("s", "activation", meanb[:], pb[4][:, 0:512], AF.Copy,
                    scale=1.0 / D, waits=[(sm, t_mm1), (sv, tok.get(("mnu", tc - 1), 0))],
                    inc=ss)
        tok[("st1", tc)] = t_s1
        t_s2 = p.op("s", "activation", vtmp[:], meanb[:], AF.Square,
                    waits=[(ss, t_s1)], inc=ss)
        t_s3 = p.op("v", "scalar_tensor_tensor", rstdb[:], pb[5][:, 0:512], 1.0 / D,
                    vtmp[:], OP.mult, OP.subtract, waits=[(ss, t_s2), (sm, t_mm2)], inc=sv)
        tok[("st2", tc)] = t_s3
        t_s4 = p.op("s", "activation", rstdb[:], rstdb[:], AF.Sqrt, bias=p.const(1e-5),
                    waits=[(sv, t_s3)], inc=ss)
        t_s5 = p.op("v", "reciprocal", rstdb[:], rstdb[:], waits=[(ss, t_s4)], inc=sv)
        t_x1 = 0
        for ft in range(FT):
            t_d = p.op("v", "tensor_tensor", vtmp[:], acc[ft][:, c0:c0 + 512], meanb[:],
                       OP.subtract, waits=[(sv, t_s5), (sv, t_x1), (ss, 0 if ft else t_s2)],
                       inc=sv)
            t_x1 = p.op("v", "tensor_tensor", xn1[ft][:], vtmp[:], rstdb[:], OP.mult,
                        waits=[(sv, t_d)] + ([tok[("dbx", tc - 1)]] if (dbg and tc and ft == 0)
                                             else []), inc=sv)
            tok[("xn1t", tc, ft)] = t_x1
        tok[("mnu", tc)] = t_x1
        if dbg:
            for ft in range(FT):
                tok[("dbx", tc)] = dbx.load(dbg_xn1[ft][:, c0:c0 + 512], xn1[ft][:], eng="g",
                                            waits=[(sv, tok[("xn1t", tc, ft)])], adv=False)
        t_mm3 = 0
        for ft in range(FT):
            t_sq2 = p.op("s", "activation", sqb[:], xn1[ft][:], AF.Square,
                         waits=[(sv, tok[("xn1t", tc, ft)]), (sm, t_mm3)], inc=ss)
            t_mm3 = p.op("t", "matmul", pb[6][:, 0:512], onesb[:], sqb[:],
                         start=(ft == 0), stop=(ft == FT - 1),
                         waits=[(ss, t_sq2), (sv, tok.get(("rru", tc - 1), 0))], inc=sm)
        t_r1 = p.op("s", "activation", rrb[:], pb[6][:, 0:512], AF.Sqrt,
                    bias=p.const(1e-6), scale=1.0 / D, waits=[(sm, t_mm3)], inc=ss)
        t_r2 = p.op("v", "reciprocal", rrb[:], rrb[:], waits=[(ss, t_r1)], inc=sv)
        t_xm = 0
        for ft in range(FT):
            t_m1 = p.op("v", "tensor_tensor", vtmp[:], xn1[ft][:], rrb[:], OP.mult,
                        waits=[(sv, t_r2), (sv, t_xm)], inc=sv)
            t_xm = p.op("v", "tensor_scalar", xm[ft][:, c0:c0 + 512], vtmp[:],
                        nwv[:, ft:ft + 1], None, OP.mult, waits=[(sv, t_m1), t_vl], inc=sv)
            tok[("xmt", tc, ft)] = t_xm
        tok[("rru", tc)] = t_xm
        tok[("xmd", tc)] = t_xm
        if dbg:
            for ft in range(FT):
                tok[("dbm", tc)] = dbm.load(dbg_xm[ft][:, c0:c0 + 512], xm[ft][:, c0:c0 + 512],
                                            eng="g", waits=[(sv, tok[("xmt", tc, ft)])], adv=False)
        t_sm1 = 0
        for ci in range(FT):
            t_sm1 = p.op("t", "matmul", pb[0][0:NH, 0:512],
                         wwin[:, ci * NH:(ci + 1) * NH], xn1[ci][:],
                         start=(ci == 0), stop=(ci == FT - 1),
                         waits=[(sv, tok[("xn1t", tc, ci)]), t_vl,
                                (sv, tok.get(("sevac", tc - 1), 0))], inc=sm)
        t_se = p.op("v", "tensor_scalar", sffb[:], pb[0][0:NH, 0:512], binv[:, 0:1], None,
                    OP.add, waits=[(sm, t_sm1), t_vl, (sv, tok.get(("hfu", tc - 1), 0))],
                    inc=sv)
        tok[("sevac", tc)] = t_se
        t_h1 = p.op("v", "tensor_tensor", hfb[:],
                    bass.AP(sffb, 0, [[512, NH], [1, 512], [0, 16]]),
                    bass.AP(W1b, 0, [[16, NH], [0, 512], [1, 16]]), OP.mult,
                    waits=[(sv, t_se), (sv, t_c1), (sv, tok.get(("g1u", tc - 1), 0))], inc=sv)
        t_h2 = p.op("v", "tensor_tensor", hfb[:], hfb[:],
                    bass.AP(b1b, 0, [[16, NH], [0, 512], [1, 16]]), OP.add,
                    waits=[(sv, t_h1)], inc=sv)
        t_h3 = p.op("v", "tensor_relu", hfb[:], hfb[:], waits=[(sv, t_h2)], inc=sv)
        t_g1 = p.op("v", "tensor_tensor", hfb[:], hfb[:],
                    bass.AP(W2b, 0, [[16, NH], [0, 512], [1, 16]]), OP.mult,
                    waits=[(sv, t_h3)], inc=sv)
        tok[("hfu", tc)] = t_g1
        t_g2 = p.op("v", "reduce_sum", g1f[:], bass.AP(hfb, 0, [[8192, NH], [16, 512], [1, 16]]),
                    axis=AX.X, waits=[(sv, t_g1), (sv, tok.get(("g1fu", tc - 1), 0))], inc=sv)
        t_g3 = p.op("v", "tensor_scalar", g1b[:], g1f[:], b2v[:, 0:1], None, OP.add,
                    waits=[(sv, t_g2), t_vl], inc=sv)
        tok[("g1fu", tc)] = t_g3
        tok[("g1u", tc)] = t_g3
        for ft in range(FT):
            t_sk = p.op("v", "tensor_tensor", acc[ft][:, c0:c0 + 512],
                        acc[ft][:, c0:c0 + 512], xn1[ft][:], OP.add,
                        waits=[(sv, tok[("xmt", tc, ft)]), (sm, t_mm3), (sm, t_sm1)]
                        + ([t_halo] if tc == NTC - 1 else []), inc=sv)
            tok[("skip", tc, ft)] = t_sk
        for co in range(FT):
            t_om = p.op("t", "matmul", pb[1][:, 0:512], wwout[0:NH, co * 128:(co + 1) * 128],
                        g1b[:], start=True, stop=True,
                        waits=[(sv, t_g3), t_vl, (sv, tok.get(("oacc", tc, co - 1),
                                                              tok.get(("oacc", tc - 1, FT - 1), 0)))],
                        inc=sm)
            t_oa = p.op("v", "scalar_tensor_tensor", acc[co][:, c0:c0 + 512],
                        pb[1][:, 0:512], boutv[:, co:co + 1], acc[co][:, c0:c0 + 512],
                        OP.add, OP.add, waits=[(sm, t_om), (sv, tok[("skip", tc, co)]), t_vl],
                        inc=sv)
            tok[("oacc", tc, co)] = t_oa
    p.glob["l1acc_done"] = (sv, sv.n)

    st3 = p.sb("l1s3", [3, 8], F32)
    sq3 = p.sb("l1q3", [3, D], F32)
    xmh = p.sb("l1xmh", [128, FT * 4], BF16)
    a_ = p.op("v", "reduce_sum", st3[:, 0:1], xh[:], axis=AX.X, waits=[t_lh3], inc=sv)
    b_ = p.op("s", "activation", sq3[:], xh[:], AF.Square, waits=[t_lh3], inc=ss)
    c_ = p.op("v", "reduce_sum", st3[:, 1:2], sq3[:], axis=AX.X, waits=[(ss, b_)], inc=sv)
    d_ = p.op("v", "tensor_scalar", st3[:, 2:3], st3[:, 0:1], 1.0 / D, None, OP.mult,
              waits=[(sv, c_)], inc=sv)
    e_ = p.op("s", "activation", st3[:, 3:4], st3[:, 2:3], AF.Square, waits=[(sv, d_)], inc=ss)
    f_ = p.op("v", "scalar_tensor_tensor", st3[:, 4:5], st3[:, 1:2], 1.0 / D, st3[:, 3:4],
              OP.mult, OP.subtract, waits=[(ss, e_)], inc=sv)
    g_ = p.op("s", "activation", st3[:, 5:6], st3[:, 4:5], AF.Sqrt, bias=p.const(1e-5, rows=3),
              waits=[(sv, f_)], inc=ss)
    h_ = p.op("v", "reciprocal", st3[:, 6:7], st3[:, 5:6], waits=[(ss, g_)], inc=sv)
    i_ = p.op("v", "tensor_scalar", xh[:], xh[:], st3[:, 2:3], st3[:, 6:7],
              OP.subtract, OP.mult, waits=[(sv, h_)], inc=sv)
    j_ = p.op("s", "activation", sq3[:], xh[:], AF.Square, waits=[(sv, i_)], inc=ss)
    k_ = p.op("v", "reduce_sum", st3[:, 1:2], sq3[:], axis=AX.X, waits=[(ss, j_)], inc=sv)
    l_ = p.op("s", "activation", st3[:, 5:6], st3[:, 1:2], AF.Sqrt, bias=p.const(1e-6, rows=3),
              scale=1.0 / D, waits=[(sv, k_)], inc=ss)
    m_ = p.op("v", "reciprocal", st3[:, 6:7], st3[:, 5:6], waits=[(ss, l_)], inc=sv)
    n_ = p.op("v", "tensor_scalar", xh[:], xh[:], st3[:, 6:7], None, OP.mult,
              waits=[(sv, m_)], inc=sv)
    t_pe = 0
    for ft in range(FT):
        t_tr = p.op("t", "transpose", pb[2][0:128, ft * 4:ft * 4 + 3],
                    xh[:, ft * 128:(ft + 1) * 128], ident[0:3, 0:3],
                    waits=[(sv, n_), (ss, t_pe) if t_pe else (sv, n_)], inc=sm)
        t_pe = p.op("s", "copy", xmh[:, ft * 4:ft * 4 + 3], pb[2][0:128, ft * 4:ft * 4 + 3],
                    waits=[(sm, t_tr)], inc=ss)
    t_xm1 = 0
    for ft in range(FT):
        t_xm1 = p.op("v", "tensor_scalar", xmh[:, ft * 4:ft * 4 + 3], xmh[:, ft * 4:ft * 4 + 3],
                     nwv[:, ft:ft + 1], None, OP.mult, waits=[(ss, t_pe), t_vl], inc=sv)
    t_xmh = 0
    for ft in range(FT):
        t_xmh = p.op("v", "tensor_scalar", xmh[:, ft * 4:ft * 4 + 3], xmh[:, ft * 4:ft * 4 + 3],
                     hmv[:], None, OP.mult, waits=[(sv, t_xm1)], inc=sv)

    wist = p.stream("l1wi", 2)
    xist = p.stream("l1xs", 2)
    wtile = [p.sb(f"l1wt{j}", [128, FT * 128], BF16) for j in range(2)]
    xib = [p.sb(f"l1xi{j}", [128, 3 + T], BF16) for j in range(2)]
    rstage = p.sb("l1rst", [128, 512], BF16)
    ctmp = p.sb("l1ct", [128, T], F32)
    xcst = p.sb("l1xc", [128, T], BF16)
    for ro in range(32):
        wb = wtile[ro % 2]
        t_lw = 0
        for ci in range(FT):
            t_lw = wist.load(wb[:, ci * 128:(ci + 1) * 128],
                             io["w_inproj"][ci * 128:(ci + 1) * 128, ro * 128:(ro + 1) * 128],
                             waits=[(sm, tok.get(("ipm", ro - 2), 0))] if ci == 0 else (),
                             adv=(ci == FT - 1))
        is_xi = ro < 16
        if is_xi:
            t_hm = 0
            for ci in range(FT):
                t_hm = p.op("t", "matmul", pb[2][:, 0:3], wb[:, ci * 128:(ci + 1) * 128],
                            xmh[:, ci * 4:ci * 4 + 3], start=(ci == 0), stop=(ci == FT - 1),
                            waits=[(sv, t_xmh), t_lw, (sv, tok.get(("xiev", ro - 2), 0)),
                                   (ss, t_pe)], inc=sm)
        t_evp = None
        for tc in range(NTC):
            t_mm = 0
            for ci in range(FT):
                w_mm = [(sv, tok[("xmt", tc, ci)]), t_lw]
                if ci == 0 and t_evp is not None:
                    w_mm.append(t_evp)
                if is_xi:
                    w_mm.append((sv, tok.get(("xiev", ro - 2), 0)))
                elif tok.get(("rsev", ro - 2)):
                    w_mm.append((ss, tok[("rsev", ro - 2)]))
                t_mm = p.op("t", "matmul", pb[0 if ro % 2 == 0 else 1][:, 0:512],
                            wb[:, ci * 128:(ci + 1) * 128],
                            xm[ci][:, tc * 512:(tc + 1) * 512],
                            start=(ci == 0), stop=(ci == FT - 1), waits=w_mm, inc=sm)
            if is_xi:
                t_ev = p.op("v", "tensor_copy", xib[ro % 2][:, 3 + tc * 512:3 + (tc + 1) * 512],
                            pb[0 if ro % 2 == 0 else 1][:, 0:512], waits=[(sm, t_mm)], inc=sv)
                t_evp = (sv, t_ev)
            else:
                t_ev = p.op("s", "activation", rstage[:], pb[0 if ro % 2 == 0 else 1][:, 0:512],
                            AF.Silu,
                            waits=[(sm, t_mm)] + ([tok[("rspl", ro, tc - 1)]] if tc else
                                                  ([tok[("rspl", ro - 1, NTC - 1)]] if ro > 16 else [])),
                            inc=ss)
                tok[("rsev", ro)] = t_ev
                t_evp = (ss, t_ev)
                tok[("rspl", ro, tc)] = xist.load(resa[ro - 16][:, tc * 512:(tc + 1) * 512],
                                                  rstage[:], waits=[(ss, t_ev)])
            tok[("ipm", ro)] = t_mm
        if is_xi:
            t_he = p.op("v", "tensor_copy", xib[ro % 2][:, 0:3], pb[2][:, 0:3],
                        waits=[(sm, t_hm)], inc=sv)
            ct = ro
            t_cv1 = p.op("v", "tensor_scalar", ctmp[:], xib[ro % 2][:, 0:T],
                         cwv[:, ct * 4:ct * 4 + 1], None, OP.mult,
                         waits=[(sv, t_ev), (sv, t_he), t_vl,
                                (ss, tok.get(("xcs", ro - 1), 0))], inc=sv)
            for kk in range(1, 4):
                t_cv1 = p.op("v", "scalar_tensor_tensor", ctmp[:],
                             xib[ro % 2][:, kk:kk + T], cwv[:, ct * 4 + kk:ct * 4 + kk + 1],
                             ctmp[:], OP.mult, OP.add, waits=[(sv, t_cv1)], inc=sv)
            t_si = p.op("s", "activation", xcst[:], ctmp[:], AF.Silu,
                        bias=cbv[:, ct:ct + 1],
                        waits=[(sv, t_cv1)] + ([tok[("xcspl", ct - 1)]] if ct else []),
                        inc=ss)
            tok[("xcs", ro)] = t_si
            tok[("xiev", ro)] = t_cv1
            tok[("xcspl", ct)] = xist.load(xca[ct], xcst[:], waits=[(ss, t_si)])
    spdone = xist.done()
    if dbg:
        p.op("y", "dma_start", xc_dbg.ap(), xc_sp.ap(), waits=spdone, inc=hst.sems[0], inck=16)
    t_cc4 = p.op("g", "collective_compute", "AllGather", OP.bypass, replica_groups=RG_PAIRS,
                 ins=[xc_sp.ap()], outs=[xc_ag.ap()], waits=spdone, inc=sg)
    p.glob["xc_ag_tok"] = (sg, t_cc4)
    p.glob["res_done"] = spdone


def phase_acc_spill(p, io):
    T = p.T
    nc = p.nc
    acc = p.glob["acc"]
    acc_sp = nc.dram_tensor("acc_sp", [8, 128, T], F32, kind="Internal")
    p.glob["acc_sp"] = acc_sp
    asp = p.stream("asp", 1)
    ad = p.glob["acc_done"]
    toks = []
    for ft in range(8):
        toks.append(asp.load(acc_sp.ap()[ft], acc[ft][:], waits=[ad] if ft == 0 else (),
                             adv=False))
    p.glob["acc_sp_tok"] = toks[-1]


def phase_mamba(p, io, dbg):
    T = p.T
    T2 = 2 * T
    NT2 = T2 // 512
    nc = p.nc
    sv, ss, sm, sg = p.sv, p.ss, p.sm, p.sg
    big = p.glob["big"]
    ofs = p.glob["ofs"]
    rg = p.glob["rg"]
    tok = {}

    dt_sp = nc.dram_tensor("dt_sp", [8, 128, T2], BF16, kind="Internal")
    y_sp = nc.dram_tensor("y_sp", [8, 128, T2], BF16, kind="Internal")
    p.glob["y_sp"] = y_sp
    xc_ag = p.glob["xc_ag"]
    t_xcag = p.glob["xc_ag_tok"]

    vst = p.stream("mvs", 1)
    ind16 = p.sb("mind", [128, 16], BF16)
    dtbv = p.sb("mdtb", [128, 8], F32)
    A_sb = p.sb("mA", [128, 512], F32)
    vst.load(ind16[:], io["ind16"], adv=False)
    vst.load(dtbv[:], io["dt_bh"], adv=False)
    vst.load(ofs[:], io["offs"], adv=False)
    t_va = vst.load(A_sb[:], io["A_log_h"], adv=False)
    p.glob["t_ofs"] = t_va
    t_ae = p.op("s", "activation", A_sb[:], A_sb[:], AF.Exp, waits=[t_va], inc=ss)
    t_an = p.op("v", "tensor_scalar", A_sb[:], A_sb[:], -1.0, None, OP.mult,
                waits=[(ss, t_ae)], inc=sv)

    B_sb = p.sb("mB", [NS, T2], BF16)
    with p.scope():
        wdt = p.sb("mwdt", [128, 16 * 1024], BF16)
        wxp = p.sb("mwxp", [128, 16 * NS], BF16)
        wst = p.stream("mws", 1)
        for ci in range(16):
            wst.load(wdt[:, ci * 1024:(ci + 1) * 1024],
                     io["w_dt_h"][ci * 128:(ci + 1) * 128, :], adv=False)
        for ci in range(16):
            t_w = wst.load(wxp[:, ci * NS:(ci + 1) * NS],
                           io["w_xproj"][ci * 128:(ci + 1) * 128, :], adv=False)
        xk = [p.sb(f"mxk{j}", [128, 16 * 512], BF16) for j in range(2)]
        dstg = [p.sb(f"mds{j}", [128, 512], F32) for j in range(4)]
        dbf = [p.sb(f"mdb{j}", [128, 512], BF16) for j in range(4)]
        sps = [p.sb(f"msp{j}", [128, 512], F32) for j in range(4)]
        bstg = [p.sb(f"mbs{j}", [NS, 512], F32) for j in range(2)]
        c1 = p.const(1.0)
        xkst = p.stream("mxk", 2)
        dsp = p.stream("mdsp", 2)
        for tch in range(NT2):
            slot, lc = tch // (NT2 // 2), (tch % (NT2 // 2)) * 512
            buf = tch % 2
            wx = [t_xcag] if tch == 0 else []
            if tch >= 2:
                wx.append((sm, tok[("bmm", tch - 2)]))
            for ci in range(16):
                t_xk = xkst.load(xk[buf][:, ci * 512:(ci + 1) * 512],
                                 bass.AP(xc_ag, slot * (16 * 128 * T) + ci * 128 * T + lc,
                                         [[T, 128], [1, 512]]),
                                 waits=wx if ci == 0 else (), adv=(ci == 15))
            t_mm = 0
            for co in range(8):
                for ki in range(16):
                    t_mm = p.op("t", "matmul", big[:, co * 512:(co + 1) * 512],
                                wdt[:, ki * 1024 + co * 128:ki * 1024 + co * 128 + 128],
                                xk[buf][:, ki * 512:(ki + 1) * 512],
                                start=(ki == 0), stop=(ki == 15),
                                waits=([t_xk, t_w] if (co == 0 and ki == 0) else [])
                                + (([(ss, tok.get(("dtev", tch - 1, co), 0))]
                                    + ([(sv, tok.get(("bev", tch - 1), 0))] if co == 0 else []))
                                   if ki == 0 else []), inc=sm)
                tok[("comm", tch, co)] = t_mm
            tok[("dtmm", tch)] = t_mm
            for cg in range(2):
                cos = range(cg * 4, cg * 4 + 4)
                te1, te3 = {}, {}
                for co in cos:
                    te1[co] = p.op("s", "activation", sps[co % 4][:],
                                   big[:, co * 512:(co + 1) * 512],
                                   AF.Abs, bias=dtbv[:, co:co + 1],
                                   waits=[(sm, tok[("comm", tch, co)]), t_va,
                                          (sv, tok.get(("dtadd", tch, co - 4),
                                                       tok.get(("dtadd", tch - 1, 4 + co), 0)))],
                                   inc=ss)
                for co in cos:
                    p.op("s", "activation", sps[co % 4][:], sps[co % 4][:], AF.Exp,
                         scale=-1.0, waits=[(ss, te1[co])], inc=ss)
                for co in cos:
                    te3[co] = p.op("s", "activation", sps[co % 4][:], sps[co % 4][:], AF.Ln,
                                   bias=c1, waits=[p._cinit], inc=ss)
                for co in cos:
                    wsp = [tok.get(("dtsp", tch, co - 4), tok.get(("dtsp", tch - 1, 4 + co),
                                                                  (sv, 0)))]
                    t_ev = p.op("s", "activation", dstg[co % 4][:],
                                big[:, co * 512:(co + 1) * 512],
                                AF.Relu, bias=dtbv[:, co:co + 1],
                                waits=[(ss, te3[co])] + wsp, inc=ss)
                    tok[("dtev", tch, co)] = t_ev
                for co in cos:
                    t_add = p.op("v", "tensor_tensor", dbf[co % 4][:], dstg[co % 4][:],
                                 sps[co % 4][:], OP.add,
                                 waits=[(ss, tok[("dtev", tch, co)])], inc=sv)
                    tok[("dtadd", tch, co)] = t_add
                    tok[("dtsp", tch, co)] = dsp.load(
                        bass.AP(dt_sp, co * 128 * T2 + tch * 512, [[T2, 128], [1, 512]]),
                        dbf[co % 4][:], waits=[(sv, t_add)])
            t_bm = 0
            for ki in range(16):
                t_bm = p.op("t", "matmul", big[0:NS, 0:512],
                            wxp[:, ki * NS:(ki + 1) * NS], xk[buf][:, ki * 512:(ki + 1) * 512],
                            start=(ki == 0), stop=(ki == 15),
                            waits=[(ss, tok[("dtev", tch, 0)])] if ki == 0 else [], inc=sm)
            t_be = p.op("v", "tensor_copy", bstg[buf][:], big[0:NS, 0:512],
                        waits=[(sm, t_bm), (sv, tok.get(("bcp", tch - 2), 0))], inc=sv)
            tok[("bev", tch)] = t_be
            t_bc = p.op("v", "tensor_copy", B_sb[:, tch * 512:(tch + 1) * 512], bstg[buf][:],
                        waits=[(sv, t_be)], inc=sv)
            tok[("bcp", tch)] = t_bc
            tok[("bmm", tch)] = t_bm
        t_ball = tok[("bcp", NT2 - 1)]
        dt_done = dsp.done()
        p.glob["m_dt_done"] = dt_done
        if dbg:
            dt_dbg = nc.dram_tensor("dt_dbg", [8, 128, T2], BF16, kind="ExternalOutput")
            p.op("y", "dma_start", dt_dbg.ap(), dt_sp.ap(), waits=dt_done, inc=p.ld, inck=16)
            B_dbg = nc.dram_tensor("B_dbg", [NS, T2], BF16, kind="ExternalOutput")
            p.op("y", "dma_start", B_dbg.ap(), B_sb[:], waits=[(sv, t_ball)],
                 inc=p.ld, inck=16)

    Brep = [p.sb(f"mBr{k}", [128, T2], BF16) for k in range(8)]
    brs = p.stream("mbrs", 1)
    for k in range(8):
        for c in range(16):
            t_br = brs.load(bass.AP(Brep[k], c * 8 * T2, [[T2, 8], [1, T2]]),
                            bass.AP(B_sb, 8 * k * T2, [[T2, 8], [1, T2]]),
                            waits=[(sv, t_ball)] if (k == 0 and c == 0) else (), adv=False)
    dt_done = p.glob["m_dt_done"]

    xcsm = p.sb("mxcs", [128, T2], BF16)
    xcrep = [p.sb(f"mxc{j}", [128, T2], BF16) for j in range(2)]
    dtrep = [p.sb(f"mdt{j}", [128, T2], BF16) for j in range(2)]
    af = [p.sb(f"maf{j}", [128, T2], F32) for j in range(2)]
    bxb = [p.sb(f"mbx{j}", [128, T2], BF16) for j in range(2)]
    ystg = [p.sb(f"mys{j}", [16, T2], BF16) for j in range(2)]
    mrd = p.stream("mrd", 2)
    mrd2 = p.stream("mrd2", 2)
    mrx = p.stream("mrx", 2)
    mrx2 = p.stream("mrx2", 2)
    mrg = p.stream("mrg", 1)
    ysp = p.stream("mysp", 2)
    zc = p.const(0.0)
    EV = T2 // 512

    def emit_yevac(cbe):
        bufe = cbe % 2
        cte, rbe = cbe // 8, cbe % 8
        for hh in range(2):
            t_ye = p.op("v", "tensor_copy", ystg[bufe][:, hh * (T2 // 2):(hh + 1) * (T2 // 2)],
                        big[0:16, hh * (T2 // 2):(hh + 1) * (T2 // 2)],
                        waits=[(sm, tok[("sum", cbe, 7, EV - 1)])]
                        + ([tok[("yspl", cbe - 2)]] if (hh == 0 and cbe >= 2) else []), inc=sv)
            tok[("yev", cbe, hh)] = t_ye
        tok[("yspl", cbe)] = ysp.load(
            bass.AP(y_sp, cte * 128 * T2 + 16 * rbe * T2, [[T2, 16], [1, T2]]),
            ystg[bufe][:], waits=[(sv, tok[("yev", cbe, 1)])])

    for cb in range(64):
        buf = cb % 2
        ct, rb = cb // 8, cb % 8
        if rb == 0:
            gfirst = ([t_va, t_xcag, t_br] if cb == 0 else [])
            gfree = [tok[("mlx", cb - 1)], tok[("mlx2", cb - 1)]] if cb else []
            p.op("g", "reg_load", rg, ofs[0:1, ct:ct + 1], waits=gfree + gfirst)
            tok[("xcs", ct)] = mrg.load(
                xcsm[:], bass.AP(xc_ag, rg, [[T, 128], [16 * 128 * T, 2], [1, T]]),
                eng="g", adv=False)
        yfirst = (dt_done + [(sv, t_an), tok[("xcs", 0)]]) if cb == 0 else []
        yfree = ([(ss, tok[("aexp", cb - 2, 7)]),
                  (sv, tok[("bx", cb - 2, 7)])] if cb >= 2 else [])
        dfree = [(ss, tok[("aexp", cb - 2, 7)])] if cb >= 2 else []
        xfree = [(sv, tok[("bx", cb - 2, 7)])] if cb >= 2 else []
        wxc = [tok[("xcs", ct)]] if rb == 0 else []
        t_d1 = mrd.load(dtrep[buf][0:64, :],
                        bass.AP(dt_sp, ct * 128 * T2 + 16 * rb * T2,
                                [[T2, 8], [0, 8], [1, T2]]),
                        waits=dfree + (yfirst if cb == 0 else []))
        t_d2 = mrd2.load(bass.AP(dtrep[buf], 64 * T2, [[T2, 64], [1, T2]]),
                         bass.AP(dt_sp, ct * 128 * T2 + (16 * rb + 8) * T2,
                                 [[T2, 8], [0, 8], [1, T2]]),
                         eng="s", waits=dfree)
        t_x1 = mrx.load(xcrep[buf][0:64, :],
                        bass.AP(xcsm, 16 * rb * T2, [[T2, 8], [0, 8], [1, T2]]),
                        waits=xfree + wxc)
        t_x2 = mrx2.load(bass.AP(xcrep[buf], 64 * T2, [[T2, 64], [1, T2]]),
                         bass.AP(xcsm, (16 * rb + 8) * T2, [[T2, 8], [0, 8], [1, T2]]),
                         eng="s", waits=xfree + wxc)
        tok[("mld", cb)] = t_d1
        tok[("mld2", cb)] = t_d2
        tok[("mlx", cb)] = t_x1
        tok[("mlx2", cb)] = t_x2
        for k in range(8):
            kb = k % 2
            t_a = p.op("s", "activation", af[kb][:], dtrep[buf][:], AF.Exp,
                       scale=A_sb[:, cb * 8 + k:cb * 8 + k + 1],
                       waits=[tok[("mld", cb)], tok[("mld2", cb)],
                              (sv, tok.get(("scan", cb, k - 2),
                                           tok.get(("scan", cb - 1, 6 + k), 0)))],
                       inc=ss)
            tok[("aexp", cb, k)] = t_a
            bfree = tok.get(("sum", cb, k - 2, EV - 1),
                            tok.get(("sum", cb - 1, 6 + k, EV - 1), 0))
            t_bx = p.op("v", "tensor_tensor", bxb[kb][:], Brep[k][:], xcrep[buf][:], OP.mult,
                        waits=[tok[("mlx", cb)], tok[("mlx2", cb)], t_br, (sm, bfree)],
                        inc=sv)
            tok[("bx", cb, k)] = t_bx
            t_sc = p.op("v", "tensor_tensor_scan", bxb[kb][:], af[kb][:], bxb[kb][:],
                        zc, OP.mult, OP.add,
                        waits=[(ss, t_a), (sv, t_bx)], inc=sv)
            tok[("scan", cb, k)] = t_sc
            if k == 0 and cb > 0:
                emit_yevac(cb - 1)
            for tc in range(EV):
                t_sum = p.op("t", "matmul", big[0:16, tc * 512:(tc + 1) * 512], ind16[:],
                             bxb[kb][:, tc * 512:(tc + 1) * 512],
                             start=(k == 0), stop=(k == 7),
                             waits=([(sv, t_sc)] if tc == 0 else [])
                             + ([(sv, tok.get(("yev", cb - 1, 1), 0))]
                                if (k == 0 and tc == 0) else [])
                             + ([t_va] if (cb == 0 and k == 0 and tc == 0) else []), inc=sm)
                tok[("sum", cb, k, tc)] = t_sum
    emit_yevac(63)
    p.glob["m_y_done"] = ysp.done()
    if dbg:
        y_dbg = nc.dram_tensor("y_dbg", [8, 128, T2], BF16, kind="ExternalOutput")
        p.op("y", "dma_start", y_dbg.ap(), y_sp.ap(), waits=p.glob["m_y_done"],
             inc=p.ld, inck=16)


def phase_epilogue(p, io, dbg):
    T = p.T
    T2 = 2 * T
    nc = p.nc
    sv, ss, sm, sg = p.sv, p.ss, p.sm, p.sg
    big = p.glob["big"]
    ofs = p.glob["ofs"]
    rg = p.glob["rg"]
    xc_sp = p.glob["xc_sp"]
    y_sp = p.glob["y_sp"]
    res_sp = p.glob["res_sp"]
    accE = p.glob["acc"]
    tok = {}

    y_ag = nc.dram_tensor("y_ag", [2, 8, 128, T2], BF16, kind="Internal")
    t_cc = p.op("g", "collective_compute", "AllGather", OP.bypass, replica_groups=RG_PAIRS,
                ins=[y_sp.ap()], outs=[y_ag.ap()], waits=p.glob["m_y_done"], inc=sg)

    ast = p.stream("ast2", 1)
    acc_sp = p.glob["acc_sp"]
    for ft in range(8):
        t_ar = ast.load(accE[ft][:], acc_sp.ap()[ft],
                        waits=[p.glob["acc_sp_tok"]] if ft == 0 else (), adv=False)

    vst = p.stream("evs", 1)
    Dv = p.sb("eDv", [128, 16], F32)
    t_dv = vst.load(Dv[:], io["D_all"], adv=False)
    wo = p.sb("ewo", [128, 16 * 1024], BF16)
    wst = p.stream("ews", 1)
    for ci in range(16):
        t_wo = wst.load(wo[:, ci * 1024:(ci + 1) * 1024],
                        io["w_outproj"][ci * 128:(ci + 1) * 128, :], adv=False)

    y2 = p.sb("ey2", [128, 16 * T], BF16)
    xstg = [p.sb(f"exs{j}", [128, T], BF16) for j in range(2)]
    rstg = [p.sb(f"ers{j}", [128, T], BF16) for j in range(2)]
    lstg = p.stream("elg", 1)
    lsty = p.stream("els", 2)
    res_done = p.glob["res_done"]
    for s in range(2):
        p.op("g", "reg_load", rg, ofs[0:1, 8 + s:8 + s + 1],
             waits=[(sg, t_cc), p.glob["t_ofs"]] if s == 0 else [])
        t_ly = lstg.load(bass.AP(y2, s * 8 * T, [[16 * T, 128], [T, 8], [1, T]]),
                         bass.AP(y_ag, rg, [[T2, 128], [128 * T2, 8], [1, T]]),
                         eng="g", adv=False)
    for ct in range(16):
        buf = ct % 2
        wfree = [(sv, tok[("y2", ct - 2)])] if ct >= 2 else []
        t_lx = lsty.load(xstg[buf][:], xc_sp.ap()[ct],
                         waits=wfree + (res_done if ct == 0 else []), adv=False)
        t_l = lsty.load(rstg[buf][:], res_sp.ap()[ct])
        t_t = p.op("v", "scalar_tensor_tensor", y2[:, ct * T:(ct + 1) * T], xstg[buf][:],
                   Dv[:, ct:ct + 1], y2[:, ct * T:(ct + 1) * T], OP.mult, OP.add,
                   waits=[t_l, t_ly, t_dv, (sv, tok.get(("y2", ct - 2), 0))], inc=sv)
        t_y2 = p.op("v", "tensor_tensor", y2[:, ct * T:(ct + 1) * T],
                    y2[:, ct * T:(ct + 1) * T], rstg[buf][:], OP.mult,
                    waits=[(sv, t_t)], inc=sv)
        tok[("y2", ct)] = t_y2
    NTCH = T // 512
    for co in range(8):
        for tch in range(NTCH):
            t_mm = 0
            for ci in range(16):
                t_mm = p.op("t", "matmul", big[:, co * 512:(co + 1) * 512],
                            wo[:, ci * 1024 + co * 128:ci * 1024 + co * 128 + 128],
                            y2[:, ci * T + tch * 512:ci * T + (tch + 1) * 512],
                            start=(ci == 0), stop=(ci == 15),
                            waits=[(sv, tok[("y2", ci)]), t_wo]
                            + ([(sv, tok.get(("oa", co, tch - 1),
                                             tok.get(("oa", co - 1, NTCH - 1), 0)))]
                               if ci == 0 else []), inc=sm)
            t_oa = p.op("v", "tensor_tensor", accE[co][:, tch * 512:(tch + 1) * 512],
                        accE[co][:, tch * 512:(tch + 1) * 512], big[:, co * 512:(co + 1) * 512],
                        OP.add, waits=[(sm, t_mm), t_ar], inc=sv)
            tok[("oa", co, tch)] = t_oa
    p.glob["acc_done"] = (sv, sv.n)


class ColView:

    def __init__(self, t, base):
        self.t, self.base = t, base

    def __getitem__(self, idx):
        s0, s1 = idx
        return self.t[s0, self.base + s1.start:self.base + s1.stop]


def phase_out(p, io):
    T = p.T
    FT = D // 128
    TT = T // 128
    acc = p.glob["acc"]
    ident = p.glob["ident"]
    pbk = p.glob["pb"]
    otok = [p.sb(f"otok{j}", [128, D], F32) for j in range(2)]
    ost = p.stream("ost", 2)
    done = {}
    acc_done = p.glob["acc_done"]
    for tt in range(TT):
        buf = otok[tt % 2]
        evs = 0
        for grp in range(2):
            bank = pbk[grp]
            tms = 0
            for j in range(4):
                ft = grp * 4 + j
                w = [acc_done, (p.sv, done.get(("e", grp), 0))]
                if (tt - 2) in done:
                    w.append(done[tt - 2])
                tms = p.op("t", "transpose", bank[:, j * 128:(j + 1) * 128],
                           acc[ft][:, tt * 128:(tt + 1) * 128], ident[:], waits=w, inc=p.sm)
            for j in range(4):
                ft = grp * 4 + j
                evs = p.op("v", "tensor_copy", buf[:, ft * 128:(ft + 1) * 128],
                           bank[:, j * 128:(j + 1) * 128],
                           waits=[(p.sm, tms)], inc=p.sv)
            done[("e", grp)] = evs
        done[tt] = ost.load(io["x_out"][tt * 128:(tt + 1) * 128, :], buf[:],
                            waits=[(p.sv, evs)])
    p.wait("y", [done[TT - 1], done[TT - 2]])


def build(Tloc=TFULL // 2, dbg=False, ncores=8):
    set_ncores(ncores)
    nc = bass.Bass("TRN2", target_bir_lowering=False)
    io = declare_io(nc, Tloc, dbg)
    es = contextlib.ExitStack()
    with es:
        block = es.enter_context(nc.Block())
        p = Prog(nc, es, block, Tloc)
        for cv in (1e-5, 1e-6, 0.0, 1.0):
            p.const(cv)
        p.glob["ofs"] = p.sb("ofs", [1, 16], I32)
        p.glob["rg"] = es.enter_context(nc.gpsimd.register("rg0"))
        rwkv_prelude(p, io, dbg)
        with p.scope():
            p.glob["pb"] = [p.ps(f"pb{i}", 512) for i in range(7)]
            p.glob["pbT"] = p.ps("pbT", 1024, BF16)
            p.glob["acc"] = [p.sb(f"acc{i}", [128, Tloc], F32) for i in range(D // 128)]
            with p.scope(), nc.named_scope("ph_ln"):
                phase_load_ln(p, io, dbg)
                rwkv_stageA(p, io, dbg)
            with p.scope(), nc.named_scope("ph_rwkv"):
                rwkv_main(p, io, dbg)
            with p.scope(), nc.named_scope("ph_rwkvpost"):
                rwkv_post(p, io, dbg)
            p.glob["attn_done"] = (p.sv, p.sv.n)
            with p.scope(), nc.named_scope("ph_ffn0"):
                phase_ffn0(p, io, dbg)
            with p.scope(), nc.named_scope("ph_l1pre"):
                phase_l1pre(p, io, dbg)
            p.glob["acc_done"] = (p.sv, p.sv.n)
            phase_acc_spill(p, io)
        big = p.ps("big", 4096)
        p.glob["big"] = big
        with p.scope(), nc.named_scope("ph_mamba"):
            phase_mamba(p, io, dbg)
        p.glob["acc"] = [p.sb(f"accE{i}", [128, Tloc], F32) for i in range(D // 128)]
        with p.scope(), nc.named_scope("ph_epi"):
            phase_epilogue(p, io, dbg)
        p.glob["pb"] = [ColView(big, i * 512) for i in range(7)]
        with nc.named_scope("ph_out"):
            phase_out(p, io)
    return nc


def kernel(**inputs):
    Tloc = TFULL // 2
    per_core = prep_inputs(inputs, Tloc)
    nc = build(Tloc)
    res = run_bass_kernel_spmd(nc, per_core, core_ids=list(range(8)))
    out = np.zeros((NB, TFULL, D), np.float32)
    for core in range(8):
        b, half = core // 2, core % 2
        out[b, half * Tloc:(half + 1) * Tloc] = res.results[core]["x_out"]
    return out
